# version 04
# baseline (speedup 1.0000x reference)
import numpy as np
import ml_dtypes
from contextlib import ExitStack
import concourse.bass as bass
import concourse.mybir as mybir
from concourse.bass_utils import run_bass_kernel_spmd

F32 = mybir.dt.float32
BF16 = mybir.dt.bfloat16
AF = mybir.ActivationFunctionType
ALU = mybir.AluOpType
AX = mybir.AxisListType
BF = ml_dtypes.bfloat16

NCORE = 8
D = 4096
NC = 32
TL = 1024
NH = 32
SEQ = 8192
L0_SCALE = float(1.0 / np.sqrt(np.float32(128.0)))
L1_SCALE = float(np.float64(192.0) ** (-0.5))
EPS = 1e-5


class Ctr:
    def __init__(self, ctx, nc, name):
        self.h = ctx.enter_context(nc.semaphore(name))
        self.n = 0


class Ring:
    def __init__(self, p, name, n):
        self.sems = [p.ctr(f"{name}_r{i}") for i in range(n)]
        self.n = n
        self.rec = {}

    def dma(self, p, eng, i, out, in_, **kw):
        s = self.sems[i % self.n]
        p.dma(eng, out, in_, s, **kw)
        self.rec[i] = (s, s.n)

    def wait(self, p, eng, i):
        p.waitf(eng, self.rec[i])

    def free(self, i):
        return self.rec[i]

    def wait_all(self, p, eng):
        for s in self.sems:
            p.wait(eng, s, s.n)


class P:
    ENGS = ("sync", "gpsimd", "tensor", "vector", "scalar")

    def __init__(self, nc, ctx):
        self.nc = nc
        self.ctx = ctx
        self.actx = ctx
        self.s = {e: [] for e in P.ENGS}
        self._ctrs = {}

    def ctr(self, name):
        if name not in self._ctrs:
            self._ctrs[name] = Ctr(self.ctx, self.nc, name)
        return self._ctrs[name]

    def wait(self, eng, ctr, val):
        if val > 0:
            h = ctr.h
            self.s[eng].append(lambda e: e.wait_ge(h, val))

    def waitf(self, eng, free):
        if free is not None:
            self.wait(eng, free[0], free[1])

    def op(self, eng, fn, inc=None, k=1):
        if inc is None:
            self.s[eng].append(fn)
            return None
        h = inc.h
        self.s[eng].append(lambda e: fn(e).then_inc(h, k))
        inc.n += k
        return inc.n

    def dma(self, eng, out, in_, inc, **kw):
        h = inc.h
        self.s[eng].append(lambda e: e.dma_start(out=out, in_=in_, **kw).then_inc(h, 16))
        inc.n += 16
        return inc.n

    def flush(self, name):
        with self.nc.Block(name) as blk:
            for eng in P.ENGS:
                ops = self.s[eng]
                if not ops:
                    continue
                def runner(e, ops=ops):
                    for f in ops:
                        f(e)
                getattr(blk, eng)(runner)
        self.s = {e: [] for e in P.ENGS}


def mk_nc():
    return bass.Bass("TRN2", target_bir_lowering=False, num_devices=NCORE)


def emit_load_xb(p, tag, x_dram, xb):
    nc, ctx = p.nc, p.actx
    xf = [ctx.enter_context(nc.sbuf_tensor(f"{tag}_xf{i}", [128, 1024], F32, allow_name_mangling=True)) for i in range(2)]
    rl = Ring(p, f"{tag}_xl", 2)
    s_c = p.ctr(f"{tag}_xc")
    c_base = s_c.n
    for f in range(NC):
        p.wait("sync", s_c, c_base + f - 1)
        rl.dma(p, "sync", f, xf[f % 2][:], x_dram[f * 128:(f + 1) * 128, :])
        rl.wait(p, "scalar", f)
        p.op("scalar", lambda e, f=f: e.activation(
            xb[:, f * 1024:(f + 1) * 1024], xf[f % 2][:], AF.Copy, scale=1.0), s_c)
    return s_c


def emit_proj(p, tag, w_dram, n_d, xb, psq, evac_cb, pre_mm_waits=None, extra_mm_cb=None,
              psum_free=None, pf_base=0):
    nc, ctx = p.nc, p.actx
    n_e = w_dram.shape[1] // 128
    nrot = len(psq)
    wsf = [ctx.enter_context(nc.sbuf_tensor(f"{tag}_wsf{i}", [128, 256], F32, allow_name_mangling=True)) for i in range(2)]
    wsb = [ctx.enter_context(nc.sbuf_tensor(f"{tag}_wsb{i}", [128, 256], BF16, allow_name_mangling=True)) for i in range(2)]
    rwd = Ring(p, f"{tag}_wd", 2)
    s_wc = p.ctr(f"{tag}_wc")
    wc_base = s_wc.n
    s_mm = p.ctr(f"{tag}_mm")
    mm_done = {}
    if psum_free is None:
        psum_free = {}
    pend = []

    def defer(fn):
        pend.append(fn)

    n_eg = n_e // 2
    for eg in range(n_eg):
        for fn in pend:
            fn()
        pend = []
        for d in range(n_d):
            idx = eg * n_d + d
            p.wait("sync", s_wc, wc_base + idx - 1)
            rwd.dma(p, "sync", idx, wsf[idx % 2][:],
                    w_dram[d * 128:(d + 1) * 128, eg * 256:(eg + 1) * 256])
            rwd.wait(p, "vector", idx)
            p.wait("vector", s_mm, mm_done.get(idx - 2, 0))
            p.op("vector", lambda e, a=wsb[idx % 2], b=wsf[idx % 2]:
                 e.tensor_copy(a[:], b[:]), s_wc)
            p.wait("tensor", s_wc, wc_base + idx + 1)
            if d == 0:
                if eg == 0 and pre_mm_waits:
                    for (ctr, val) in pre_mm_waits:
                        p.wait("tensor", ctr, val)
                for e2 in range(2):
                    ec = eg * 2 + e2
                    if pf_base + ec - nrot >= 0:
                        p.waitf("tensor", psum_free[pf_base + ec - nrot])
            for e2 in range(2):
                ec = eg * 2 + e2
                psten = psq[ec % nrot]
                for t in range(2):
                    fn = (lambda e, psten=psten, w=wsb[idx % 2], e2=e2, d=d, t=t:
                          e.matmul(psten[:, t * 512:(t + 1) * 512],
                                   w[:, e2 * 128:(e2 + 1) * 128],
                                   xb[:, d * 1024 + t * 512: d * 1024 + (t + 1) * 512],
                                   start=(d == 0),
                                   stop=(d == n_d - 1 and extra_mm_cb is None)))
                    p.op("tensor", fn, s_mm if (e2 == 1 and t == 1) else None)
            mm_done[idx] = s_mm.n
        for e2 in range(2):
            ec = eg * 2 + e2
            if extra_mm_cb is not None:
                extra_mm_cb(ec, psq[ec % nrot], s_mm)
            psum_free[pf_base + ec] = evac_cb(ec, psq[ec % nrot], defer, s_mm.n, s_mm)
    for fn in pend:
        fn()
    return psum_free, s_mm


def emit_qkv(nc, p, ctx, tag, rope, xin, wq, wk, wv, qout, kout, vout,
             cos_in=None, sin_in=None):
    if rope:
        rsign = np.zeros((128, 128), dtype=BF)
        for i in range(64):
            rsign[i + 64, i] = -1.0
            rsign[i, i + 64] = 1.0
        rsign_d = nc.inline_tensor(rsign, f"rsign{tag}")
    ident_d = nc.inline_tensor(np.eye(128, dtype=BF), f"ident{tag}")
    sb = lambda n, s, d: ctx.enter_context(nc.sbuf_tensor(n, s, d, allow_name_mangling=True))
    psa = lambda n, s, d=F32: ctx.enter_context(nc.psum_tensor(n, s, d, allow_name_mangling=True))

    xb = sb("xb", [128, NC * 1024], BF16)
    psq = [psa(f"psq{i}", [128, 1024]) for i in range(3)]
    ptr_bf = psa("ptr_bf", [128, 1024], BF16)
    prot = psa("prot", [128, 512])
    ev_sb = [sb(f"ev{i}", [128, 1024], BF16) for i in range(3)]
    ob_sb = [sb(f"ob{i}", [128, 1024], BF16) for i in range(2)]
    tmp1 = [sb(f"tmp1{i}", [128, 1024], BF16) for i in range(2)]
    tmp2 = [sb(f"tmp2{i}", [128, 1024], BF16) for i in range(2)]
    ident_sb = sb("ident_sb", [128, 128], BF16)
    if rope:
        cosb = sb("cosb", [128, 1024], BF16)
        sinb = sb("sinb", [128, 1024], BF16)
        cosf = sb("cosf", [128, 1024], F32)
        rs_sb = sb("rs_sb", [128, 128], BF16)

    s_aux = p.ctr("aux")
    s_rot = p.ctr("rot")
    s_rop = p.ctr("rope")
    rst_ob = Ring(p, "st_ob", 2)
    rst_eb = Ring(p, "st_eb", 3)
    ob_sti = [0]
    eb_sti = [0]

    aux0 = s_aux.n
    p.dma("sync", ident_sb[:], ident_d[:], s_aux)
    if rope:
        p.dma("sync", cosf[:], cos_in[:], s_aux)
        p.dma("sync", rs_sb[:], rsign_d[:], s_aux)
        p.wait("vector", s_aux, aux0 + 48)
        p.op("vector", lambda e: e.tensor_copy(cosb[:], cosf[:]), s_aux)
        p.wait("sync", s_aux, aux0 + 49)
        p.dma("sync", cosf[:], sin_in[:], s_aux)
        p.wait("vector", s_aux, aux0 + 65)
        p.op("vector", lambda e: e.tensor_copy(sinb[:], cosf[:]), s_aux)
    aux_ready = s_aux.n

    xb_ready = emit_load_xb(p, "x", xin, xb)

    G = {"n": 0}
    eb_free = {}
    ob_free = {}
    rot_free = {}
    tmp_free = {}

    s_evq = p.ctr("evq")

    def make_evac(kind, out_dram):
        def evac(ec, psten, defer, mm_need, s_mm):
            gec = G["n"]; G["n"] += 1
            eb = ev_sb[gec % 3]
            ob = ob_sb[gec % 2]
            t1, t2 = tmp1[gec % 2], tmp2[gec % 2]
            p.wait("scalar", s_mm, mm_need)
            p.waitf("scalar", eb_free.get(gec - 3))
            p.op("scalar", lambda e: e.activation(
                eb[:], psten[:], AF.Copy, scale=1.0), s_evq)
            ev_now = s_evq.n

            if kind == "plain":
                i = eb_sti[0]; eb_sti[0] += 1
                assert i % 3 == gec % 3
                p.wait("sync", s_evq, ev_now)
                rst_eb.dma(p, "sync", i, out_dram[ec * 128:(ec + 1) * 128, :], eb[:])
                eb_free[gec] = rst_eb.free(i)
                return (s_evq, ev_now)

            def emit_def():
                if kind == "rope":
                    p.wait("tensor", s_evq, ev_now)
                    p.waitf("tensor", rot_free.get(gec - 1))
                    p.op("tensor", lambda e: e.matmul(
                        prot[:], rs_sb[:], eb[:, 0:512], start=True, stop=True), s_rot)
                    r1 = s_rot.n
                    p.wait("vector", s_rot, r1)
                    p.waitf("vector", tmp_free.get(gec - 2))
                    p.op("vector", lambda e: e.tensor_tensor(
                        t2[:, 0:512], prot[:], sinb[:, 0:512], ALU.mult), s_rop)
                    v1 = s_rop.n
                    p.wait("tensor", s_rop, v1)
                    p.op("tensor", lambda e: e.matmul(
                        prot[:], rs_sb[:], eb[:, 512:1024], start=True, stop=True), s_rot)
                    r2 = s_rot.n
                    rot_free[gec] = (s_rop, v1 + 1)
                    p.wait("vector", s_rot, r2)
                    p.op("vector", lambda e: e.tensor_tensor(
                        t2[:, 512:1024], prot[:], sinb[:, 512:1024], ALU.mult), s_rop)
                    p.op("vector", lambda e: e.tensor_mul(t1[:], eb[:], cosb[:]), s_rop)
                    v2 = s_rop.n
                    eb_free[gec] = (s_rop, v2)
                    p.waitf("vector", ob_free.get(gec - 2))
                    p.wait("vector", s_rop, v2)
                    p.op("vector", lambda e: e.tensor_add(ob[:], t1[:], t2[:]), s_rop)
                    tmp_free[gec] = (s_rop, s_rop.n)
                    i = ob_sti[0]; ob_sti[0] += 1
                    assert i % 2 == gec % 2
                    p.wait("sync", s_rop, s_rop.n)
                    rst_ob.dma(p, "sync", i, out_dram[ec * 128:(ec + 1) * 128, :], ob[:])
                    ob_free[gec] = rst_ob.free(i)
                else:
                    p.wait("tensor", s_evq, ev_now)
                    p.waitf("tensor", rot_free.get(gec - 1))
                    def pe_tr(e):
                        inst = None
                        for i in range(8):
                            inst = e.transpose(ptr_bf[:, i * 128:(i + 1) * 128],
                                               eb[:, i * 128:(i + 1) * 128], ident_sb[:])
                        return inst
                    p.op("tensor", pe_tr, s_rot)
                    eb_free[gec] = (s_rot, s_rot.n)
                    p.waitf("scalar", ob_free.get(gec - 2))
                    p.wait("scalar", s_rot, s_rot.n)
                    p.op("scalar", lambda e: e.activation(
                        ob[:], ptr_bf[:], AF.Copy, scale=1.0), s_rop)
                    rot_free[gec] = (s_rop, s_rop.n)
                    i = ob_sti[0]; ob_sti[0] += 1
                    assert i % 2 == gec % 2
                    p.wait("sync", s_rop, s_rop.n)
                    dst = out_dram.ap().rearrange("(b p) e -> p b e", p=128)[:, :, ec * 128:(ec + 1) * 128]
                    srcap = ob[:].rearrange("p (b d) -> p b d", d=128)
                    rst_ob.dma(p, "sync", i, dst, srcap)
                    ob_free[gec] = rst_ob.free(i)
            defer(emit_def)
            return (s_evq, ev_now)
        return evac

    kind_q = "rope" if rope else "plain"
    pf = {}
    emit_proj(p, "q", wq, NC, xb, psq, make_evac(kind_q, qout),
              pre_mm_waits=[(xb_ready, xb_ready.n), (s_aux, aux_ready)],
              psum_free=pf, pf_base=0)
    emit_proj(p, "k", wk, NC, xb, psq, make_evac(kind_q, kout),
              psum_free=pf, pf_base=32)
    emit_proj(p, "v", wv, NC, xb, psq, make_evac("v", vout),
              psum_free=pf, pf_base=64)

    rst_ob.wait_all(p, "gpsimd")
    rst_eb.wait_all(p, "gpsimd")


def build_qkv(rope):
    nc = mk_nc()
    xin = nc.dram_tensor("xin", [D, TL], F32, kind="ExternalInput")
    wq = nc.dram_tensor("wq", [D, D], F32, kind="ExternalInput")
    wk = nc.dram_tensor("wk", [D, D], F32, kind="ExternalInput")
    wv = nc.dram_tensor("wv", [D, D], F32, kind="ExternalInput")
    qout = nc.dram_tensor("qout", [D, TL], BF16, kind="ExternalOutput")
    kout = nc.dram_tensor("kout", [D, TL], BF16, kind="ExternalOutput")
    vout = nc.dram_tensor("vout", [TL, D], BF16, kind="ExternalOutput")
    cos_in = sin_in = None
    if rope:
        cos_in = nc.dram_tensor("cosin", [128, TL], F32, kind="ExternalInput")
        sin_in = nc.dram_tensor("sinin", [128, TL], F32, kind="ExternalInput")
    with ExitStack() as ctx:
        ctx.enter_context(nc.allow_low_precision(reason="bf16 compute per stated tolerance"))
        p = P(nc, ctx)
        emit_qkv(nc, p, ctx, "", rope, xin, wq, wk, wv, qout, kout, vout, cos_in, sin_in)
        p.flush("qkv")
    return nc



def emit_attn(nc, p, ctx, causal, scale, qin, kag, vag, att, msk_in=None):
    sb = lambda n, s, d: ctx.enter_context(nc.sbuf_tensor(n, s, d, allow_name_mangling=True))
    psa = lambda n, s, d=F32: ctx.enter_context(nc.psum_tensor(n, s, d, allow_name_mangling=True))

    ksb = [sb(f"ksb{i}", [128, 64 * 128], BF16) for i in range(2)]
    vsb = [sb(f"vsb{i}", [128, 64 * 128], BF16) for i in range(2)]
    qsb = [sb(f"qsb{i}", [128, TL], BF16) for i in range(2)]
    esb = [sb(f"esb{i}", [128, 1024], BF16) for i in range(3)]
    dsb = [sb(f"dsb{i}", [128, 1024], F32) for i in range(2)]
    dbf = sb("dbf", [128, 1024], BF16)
    rbf = sb("rbf", [1, 1024], BF16)
    bcs = sb("bcs", [128, 512], F32)
    att_sb = [sb(f"attsb{i}", [128, TL], BF16) for i in range(2)]
    onec = sb("onec", [128, 1], BF16)
    oner = sb("oner", [1, 128], BF16)
    if causal:
        msk_sb = sb("msk_sb", [128, 8 * 128], BF16)

    pst = [psa(f"pst{i}", [128, 1024]) for i in range(2)]
    ps_pv = [psa(f"ps_pv{i}", [128, 512]) for i in range(2)]
    ps_den = psa("ps_den", [1, 512])
    ps_bc = psa("ps_bc", [128, 512])

    rkv = [p.ctr("rkv0"), p.ctr("rkv1")]
    rst = Ring(p, "rst", 2)
    a_s = p.ctr("a_s")
    a_x = p.ctr("a_x")
    a_e = p.ctr("a_e")
    a_p = p.ctr("a_p")
    a_v = p.ctr("a_v")
    a_x2 = p.ctr("a_x2")
    s_aux = p.ctr("aux")

    p.op("vector", lambda e: e.memset(onec[:], 1.0), s_aux)
    p.op("vector", lambda e: e.memset(oner[:], 1.0), s_aux)
    if causal:
        p.dma("sync", msk_sb[:].rearrange("p (r q) -> p r q", q=128),
              msk_in.ap().rearrange("(r p) q -> p r q", p=128), s_aux)
        p.wait("vector", s_aux, s_aux.n)

    kv_ready = {}

    def loads(h):
        hb = h % 2
        s = rkv[hb]
        for k8 in range(8):
            p.dma("sync", ksb[hb][:, k8 * 1024:(k8 + 1) * 1024].rearrange("p (c t) -> p c t", t=128),
                  kag.ap().rearrange("(c x) t -> x c t", c=8)[h * 128:(h + 1) * 128, :, k8 * 128:(k8 + 1) * 128], s)
            p.dma("sync", vsb[hb][:, k8 * 1024:(k8 + 1) * 1024].rearrange("p (c t) -> p c t", t=128),
                  vag.ap().rearrange("(c y) e -> y c e", c=8)[k8 * 128:(k8 + 1) * 128, :, h * 128:(h + 1) * 128], s)
        p.dma("sync", qsb[hb][:], qin[h * 128:(h + 1) * 128, :], s)
        kv_ready[h] = s.n

    gj0 = [None]
    scount = {}
    xcount = {}
    ecount = {}
    pcount = {}
    div_done = {}
    dme_done = {}

    loads(0)
    loads(1)

    nB = 64
    npair = 32 if causal else 64

    for h in range(NH):
        hb = h % 2
        p.wait("tensor", rkv[hb], kv_ready[h])

        p.waitf("vector", div_done.get(h - 2))
        p.op("vector", lambda e: e.memset(dsb[0][:], 0.0), a_v)
        p.op("vector", lambda e: e.memset(dsb[1][:], 0.0), a_v)
        dmn = a_v.n

        if gj0[0] is None:
            gj0[0] = 0
        gj_head = gj0[0]

        def colA(j):
            return (j // 8) * 128 if causal else 0

        def colB(j):
            if causal and j >= 32:
                return (j // 8 - 4) * 128
            return 0

        for j in range(nB):
            gj = gj0[0]; gj0[0] += 1
            paired = j < npair
            c0a = colA(j)
            c0b = colB(j)
            lo = c0a if paired else 512 + c0b
            p.waitf("tensor", ecount.get(gj - 2))
            if j == 0:
                p.waitf("tensor", div_done.get(h - 2))
            if paired:
                p.op("tensor", lambda e, j=j, gj=gj, c0a=c0a, hb=hb: e.matmul(
                    pst[gj % 2][:, c0a:512],
                    ksb[hb][:, j * 128:(j + 1) * 128],
                    qsb[hb][:, c0a:512], start=True, stop=True), None)
            p.op("tensor", lambda e, j=j, gj=gj, c0b=c0b, hb=hb: e.matmul(
                pst[gj % 2][:, 512 + c0b:1024],
                ksb[hb][:, j * 128:(j + 1) * 128],
                qsb[hb][:, 512 + c0b:1024], start=True, stop=True), a_s)
            scount[gj] = a_s.n
            if j >= 2:
                jj = j - 2
                gjj = gj - 2
                p.waitf("tensor", ecount.get(gjj))
                if jj < npair:
                    p.op("tensor", lambda e, jj=jj, gjj=gjj, hb=hb, c0=colA(jj): e.matmul(
                        ps_pv[0][:, c0:512],
                        vsb[hb][:, jj * 128:(jj + 1) * 128],
                        esb[gjj % 3][:, c0:512],
                        start=(jj == 0), stop=(jj == npair - 1)), None)
                p.op("tensor", lambda e, jj=jj, gjj=gjj, hb=hb, c0=colB(jj): e.matmul(
                    ps_pv[1][:, c0:512],
                    vsb[hb][:, jj * 128:(jj + 1) * 128],
                    esb[gjj % 3][:, 512 + c0:1024],
                    start=(jj == 0), stop=(jj == nB - 1)), a_p)
                pcount[gjj] = (a_p, a_p.n)
            p.wait("scalar", a_s, scount[gj])
            p.waitf("scalar", pcount.get(gj - 3))
            p.waitf("scalar", ecount.get(gj - 3))
            p.op("scalar", lambda e, gj=gj, lo=lo: e.activation(
                esb[gj % 3][:, lo:1024], pst[gj % 2][:, lo:1024], AF.Exp,
                scale=scale), a_x)
            xcount[gj] = a_x.n
            p.wait("vector", a_x, xcount[gj])
            if j <= 1:
                p.wait("vector", a_v, dmn)
            else:
                p.waitf("vector", ecount.get(gj - 2))
            mr = j % 8
            if causal and paired:
                p.op("vector", lambda e, gj=gj, c0a=c0a, mr=mr: e.tensor_tensor(
                    esb[gj % 3][:, c0a:c0a + 128], esb[gj % 3][:, c0a:c0a + 128],
                    msk_sb[:, mr * 128:(mr + 1) * 128], ALU.mult), a_v)
                p.wait("vector", a_v, a_v.n)
            if causal and j >= 32:
                p.op("vector", lambda e, gj=gj, c0b=c0b, mr=mr: e.tensor_tensor(
                    esb[gj % 3][:, 512 + c0b:512 + c0b + 128], esb[gj % 3][:, 512 + c0b:512 + c0b + 128],
                    msk_sb[:, mr * 128:(mr + 1) * 128], ALU.mult), a_v)
                p.wait("vector", a_v, a_v.n)
            p.op("vector", lambda e, gj=gj, lo=lo: e.tensor_tensor(
                dsb[gj % 2][:, lo:1024], dsb[gj % 2][:, lo:1024],
                esb[gj % 3][:, lo:1024], ALU.add), a_e)
            ecount[gj] = (a_e, a_e.n)

        for jj in (nB - 2, nB - 1):
            gjj = gj_head + jj
            p.waitf("tensor", ecount.get(gjj))
            if jj < npair:
                p.op("tensor", lambda e, jj=jj, gjj=gjj, hb=hb, c0=colA(jj): e.matmul(
                    ps_pv[0][:, c0:512],
                    vsb[hb][:, jj * 128:(jj + 1) * 128],
                    esb[gjj % 3][:, c0:512],
                    start=(jj == 0), stop=(jj == npair - 1)), None)
            p.op("tensor", lambda e, jj=jj, gjj=gjj, hb=hb, c0=colB(jj): e.matmul(
                ps_pv[1][:, c0:512],
                vsb[hb][:, jj * 128:(jj + 1) * 128],
                esb[gjj % 3][:, 512 + c0:1024],
                start=(jj == 0), stop=(jj == nB - 1)), a_p)
            pcount[gjj] = (a_p, a_p.n)
        hp_end = a_p.n

        p.waitf("vector", ecount[gj0[0] - 1])
        p.op("vector", lambda e: e.tensor_add(dsb[0][:], dsb[0][:], dsb[1][:]), a_v)
        p.wait("vector", a_v, a_v.n)
        p.op("vector", lambda e: e.tensor_copy(dbf[:], dsb[0][:]), a_v)
        castn = a_v.n
        div_done[h] = None
        for half, (pvp, off) in enumerate(((0, 0), (1, 512))):
            p.wait("tensor", a_v, castn if half == 0 else a_v.n)
            if half == 0 and h >= 1:
                p.waitf("tensor", div_done.get(h - 1))
            p.op("tensor", lambda e, off=off: e.matmul(
                ps_den[:], onec[:], dbf[:, off:off + 512], start=True, stop=True), a_s)
            on_n = a_s.n
            p.wait("vector", a_s, on_n)
            p.op("vector", lambda e, off=off: e.reciprocal(rbf[:, off:off + 512], ps_den[:]), a_v)
            rc_n = a_v.n
            p.wait("tensor", a_v, rc_n)
            p.op("tensor", lambda e, off=off: e.matmul(
                ps_bc[:], oner[:], rbf[:, off:off + 512], start=True, stop=True), a_s)
            bc_n = a_s.n
            p.wait("scalar", a_s, bc_n)
            p.op("scalar", lambda e: e.activation(bcs[:], ps_bc[:], AF.Copy, scale=1.0), a_x2)
            bce_n = a_x2.n
            p.wait("vector", a_x2, bce_n)
            p.wait("vector", a_p, hp_end)
            p.waitf("vector", rst.free(h - 2) if h >= 2 else None)
            p.op("vector", lambda e, pvp=pvp, off=off, hb=hb: e.tensor_tensor(
                att_sb[hb][:, off:off + 512], ps_pv[pvp][:], bcs[:], ALU.mult), a_v)
        div_done[h] = (a_v, a_v.n)

        p.wait("sync", a_v, div_done[h][1])
        rst.dma(p, "sync", h, att[h * 128:(h + 1) * 128, :], att_sb[hb][:])
        if h + 2 < NH:
            p.wait("sync", a_p, hp_end)
            loads(h + 2)

    rst.wait_all(p, "gpsimd")


def build_attn(causal, scale):
    nc = mk_nc()
    qin = nc.dram_tensor("qin", [D, TL], BF16, kind="ExternalInput")
    kag = nc.dram_tensor("kag", [NCORE * D, TL], BF16, kind="ExternalInput")
    vag = nc.dram_tensor("vag", [NCORE * TL, D], BF16, kind="ExternalInput")
    att = nc.dram_tensor("att", [D, TL], BF16, kind="ExternalOutput")
    msk_in = None
    if causal:
        msk_in = nc.dram_tensor("msk", [8 * 128, 128], BF16, kind="ExternalInput")
    with ExitStack() as ctx:
        ctx.enter_context(nc.allow_low_precision(reason="bf16 compute per stated tolerance"))
        p = P(nc, ctx)
        emit_attn(nc, p, ctx, causal, scale, qin, kag, vag, att, msk_in)
        p.flush("attn")
    return nc



def emit_post(nc, p, ctx, tag, layer, attin, wo, xprev, xout, xr_dram, x1_dram,
              rw=None, rbias=None, eb64=None,
              wbase=None, bb_in=None, wr=None, astk=None, bstk=None):
    if layer == 0:
        NE = 64
        KK = 8
    else:
        ex8 = np.zeros((8, 128), dtype=BF)
        for e8 in range(8):
            ex8[e8, e8 * 16:(e8 + 1) * 16] = 1.0
        ex8_d = nc.inline_tensor(ex8, f"ex8{tag}")
        NE = 8
        KK = 2
    identb_d = nc.inline_tensor(np.eye(128, dtype=BF), f"identb{tag}")
    identf_d = nc.inline_tensor(np.eye(128, dtype=np.float32), f"identf{tag}")
    sb = lambda n, s, d: ctx.enter_context(nc.sbuf_tensor(n, s, d, allow_name_mangling=True))

    ab = sb("ab", [128, NC * 1024], BF16)
    xr_sb = [sb(f"xr{i}", [128, 1024], F32) for i in range(2)]
    xp_sb = [sb(f"xp{i}", [128, 1024], F32) for i in range(2)]
    xrb_sb = [sb(f"xrb{i}", [128, 1024], BF16) for i in range(2)]
    sq_sb = [sb(f"sq{i}", [128, 1024], BF16) for i in range(2)]
    x1c_sb = [sb(f"x1c{i}", [128, 1024], F32) for i in range(2)]
    onec = sb("onec", [128, 1], BF16)
    oner = sb("oner", [1, 128], BF16)
    epsb = sb("epsb", [1, 1], F32)
    identb = sb("identb_sb", [128, 128], BF16)
    identf = sb("identf_sb", [128, 128], F32)
    m_sb = sb("m_sb", [1, 1024], F32)
    v_sb = sb("v_sb", [1, 1024], F32)
    t1_sb = sb("t1_sb", [1, 1024], F32)
    rstd_sb = sb("rstd_sb", [1, 1024], F32)
    rstdb = sb("rstdb", [1, 1024], BF16)
    mrb = sb("mrb", [1, 1024], BF16)

    s_aux = p.ctr("aux")
    a_r = p.ctr("a_r")
    a_c = p.ctr("a_c")
    a_q = p.ctr("a_q")
    a_sm = p.ctr("a_sm")
    a_ln = p.ctr("a_ln")
    a_f = p.ctr("a_f")
    a_t = p.ctr("a_t")

    p.dma("sync", identb[:], identb_d[:], s_aux)
    p.dma("sync", identf[:], identf_d[:], s_aux)
    p.op("vector", lambda e: e.memset(onec[:], 1.0), s_aux)
    p.op("vector", lambda e: e.memset(oner[:], 1.0), s_aux)
    p.op("vector", lambda e: e.memset(epsb[:], EPS), s_aux)
    aux_n = s_aux.n

    ab_ready = emit_load_xb_bf(p, "ab", attin, ab)

    def ln_finalize_apply(stats, stats_total, src_dram, src_ring, dst_dram, cast_to_ab, tagn):
        ld = Ring(p, f"{tagn}_ld", 2)
        st = Ring(p, f"{tagn}_st", 2)
        src_ring.wait_all(p, "sync")
        for eng in ("sync", "vector", "scalar"):
            p.wait(eng, a_sm, stats_total)
            p.wait(eng, a_r, a_r.n)
            p.wait(eng, a_c, a_c.n)
            p.wait(eng, a_q, a_q.n)
        for hh in range(2):
            p.op("vector", lambda e, hh=hh: e.tensor_scalar(
                m_sb[:, hh * 512:(hh + 1) * 512], stats[hh][:], 1.0 / D, None, ALU.mult), a_ln)
            p.op("vector", lambda e, hh=hh: e.tensor_scalar(
                v_sb[:, hh * 512:(hh + 1) * 512], stats[2 + hh][:], 1.0 / D, None, ALU.mult), a_ln)
        p.wait("vector", a_ln, a_ln.n)
        p.op("vector", lambda e: e.tensor_mul(t1_sb[:], m_sb[:], m_sb[:]), a_ln)
        p.wait("vector", a_ln, a_ln.n)
        p.op("vector", lambda e: e.tensor_sub(v_sb[:], v_sb[:], t1_sb[:]), a_ln)
        vn = a_ln.n
        p.wait("scalar", a_ln, vn)
        p.op("scalar", lambda e: e.activation(t1_sb[:], v_sb[:], AF.Ln, bias=epsb[:], scale=1.0), a_ln)
        p.wait("scalar", a_ln, a_ln.n)
        p.op("scalar", lambda e: e.activation(rstd_sb[:], t1_sb[:], AF.Exp, scale=-0.5), a_ln)
        rn = a_ln.n
        p.wait("vector", a_ln, rn)
        p.op("vector", lambda e: e.tensor_mul(t1_sb[:], m_sb[:], rstd_sb[:]), a_ln)
        p.op("vector", lambda e: e.tensor_copy(rstdb[:], rstd_sb[:]), a_ln)
        p.wait("vector", a_ln, a_ln.n)
        p.op("vector", lambda e: e.tensor_copy(mrb[:], t1_sb[:]), a_ln)
        castn = a_ln.n
        with nc.psum_tensor(f"{tagn}_rsb", [128, 1024], F32, allow_name_mangling=True) as rs_bc, \
             nc.psum_tensor(f"{tagn}_mrb", [128, 1024], F32, allow_name_mangling=True) as mr_bc:
            p.wait("tensor", a_ln, castn)
            for hh in range(2):
                p.op("tensor", lambda e, hh=hh: e.matmul(
                    rs_bc[:, hh * 512:(hh + 1) * 512], oner[:],
                    rstdb[:, hh * 512:(hh + 1) * 512], start=True, stop=True), a_ln)
                p.op("tensor", lambda e, hh=hh: e.matmul(
                    mr_bc[:, hh * 512:(hh + 1) * 512], oner[:],
                    mrb[:, hh * 512:(hh + 1) * 512], start=True, stop=True), a_ln)
            bcn = a_ln.n
            apply_done = {}
            cast_done = {}
            for f in range(NC):
                p.waitf("sync", apply_done.get(f - 2))
                ld.dma(p, "sync", f, xr_sb[f % 2][:], src_dram[f * 128:(f + 1) * 128, :])
                ld.wait(p, "vector", f)
                p.wait("vector", a_ln, bcn)
                if f >= 2:
                    p.waitf("vector", apply_done.get(f - 2))
                    if cast_to_ab:
                        p.wait("vector", a_c, cast_done[f - 2])
                p.op("vector", lambda e, f=f: e.tensor_mul(x1c_sb[f % 2][:], xr_sb[f % 2][:], rs_bc[:]), a_ln)
                p.wait("vector", a_ln, a_ln.n)
                p.op("vector", lambda e, f=f: e.tensor_sub(x1c_sb[f % 2][:], x1c_sb[f % 2][:], mr_bc[:]), a_ln)
                here = a_ln.n
                if cast_to_ab:
                    p.wait("scalar", a_ln, here)
                    p.op("scalar", lambda e, f=f: e.activation(
                        ab[:, f * 1024:(f + 1) * 1024], x1c_sb[f % 2][:], AF.Copy, scale=1.0), a_c)
                    cast_done[f] = a_c.n
                p.wait("sync", a_ln, here)
                st.dma(p, "sync", f, dst_dram[f * 128:(f + 1) * 128, :], x1c_sb[f % 2][:])
                apply_done[f] = st.free(f)
            st.wait_all(p, "tensor")
        return st

    rxp1 = Ring(p, "rxp1", 2)
    rxr1 = Ring(p, "rxr1", 2)
    stats_pend = []

    def mk_evac(rxp, rxr, stats, a_sm_base, xr_src, resid_dram):
        def evac(ec, psten, defer, mm_need, s_mm):
            rxp.dma(p, "sync", ec, xp_sb[ec % 2][:], resid_dram[ec * 128:(ec + 1) * 128, :])
            p.wait("vector", s_mm, mm_need)
            rxp.wait(p, "vector", ec)
            if ec >= 2:
                p.waitf("vector", rxr.free(ec - 2))
            p.op("vector", lambda e, ec=ec, psten=psten: xr_src(e, ec, psten), a_r)
            rn = a_r.n
            p.wait("sync", a_r, rn)
            rxr.dma(p, "sync", ec, xr_dram[ec * 128:(ec + 1) * 128, :], xr_sb[ec % 2][:])
            p.wait("scalar", a_r, rn)
            if ec >= 2:
                p.wait("scalar", a_sm, a_sm_base + ec - 1)
            p.op("scalar", lambda e, ec=ec: e.activation(
                xrb_sb[ec % 2][:], xr_sb[ec % 2][:], AF.Copy, scale=1.0), a_c)
            cn = a_c.n
            p.wait("vector", a_c, cn)
            if ec >= 2:
                p.wait("vector", a_sm, a_sm_base + ec - 1)
            p.op("vector", lambda e, ec=ec: e.tensor_mul(
                sq_sb[ec % 2][:], xrb_sb[ec % 2][:], xrb_sb[ec % 2][:]), a_q)
            qn = a_q.n
            def emit_def(ec=ec, cn=cn, qn=qn):
                p.wait("tensor", a_c, cn)
                p.wait("tensor", a_q, qn)
                if ec == 0:
                    p.wait("tensor", s_aux, aux_n)
                for hh in range(2):
                    p.op("tensor", lambda e, ec=ec, hh=hh: e.matmul(
                        stats[hh][:], onec[:], xrb_sb[ec % 2][:, hh * 512:(hh + 1) * 512],
                        start=(ec == 0), stop=(ec == NC - 1), skip_group_check=True), None)
                    p.op("tensor", lambda e, ec=ec, hh=hh: e.matmul(
                        stats[2 + hh][:], onec[:], sq_sb[ec % 2][:, hh * 512:(hh + 1) * 512],
                        start=(ec == 0), stop=(ec == NC - 1), skip_group_check=True),
                        a_sm if hh == 1 else None)
            defer(emit_def)
            return (a_r, rn)
        return evac

    with ExitStack() as s1s:
        stats = [s1s.enter_context(nc.psum_tensor(f"st{i}", [1, 512], F32, allow_name_mangling=True)) for i in range(4)]
        a_r_base1 = a_r.n
        with ExitStack() as s1p:
            psq2 = [s1p.enter_context(nc.psum_tensor(f"psq{i}", [128, 1024], F32, allow_name_mangling=True)) for i in range(2)]
            ev1 = mk_evac(rxp1, rxr1, stats, 0,
                          lambda e, ec, psten: e.tensor_add(xr_sb[ec % 2][:], psten[:], xp_sb[ec % 2][:]),
                          xprev)
            def ev1g(ec, psten, defer, mm_need, s_mm):
                p.wait("sync", a_r, a_r_base1 + max(0, ec - 1))
                return ev1(ec, psten, defer, mm_need, s_mm)
            emit_proj(p, "wo", wo, NC, ab, psq2, ev1g,
                      pre_mm_waits=[(ab_ready, ab_ready.n)])
        st1 = ln_finalize_apply(stats, a_sm.n, xr_dram, rxr1, x1_dram, True, "ln1")
    ab2_ready = a_c.n

    lgf = sb("lgf", [64, 1024], F32)
    tokl2 = sb("tokl2", [128, 8 * NE], F32)
    ex_sb = sb("ex_t", [128, NE], F32)
    cur_sb = sb("cur_t", [128, NE], F32)
    w_sb = sb("w_t", [128, NE], F32)
    wb_sb = sb("wb_t", [128, NE], BF16)
    mx_sb = sb("mx_t", [128, 1], F32)
    nmx_sb = sb("nmx_t", [128, 1], F32)
    sm_sb = sb("sm_t", [128, 1], F32)
    rc_sb = sb("rc_t", [128, 1], F32)
    t8_sb = sb("t8_t", [128, 1], F32)
    wdt = sb("wdt", [NE, 1024], BF16)
    wsb_sb = sb("wsb_sb", [128, 1024], F32)
    ws1b = sb("ws1b", [1, 1024], BF16)

    if layer == 0:
        rwf = sb("rwf", [128, NC * 64], F32)
        rwb = sb("rwb", [128, NC * 64], BF16)
        ebf = sb("ebf", [64, D], F32)
        ebb = sb("ebb", [64, D], BF16)
        rb_sb = sb("rb_sb", [64, 1], F32)
        p.dma("sync", rwf[:].rearrange("p (d e) -> p d e", e=64),
              rw.ap().rearrange("(d p) e -> p d e", p=128), s_aux)
        p.dma("sync", ebf[:], eb64[:], s_aux)
        p.dma("sync", rb_sb[:], rbias[:], s_aux)
        p.wait("vector", s_aux, s_aux.n)
        p.op("vector", lambda e: e.tensor_copy(rwb[:], rwf[:]), a_f)
        p.op("vector", lambda e: e.tensor_copy(ebb[:], ebf[:]), a_f)
        wready = a_f.n
        wchunks = rwb
    else:
        bbf = sb("bbf", [128, 32], F32)
        wrf = sb("wrf", [128, NC * 8], F32)
        wrb = sb("wrb", [128, NC * 8], BF16)
        asf = sb("asf", [128, NC * 128], F32)
        asb = sb("asb", [128, NC * 128], BF16)
        bsf = sb("bsf", [128, D], F32)
        bsb = sb("bsb", [128, D], BF16)
        ex8_sb = sb("ex8_sb", [8, 128], BF16)
        hs_sb = sb("hs_sb", [128, 1024], BF16)
        cb_sb = sb("cb_sb", [128, 1024], F32)
        p.dma("sync", bbf[:].rearrange("p (f o) -> p f o", o=1),
              bb_in.ap().rearrange("(f p) o -> p f o", p=128), s_aux,
              allow_slow_non_contiguous=True)
        p.dma("sync", wrf[:].rearrange("p (d e) -> p d e", e=8),
              wr.ap().rearrange("(d p) e -> p d e", p=128), s_aux)
        p.dma("sync", asf[:].rearrange("p (d e) -> p d e", e=128),
              astk.ap().rearrange("(d p) e -> p d e", p=128), s_aux)
        p.dma("sync", bsf[:], bstk[:], s_aux)
        p.dma("sync", ex8_sb[:], ex8_d[:], s_aux)
        p.wait("vector", s_aux, s_aux.n)
        p.op("vector", lambda e: e.tensor_copy(wrb[:], wrf[:]), a_f)
        p.op("vector", lambda e: e.tensor_copy(asb[:], asf[:]), a_f)
        p.op("vector", lambda e: e.tensor_copy(bsb[:], bsf[:]), a_f)
        wready = a_f.n
        wchunks = wrb

    def emit_router_and_topk(n_e, kk):
        with ExitStack() as s2:
            pr = s2.enter_context(nc.psum_tensor("pr", [n_e, 1024], F32, allow_name_mangling=True))
            ptt = s2.enter_context(nc.psum_tensor("ptt", [128, n_e], F32, allow_name_mangling=True))
            ptw = s2.enter_context(nc.psum_tensor("ptw", [n_e, 128], BF16, allow_name_mangling=True))
            p.wait("tensor", a_f, wready)
            p.wait("tensor", a_c, ab2_ready)
            for dd in range(NC):
                for t in range(2):
                    p.op("tensor", lambda e, dd=dd, t=t: e.matmul(
                        pr[:, t * 512:(t + 1) * 512],
                        wchunks[:, dd * n_e:(dd + 1) * n_e],
                        ab[:, dd * 1024 + t * 512: dd * 1024 + (t + 1) * 512],
                        start=(dd == 0), stop=(dd == NC - 1)),
                        a_f if (dd == NC - 1 and t == 1) else None)
            prn = a_f.n
            p.wait("vector", a_f, prn)
            if layer == 0:
                p.op("vector", lambda e: e.tensor_scalar(
                    lgf[0:n_e, :], pr[:], rb_sb[:], None, ALU.add), a_f)
            else:
                p.op("vector", lambda e: e.tensor_copy(lgf[0:n_e, :], pr[:]), a_f)
            for tg in range(8):
                p.wait("tensor", a_f, a_f.n)
                p.op("tensor", lambda e, tg=tg: e.transpose(
                    ptt[:, 0:n_e], lgf[0:n_e, tg * 128:(tg + 1) * 128],
                    identf[0:n_e, 0:n_e]), a_f)
                p.wait("scalar", a_f, a_f.n)
                p.op("scalar", lambda e, tg=tg: e.activation(
                    tokl2[:, tg * n_e:(tg + 1) * n_e], ptt[:, 0:n_e], AF.Copy, scale=1.0), a_f)
            for tg in range(8):
                base = tokl2[:, tg * n_e:(tg + 1) * n_e]
                p.wait("vector", a_f, a_f.n)
                p.wait("vector", a_t, a_t.n)
                p.op("vector", lambda e, base=base: e.reduce_max(mx_sb[:], base, axis=AX.X), a_t)
                p.wait("vector", a_t, a_t.n)
                p.op("vector", lambda e: e.tensor_scalar(nmx_sb[:], mx_sb[:], -1.0, None, ALU.mult), a_t)
                nmn = a_t.n
                p.wait("scalar", a_t, nmn)
                p.op("scalar", lambda e, base=base: e.activation(
                    ex_sb[:], base, AF.Exp, bias=nmx_sb[:], scale=1.0), a_t)
                p.wait("vector", a_t, a_t.n)
                p.op("vector", lambda e: e.reduce_sum(sm_sb[:], ex_sb[:], axis=AX.X), a_t)
                p.wait("vector", a_t, a_t.n)
                p.op("vector", lambda e: e.reciprocal(rc_sb[:], sm_sb[:]), a_t)
                p.wait("vector", a_t, a_t.n)
                p.op("vector", lambda e: e.tensor_scalar(
                    ex_sb[:], ex_sb[:], rc_sb[:], None, ALU.mult), a_t)
                p.wait("vector", a_t, a_t.n)
                p.op("vector", lambda e: e.tensor_copy(cur_sb[:], ex_sb[:]), a_t)
                for ki in range(kk):
                    p.wait("vector", a_t, a_t.n)
                    p.op("vector", lambda e: e.reduce_max(t8_sb[:], cur_sb[:], axis=AX.X), a_t)
                    if ki < kk - 1:
                        p.wait("vector", a_t, a_t.n)
                        p.op("vector", lambda e: e.tensor_scalar(
                            w_sb[:], cur_sb[:], t8_sb[:], None, ALU.is_ge), a_t)
                        p.wait("vector", a_t, a_t.n)
                        p.op("vector", lambda e: e.scalar_tensor_tensor(
                            cur_sb[:], w_sb[:], -1e30, cur_sb[:], ALU.mult, ALU.add), a_t)
                p.wait("vector", a_t, a_t.n)
                p.op("vector", lambda e: e.tensor_scalar(
                    w_sb[:], ex_sb[:], t8_sb[:], None, ALU.is_ge), a_t)
                p.wait("vector", a_t, a_t.n)
                p.op("vector", lambda e: e.tensor_mul(w_sb[:], w_sb[:], ex_sb[:]), a_t)
                if layer == 1:
                    p.wait("vector", a_t, a_t.n)
                    p.op("vector", lambda e: e.reduce_sum(sm_sb[:], w_sb[:], axis=AX.X), a_t)
                    p.wait("vector", a_t, a_t.n)
                    p.op("vector", lambda e: e.reciprocal(rc_sb[:], sm_sb[:]), a_t)
                    p.wait("vector", a_t, a_t.n)
                    p.op("vector", lambda e: e.tensor_scalar(
                        w_sb[:], w_sb[:], rc_sb[:], None, ALU.mult), a_t)
                p.wait("vector", a_t, a_t.n)
                p.op("vector", lambda e: e.tensor_copy(wb_sb[:], w_sb[:]), a_t)
                wbn = a_t.n
                p.wait("tensor", a_t, wbn)
                p.wait("tensor", a_f, a_f.n)
                p.op("tensor", lambda e: e.transpose(
                    ptw[:, :], wb_sb[:, 0:n_e], identb[:]), a_f)
                p.wait("scalar", a_f, a_f.n)
                p.op("scalar", lambda e, tg=tg: e.activation(
                    wdt[:, tg * 128:(tg + 1) * 128], ptw[0:n_e, :], AF.Copy, scale=1.0), a_f)
            p.wait("tensor", a_f, a_f.n)

    emit_router_and_topk(NE, KK)

    if layer == 0:
        with ExitStack() as s3:
            ps_ws = [s3.enter_context(nc.psum_tensor(f"ps_ws{i}", [1, 512], F32, allow_name_mangling=True)) for i in range(2)]
            for hh in range(2):
                p.op("tensor", lambda e, hh=hh: e.matmul(
                    ps_ws[hh][:], onec[0:64, :], wdt[:, hh * 512:(hh + 1) * 512],
                    start=True, stop=True), a_f)
            wsn = a_f.n
            p.wait("vector", a_f, wsn)
            for hh in range(2):
                p.op("vector", lambda e, hh=hh: e.tensor_scalar(
                    ws1b[:, hh * 512:(hh + 1) * 512], ps_ws[hh][:], 1.0, None, ALU.add), a_f)
            p.wait("tensor", a_f, a_f.n)
            ps_wb = s3.enter_context(nc.psum_tensor("ps_wb", [128, 1024], F32, allow_name_mangling=True))
            for hh in range(2):
                p.op("tensor", lambda e, hh=hh: e.matmul(
                    ps_wb[:, hh * 512:(hh + 1) * 512], oner[:],
                    ws1b[:, hh * 512:(hh + 1) * 512], start=True, stop=True), a_f)
            p.wait("scalar", a_f, a_f.n)
            p.op("scalar", lambda e: e.activation(wsb_sb[:], ps_wb[:], AF.Copy, scale=1.0), a_f)
            p.wait("tensor", a_f, a_f.n)
        wsbn = a_f.n
        rx1b = Ring(p, "rx1b", 2)
        rxr2 = Ring(p, "rxr2", 2)
        st1.wait_all(p, "sync")
        p.wait("sync", a_r, a_r.n)
        with ExitStack() as s5s:
            stats2 = [s5s.enter_context(nc.psum_tensor(f"st2_{i}", [1, 512], F32, allow_name_mangling=True)) for i in range(4)]
            s5p = ExitStack()
            pbm = [s5p.enter_context(nc.psum_tensor(f"pbm{i}", [128, 1024], F32, allow_name_mangling=True)) for i in range(2)]
            stat_pend = []
            a_r_base = a_r.n
            a_sm_base = a_sm.n
            for ec in range(NC):
                for fn2 in stat_pend:
                    fn2()
                stat_pend = []
                if ec >= 2:
                    p.waitf("tensor", (a_r, a_r_base + ec - 1))
                else:
                    p.wait("tensor", a_f, wsbn)
                for t in range(2):
                    p.op("tensor", lambda e, ec=ec, t=t: e.matmul(
                        pbm[ec % 2][:, t * 512:(t + 1) * 512],
                        ebb[:, ec * 128:(ec + 1) * 128],
                        wdt[:, t * 512:(t + 1) * 512], start=True, stop=True),
                        a_f if t == 1 else None)
                bmn = a_f.n
                p.wait("sync", a_r, a_r_base + max(0, ec - 1))
                rx1b.dma(p, "sync", ec, xp_sb[ec % 2][:], x1_dram[ec * 128:(ec + 1) * 128, :])
                rx1b.wait(p, "vector", ec)
                p.wait("vector", a_f, bmn)
                if ec >= 2:
                    p.waitf("vector", rxr2.free(ec - 2))
                p.wait("vector", a_ln, a_ln.n)
                p.op("vector", lambda e, ec=ec: e.tensor_mul(
                    xr_sb[ec % 2][:], xp_sb[ec % 2][:], wsb_sb[:]), a_ln)
                p.wait("vector", a_ln, a_ln.n)
                p.op("vector", lambda e, ec=ec: e.tensor_add(
                    xr_sb[ec % 2][:], xr_sb[ec % 2][:], pbm[ec % 2][:]), a_r)
                rn = a_r.n
                p.wait("sync", a_r, rn)
                rxr2.dma(p, "sync", ec, xr_dram[ec * 128:(ec + 1) * 128, :], xr_sb[ec % 2][:])
                p.wait("scalar", a_r, rn)
                if ec >= 2:
                    p.wait("scalar", a_sm, a_sm_base + ec - 1)
                p.op("scalar", lambda e, ec=ec: e.activation(
                    xrb_sb[ec % 2][:], xr_sb[ec % 2][:], AF.Copy, scale=1.0), a_c)
                cn = a_c.n
                p.wait("vector", a_c, cn)
                if ec >= 2:
                    p.wait("vector", a_sm, a_sm_base + ec - 1)
                p.op("vector", lambda e, ec=ec: e.tensor_mul(
                    sq_sb[ec % 2][:], xrb_sb[ec % 2][:], xrb_sb[ec % 2][:]), a_q)
                qn = a_q.n
                def emit_def(ec=ec, cn=cn, qn=qn):
                    p.wait("tensor", a_c, cn)
                    p.wait("tensor", a_q, qn)
                    for hh in range(2):
                        p.op("tensor", lambda e, ec=ec, hh=hh: e.matmul(
                            stats2[hh][:], onec[:], xrb_sb[ec % 2][:, hh * 512:(hh + 1) * 512],
                            start=(ec == 0), stop=(ec == NC - 1), skip_group_check=True), None)
                        p.op("tensor", lambda e, ec=ec, hh=hh: e.matmul(
                            stats2[2 + hh][:], onec[:], sq_sb[ec % 2][:, hh * 512:(hh + 1) * 512],
                            start=(ec == 0), stop=(ec == NC - 1), skip_group_check=True),
                            a_sm if hh == 1 else None)
                stat_pend.append(emit_def)
            for fn2 in stat_pend:
                fn2()
            s5p.close()
            stf = ln_finalize_apply(stats2, a_sm.n, xr_dram, rxr2, xout, False, "ln2")
    else:
        with ExitStack() as s3:
            ps_h = s3.enter_context(nc.psum_tensor("ps_h", [128, 1024], F32, allow_name_mangling=True))
            ps_cb = s3.enter_context(nc.psum_tensor("ps_cb", [128, 1024], F32, allow_name_mangling=True))
            p.wait("tensor", a_f, a_f.n)
            for t in range(2):
                p.op("tensor", lambda e, t=t: e.matmul(
                    ps_cb[:, t * 512:(t + 1) * 512], ex8_sb[:],
                    wdt[:, t * 512:(t + 1) * 512], start=True, stop=True), a_f)
            cbn = a_f.n
            p.wait("scalar", a_f, cbn)
            p.op("scalar", lambda e: e.activation(cb_sb[:], ps_cb[:], AF.Copy, scale=1.0), a_f)
            p.wait("tensor", a_f, a_f.n)
            for dd in range(NC):
                for t in range(2):
                    p.op("tensor", lambda e, dd=dd, t=t: e.matmul(
                        ps_h[:, t * 512:(t + 1) * 512],
                        asb[:, dd * 128:(dd + 1) * 128],
                        ab[:, dd * 1024 + t * 512: dd * 1024 + (t + 1) * 512],
                        start=(dd == 0), stop=(dd == NC - 1)),
                        a_f if (dd == NC - 1 and t == 1) else None)
            hn = a_f.n
            p.wait("vector", a_f, hn)
            p.op("vector", lambda e: e.scalar_tensor_tensor(
                hs_sb[:], ps_h[:], 2.0, cb_sb[:], ALU.mult, ALU.mult), a_f)
            p.wait("tensor", a_f, a_f.n)
        hs_n = a_f.n
        rxp2 = Ring(p, "rxp2", 2)
        rxr2 = Ring(p, "rxr2", 2)
        p.wait("sync", a_r, a_r.n)
        with ExitStack() as s6s:
            stats2 = [s6s.enter_context(nc.psum_tensor(f"st2_{i}", [1, 512], F32, allow_name_mangling=True)) for i in range(4)]
            s6p = ExitStack()
            psq2b = [s6p.enter_context(nc.psum_tensor(f"psqb{i}", [128, 1024], F32, allow_name_mangling=True)) for i in range(2)]

            def extra1(ec, psten, s_mm):
                p.wait("tensor", a_f, hs_n)
                for t in range(2):
                    p.op("tensor", lambda e, ec=ec, t=t, psten=psten: e.matmul(
                        psten[:, t * 512:(t + 1) * 512],
                        bsb[:, ec * 128:(ec + 1) * 128],
                        hs_sb[:, t * 512:(t + 1) * 512], start=False, stop=True),
                        s_mm if t == 1 else None)

            a_r_base = a_r.n
            st1.wait_all(p, "sync")
            ev2 = mk_evac(rxp2, rxr2, stats2, a_sm.n,
                          lambda e, ec, psten: e.scalar_tensor_tensor(
                              xr_sb[ec % 2][:], psten[:], bbf[:, ec:ec + 1],
                              xp_sb[ec % 2][:], ALU.add, ALU.add),
                          x1_dram)
            def ev2g(ec, psten, defer, mm_need, s_mm):
                p.wait("sync", a_r, a_r_base + max(0, ec - 1))
                return ev2(ec, psten, defer, mm_need, s_mm)
            emit_proj(p, "wb", wbase, NC, ab, psq2b, ev2g,
                      pre_mm_waits=[(a_f, hs_n), (a_c, ab2_ready)], extra_mm_cb=extra1)
            s6p.close()
            stf = ln_finalize_apply(stats2, a_sm.n, xr_dram, rxr2, xout, False, "ln2")

    stf.wait_all(p, "gpsimd")


def build_post(layer):
    nc = mk_nc()
    attin = nc.dram_tensor("attin", [D, TL], BF16, kind="ExternalInput")
    wo = nc.dram_tensor("wo", [D, D], F32, kind="ExternalInput")
    xprev = nc.dram_tensor("xprev", [D, TL], F32, kind="ExternalInput")
    xout = nc.dram_tensor("xout", [D, TL], F32, kind="ExternalOutput")
    kw = {}
    if layer == 0:
        kw["rw"] = nc.dram_tensor("rw", [D, 64], F32, kind="ExternalInput")
        kw["rbias"] = nc.dram_tensor("rbias", [64, 1], F32, kind="ExternalInput")
        kw["eb64"] = nc.dram_tensor("eb64", [64, D], F32, kind="ExternalInput")
    else:
        kw["wbase"] = nc.dram_tensor("wbase", [D, D], F32, kind="ExternalInput")
        kw["bb_in"] = nc.dram_tensor("bb", [D, 1], F32, kind="ExternalInput")
        kw["wr"] = nc.dram_tensor("wr", [D, 8], F32, kind="ExternalInput")
        kw["astk"] = nc.dram_tensor("astk", [D, 128], F32, kind="ExternalInput")
        kw["bstk"] = nc.dram_tensor("bstk", [128, D], F32, kind="ExternalInput")
    xr_dram = nc.dram_tensor("xr_scr", [D, TL], F32)
    x1_dram = nc.dram_tensor("x1_scr", [D, TL], F32)
    with ExitStack() as ctx:
        ctx.enter_context(nc.allow_low_precision(reason="bf16 compute per stated tolerance"))
        p = P(nc, ctx)
        emit_post(nc, p, ctx, "", layer, attin, wo, xprev, xout, xr_dram, x1_dram, **kw)
        p.flush("post")
    return nc



def emit_load_xb_bf(p, tag, src_dram, xb):
    s_l = p.ctr(f"{tag}_l")
    for f in range(NC):
        p.dma("sync", xb[:, f * 1024:(f + 1) * 1024], src_dram[f * 128:(f + 1) * 128, :], s_l)
    return s_l


def owned_cols(c):
    return np.concatenate([np.arange((8 * k + c) * 128, (8 * k + c) * 128 + 128)
                           for k in range(8)])


def rope_tables():
    inv = 1.0 / (10000.0 ** (np.arange(0, 128, 2, dtype=np.float64) / 128.0))
    t = np.arange(SEQ, dtype=np.float64)
    fr = np.outer(t, inv)
    emb = np.concatenate([fr, fr], axis=-1)
    return np.cos(emb).astype(np.float32), np.sin(emb).astype(np.float32)


def build_fused():
    nc = mk_nc()
    xin = nc.dram_tensor("xin", [D, TL], F32, kind="ExternalInput")
    cos_in = nc.dram_tensor("cosin", [128, TL], F32, kind="ExternalInput")
    sin_in = nc.dram_tensor("sinin", [128, TL], F32, kind="ExternalInput")
    wq0 = nc.dram_tensor("wq0", [D, D], F32, kind="ExternalInput")
    wk0 = nc.dram_tensor("wk0", [D, D], F32, kind="ExternalInput")
    wv0 = nc.dram_tensor("wv0", [D, D], F32, kind="ExternalInput")
    wo0 = nc.dram_tensor("wo0", [D, D], F32, kind="ExternalInput")
    rw = nc.dram_tensor("rw", [D, 64], F32, kind="ExternalInput")
    rbias = nc.dram_tensor("rbias", [64, 1], F32, kind="ExternalInput")
    eb64 = nc.dram_tensor("eb64", [64, D], F32, kind="ExternalInput")
    wq1 = nc.dram_tensor("wq1", [D, D], F32, kind="ExternalInput")
    wk1 = nc.dram_tensor("wk1", [D, D], F32, kind="ExternalInput")
    wv1 = nc.dram_tensor("wv1", [D, D], F32, kind="ExternalInput")
    wo1 = nc.dram_tensor("wo1", [D, D], F32, kind="ExternalInput")
    wbase = nc.dram_tensor("wbase", [D, D], F32, kind="ExternalInput")
    bb_in = nc.dram_tensor("bb", [D, 1], F32, kind="ExternalInput")
    wr = nc.dram_tensor("wr", [D, 8], F32, kind="ExternalInput")
    astk = nc.dram_tensor("astk", [D, 128], F32, kind="ExternalInput")
    bstk = nc.dram_tensor("bstk", [128, D], F32, kind="ExternalInput")
    msk_in = nc.dram_tensor("msk", [8 * 128, 128], BF16, kind="ExternalInput")
    xout = nc.dram_tensor("xout", [D, TL], F32, kind="ExternalOutput")
    q0l = nc.dram_tensor("q0l", [D, TL], BF16)
    k0l = nc.dram_tensor("k0l", [D, TL], BF16)
    v0l = nc.dram_tensor("v0l", [TL, D], BF16)
    kag0 = nc.dram_tensor("kag0", [NCORE * D, TL], BF16, addr_space="Shared")
    vag0 = nc.dram_tensor("vag0", [NCORE * TL, D], BF16, addr_space="Shared")
    att0 = nc.dram_tensor("att0", [D, TL], BF16)
    x2d = nc.dram_tensor("x2d", [D, TL], F32)
    q1l = nc.dram_tensor("q1l", [D, TL], BF16)
    k1l = nc.dram_tensor("k1l", [D, TL], BF16)
    v1l = nc.dram_tensor("v1l", [TL, D], BF16)
    kag1 = nc.dram_tensor("kag1", [NCORE * D, TL], BF16, addr_space="Shared")
    vag1 = nc.dram_tensor("vag1", [NCORE * TL, D], BF16, addr_space="Shared")
    att1 = nc.dram_tensor("att1", [D, TL], BF16)
    xr_dram = nc.dram_tensor("xr_scr", [D, TL], F32)
    x1_dram = nc.dram_tensor("x1_scr", [D, TL], F32)

    with ExitStack() as octx:
        octx.enter_context(nc.allow_low_precision(reason="bf16 compute per stated tolerance"))
        p = P(nc, octx)

        def ag_phase(name, pairs):
            cc = p.ctr("cc")
            for (s, d) in pairs:
                p.op("gpsimd", lambda e, s=s, d=d: e.collective_compute(
                    "AllGather", ALU.bypass, replica_groups=[list(range(NCORE))],
                    ins=[s[:]], outs=[d[:]]), cc)
            p.wait("gpsimd", cc, cc.n)
            p.flush(name)
            nc.all_engine_barrier()

        with ExitStack() as c:
            p.actx = c
            emit_qkv(nc, p, c, "0", True, xin, wq0, wk0, wv0, q0l, k0l, v0l, cos_in, sin_in)
        p.flush("phA")
        nc.all_engine_barrier()
        ag_phase("ag0", [(k0l, kag0), (v0l, vag0)])
        with ExitStack() as c:
            p.actx = c
            emit_attn(nc, p, c, False, L0_SCALE, q0l, kag0, vag0, att0)
        p.flush("phB")
        nc.all_engine_barrier()
        with ExitStack() as c:
            p.actx = c
            emit_post(nc, p, c, "0", 0, att0, wo0, xin, x2d, xr_dram, x1_dram,
                      rw=rw, rbias=rbias, eb64=eb64)
        p.flush("phC")
        nc.all_engine_barrier()
        with ExitStack() as c:
            p.actx = c
            emit_qkv(nc, p, c, "1", False, x2d, wq1, wk1, wv1, q1l, k1l, v1l)
        p.flush("phD")
        nc.all_engine_barrier()
        ag_phase("ag1", [(k1l, kag1), (v1l, vag1)])
        with ExitStack() as c:
            p.actx = c
            emit_attn(nc, p, c, True, L1_SCALE, q1l, kag1, vag1, att1, msk_in)
        p.flush("phE")
        nc.all_engine_barrier()
        with ExitStack() as c:
            p.actx = c
            emit_post(nc, p, c, "1", 1, att1, wo1, x2d, xout, xr_dram, x1_dram,
                      wbase=wbase, bb_in=bb_in, wr=wr, astk=astk, bstk=bstk)
        p.flush("phF")
    return nc


def kernel(**inputs):
    g = {k: np.asarray(v) for k, v in inputs.items()}
    x = np.ascontiguousarray(g["L0_attn_hidden_states"][0].T)
    wqT = np.ascontiguousarray(g["L0_attn_Wq"].T.astype(np.float32, copy=False))
    wkT = np.ascontiguousarray(g["L0_attn_Wk"].T.astype(np.float32, copy=False))
    wvT = np.ascontiguousarray(g["L0_attn_Wv"].T.astype(np.float32, copy=False))
    woT = np.ascontiguousarray(g["L0_attn_Wo"].T.astype(np.float32, copy=False))
    rwT = np.ascontiguousarray(g["L0_ffn_router_weight"].T.astype(np.float32, copy=False))
    rbias = np.ascontiguousarray(g["L0_ffn_router_bias"].reshape(64, 1).astype(np.float32, copy=False))
    eb64v = np.ascontiguousarray(g["L0_ffn_expert_bias"].astype(np.float32, copy=False))
    wq1f = g["L1_attn_wq"].astype(np.float32, copy=False)
    wq1u = np.ascontiguousarray(
        np.concatenate([wq1f[:, h * 192:h * 192 + 128] for h in range(32)], axis=1))
    wkv1 = g["L1_attn_wkv"].astype(np.float32, copy=False)
    wk1v = np.ascontiguousarray(wkv1[:, 0:4096])
    wv1v = np.ascontiguousarray(wkv1[:, 4096:8192])
    wo1v = np.ascontiguousarray(g["L1_attn_wo"].astype(np.float32, copy=False))
    wbT = np.ascontiguousarray(g["L1_ffn_W_base"].T.astype(np.float32, copy=False))
    bbv = np.ascontiguousarray(g["L1_ffn_b_base"].reshape(4096, 1).astype(np.float32, copy=False))
    wrT = np.ascontiguousarray(g["L1_ffn_W_router"].T.astype(np.float32, copy=False))
    astkv = np.ascontiguousarray(
        g["L1_ffn_A_stack"].astype(np.float32, copy=False).reshape(128, 4096).T)
    bstkv = np.ascontiguousarray(
        g["L1_ffn_B_stack"].astype(np.float32, copy=False).transpose(0, 2, 1).reshape(128, 4096))

    cols = [owned_cols(c) for c in range(NCORE)]
    cos, sin = rope_tables()
    nc = build_fused()
    in_maps = []
    for c in range(NCORE):
        in_maps.append({
            "xin": np.ascontiguousarray(x[:, cols[c]]),
            "cosin": np.ascontiguousarray(cos[cols[c]].T),
            "sinin": np.ascontiguousarray(sin[cols[c]].T),
            "wq0": wqT, "wk0": wkT, "wv0": wvT, "wo0": woT,
            "rw": rwT, "rbias": rbias, "eb64": eb64v,
            "wq1": wq1u, "wk1": wk1v, "wv1": wv1v, "wo1": wo1v,
            "wbase": wbT, "bb": bbv, "wr": wrT, "astk": astkv, "bstk": bstkv,
            "msk": build_masks(c),
        })
    res = run_bass_kernel_spmd(nc, in_maps, core_ids=list(range(NCORE))).results
    out = np.empty((D, SEQ), dtype=np.float32)
    for c in range(NCORE):
        out[:, cols[c]] = res[c]["xout"]
    return np.ascontiguousarray(out.T).reshape(1, SEQ, D)


def run_qkv_launch(rope, x_full_T, wq_de, wk_de, wv_de):
    nc = build_qkv(rope)
    in_maps = []
    if rope:
        cos, sin = rope_tables()
    for c in range(NCORE):
        cols = owned_cols(c)
        m = {"xin": np.ascontiguousarray(x_full_T[:, cols]),
             "wq": wq_de, "wk": wk_de, "wv": wv_de}
        if rope:
            m["cosin"] = np.ascontiguousarray(cos[cols].T)
            m["sinin"] = np.ascontiguousarray(sin[cols].T)
        in_maps.append(m)
    res = run_bass_kernel_spmd(nc, in_maps, core_ids=list(range(NCORE)))
    return res.results


def build_masks(c):
    m = np.zeros((8, 128, 128), dtype=np.float32)
    for r in range(8):
        if r < c:
            m[r] = 1.0
        elif r == c:
            m[r] = np.triu(np.ones((128, 128), dtype=np.float32))
    return np.ascontiguousarray(m.reshape(8 * 128, 128).astype(BF))


def run_qkv(rope, xT_cores, wq_de, wk_de, wv_de, extra=None):
    nc = build_qkv(rope)
    in_maps = []
    for c in range(NCORE):
        m = {"xin": xT_cores[c], "wq": wq_de, "wk": wk_de, "wv": wv_de}
        if rope:
            m.update(extra[c])
        in_maps.append(m)
    return run_bass_kernel_spmd(nc, in_maps, core_ids=list(range(NCORE))).results


def kernel_unfused(**inputs):
    g = {k: np.asarray(v) for k, v in inputs.items()}
    x = np.ascontiguousarray(g["L0_attn_hidden_states"][0].T)
    wqT = np.ascontiguousarray(g["L0_attn_Wq"].T.astype(np.float32, copy=False))
    wkT = np.ascontiguousarray(g["L0_attn_Wk"].T.astype(np.float32, copy=False))
    wvT = np.ascontiguousarray(g["L0_attn_Wv"].T.astype(np.float32, copy=False))
    woT = np.ascontiguousarray(g["L0_attn_Wo"].T.astype(np.float32, copy=False))
    rwT = np.ascontiguousarray(g["L0_ffn_router_weight"].T.astype(np.float32, copy=False))
    rbias = np.ascontiguousarray(g["L0_ffn_router_bias"].reshape(64, 1).astype(np.float32, copy=False))
    eb64 = np.ascontiguousarray(g["L0_ffn_expert_bias"].astype(np.float32, copy=False))
    wq1 = g["L1_attn_wq"].astype(np.float32, copy=False)
    wq1u = np.ascontiguousarray(
        np.concatenate([wq1[:, h * 192:h * 192 + 128] for h in range(32)], axis=1))
    wkv1 = g["L1_attn_wkv"].astype(np.float32, copy=False)
    wk1 = np.ascontiguousarray(wkv1[:, 0:4096])
    wv1 = np.ascontiguousarray(wkv1[:, 4096:8192])
    wo1 = np.ascontiguousarray(g["L1_attn_wo"].astype(np.float32, copy=False))
    wbT = np.ascontiguousarray(g["L1_ffn_W_base"].T.astype(np.float32, copy=False))
    bb = np.ascontiguousarray(g["L1_ffn_b_base"].reshape(4096, 1).astype(np.float32, copy=False))
    wrT = np.ascontiguousarray(g["L1_ffn_W_router"].T.astype(np.float32, copy=False))
    astk = np.ascontiguousarray(
        g["L1_ffn_A_stack"].astype(np.float32, copy=False).reshape(128, 4096).T)
    bstk = np.ascontiguousarray(
        g["L1_ffn_B_stack"].astype(np.float32, copy=False).transpose(0, 2, 1).reshape(128, 4096))

    cols = [owned_cols(c) for c in range(NCORE)]
    xT_cores = [np.ascontiguousarray(x[:, cols[c]]) for c in range(NCORE)]
    cos, sin = rope_tables()
    rope_extra = [{"cosin": np.ascontiguousarray(cos[cols[c]].T),
                   "sinin": np.ascontiguousarray(sin[cols[c]].T)} for c in range(NCORE)]

    resA = run_qkv(True, xT_cores, wqT, wkT, wvT, rope_extra)
    kag0 = np.concatenate([r["kout"] for r in resA], axis=0)
    vag0 = np.concatenate([r["vout"] for r in resA], axis=0)

    ncB = build_attn(False, L0_SCALE)
    in_maps = [{"qin": resA[c]["qout"], "kag": kag0, "vag": vag0} for c in range(NCORE)]
    resB = run_bass_kernel_spmd(ncB, in_maps, core_ids=list(range(NCORE))).results

    ncC = build_post(0)
    in_maps = [{"attin": resB[c]["att"], "wo": woT, "xprev": xT_cores[c],
                "rw": rwT, "rbias": rbias, "eb64": eb64} for c in range(NCORE)]
    resC = run_bass_kernel_spmd(ncC, in_maps, core_ids=list(range(NCORE))).results

    x2_cores = [np.ascontiguousarray(resC[c]["xout"]) for c in range(NCORE)]
    resD = run_qkv(False, x2_cores, wq1u, wk1, wv1)
    kag1 = np.concatenate([r["kout"] for r in resD], axis=0)
    vag1 = np.concatenate([r["vout"] for r in resD], axis=0)

    ncE = build_attn(True, L1_SCALE)
    in_maps = [{"qin": resD[c]["qout"], "kag": kag1, "vag": vag1,
                "msk": build_masks(c)} for c in range(NCORE)]
    resE = run_bass_kernel_spmd(ncE, in_maps, core_ids=list(range(NCORE))).results

    ncF = build_post(1)
    in_maps = [{"attin": resE[c]["att"], "wo": wo1, "xprev": x2_cores[c],
                "wbase": wbT, "bb": bb, "wr": wrT, "astk": astk, "bstk": bstk}
               for c in range(NCORE)]
    resF = run_bass_kernel_spmd(ncF, in_maps, core_ids=list(range(NCORE))).results

    out = np.empty((D, SEQ), dtype=np.float32)
    for c in range(NCORE):
        out[:, cols[c]] = resF[c]["xout"]
    return np.ascontiguousarray(out.T).reshape(1, SEQ, D)


# version 05
# speedup vs baseline: 1.1954x; 1.1954x over previous
import numpy as np
import ml_dtypes
from contextlib import ExitStack
import concourse.bass as bass
import concourse.mybir as mybir
from concourse.bass_utils import run_bass_kernel_spmd

F32 = mybir.dt.float32
BF16 = mybir.dt.bfloat16
AF = mybir.ActivationFunctionType
ALU = mybir.AluOpType
AX = mybir.AxisListType
BF = ml_dtypes.bfloat16

NCORE = 8
D = 4096
NC = 32
TL = 1024
NH = 32
SEQ = 8192
L0_SCALE = float(1.0 / np.sqrt(np.float32(128.0)))
L1_SCALE = float(np.float64(192.0) ** (-0.5))
EPS = 1e-5


class Ctr:
    def __init__(self, ctx, nc, name):
        self.h = ctx.enter_context(nc.semaphore(name))
        self.n = 0


class Ring:
    def __init__(self, p, name, n):
        self.sems = [p.ctr(f"{name}_r{i}") for i in range(n)]
        self.n = n
        self.rec = {}

    def dma(self, p, eng, i, out, in_, **kw):
        s = self.sems[i % self.n]
        p.dma(eng, out, in_, s, **kw)
        self.rec[i] = (s, s.n)

    def wait(self, p, eng, i):
        p.waitf(eng, self.rec[i])

    def free(self, i):
        return self.rec[i]

    def wait_all(self, p, eng):
        for s in self.sems:
            p.wait(eng, s, s.n)


class P:
    ENGS = ("sync", "gpsimd", "tensor", "vector", "scalar")

    def __init__(self, nc, ctx):
        self.nc = nc
        self.ctx = ctx
        self.actx = ctx
        self.s = {e: [] for e in P.ENGS}
        self._ctrs = {}

    def ctr(self, name):
        if name not in self._ctrs:
            self._ctrs[name] = Ctr(self.ctx, self.nc, name)
        return self._ctrs[name]

    def wait(self, eng, ctr, val):
        if val > 0:
            h = ctr.h
            self.s[eng].append(lambda e: e.wait_ge(h, val))

    def waitf(self, eng, free):
        if free is not None:
            self.wait(eng, free[0], free[1])

    def op(self, eng, fn, inc=None, k=1):
        if inc is None:
            self.s[eng].append(fn)
            return None
        h = inc.h
        self.s[eng].append(lambda e: fn(e).then_inc(h, k))
        inc.n += k
        return inc.n

    def dma(self, eng, out, in_, inc, **kw):
        h = inc.h
        self.s[eng].append(lambda e: e.dma_start(out=out, in_=in_, **kw).then_inc(h, 16))
        inc.n += 16
        return inc.n

    def flush(self, name):
        with self.nc.Block(name) as blk:
            for eng in P.ENGS:
                ops = self.s[eng]
                if not ops:
                    continue
                def runner(e, ops=ops):
                    for f in ops:
                        f(e)
                getattr(blk, eng)(runner)
        self.s = {e: [] for e in P.ENGS}


def mk_nc():
    return bass.Bass("TRN2", target_bir_lowering=False, num_devices=NCORE)


def emit_load_xb(p, tag, x_dram, xb):
    nc, ctx = p.nc, p.actx
    xf = [ctx.enter_context(nc.sbuf_tensor(f"{tag}_xf{i}", [128, 1024], F32, allow_name_mangling=True)) for i in range(2)]
    rl = Ring(p, f"{tag}_xl", 2)
    s_c = p.ctr(f"{tag}_xc")
    c_base = s_c.n
    for f in range(NC):
        p.wait("sync", s_c, c_base + f - 1)
        rl.dma(p, "sync", f, xf[f % 2][:], x_dram[f * 128:(f + 1) * 128, :])
        rl.wait(p, "scalar", f)
        p.op("scalar", lambda e, f=f: e.activation(
            xb[:, f * 1024:(f + 1) * 1024], xf[f % 2][:], AF.Copy, scale=1.0), s_c)
    return s_c


def emit_proj(p, tag, w_dram, n_d, xb, psq, evac_cb, pre_mm_waits=None, extra_mm_cb=None,
              psum_free=None, pf_base=0):
    nc, ctx = p.nc, p.actx
    n_e = w_dram.shape[1] // 128
    nrot = len(psq)
    wsf = [ctx.enter_context(nc.sbuf_tensor(f"{tag}_wsf{i}", [128, 256], F32, allow_name_mangling=True)) for i in range(4)]
    wsb = [ctx.enter_context(nc.sbuf_tensor(f"{tag}_wsb{i}", [128, 256], BF16, allow_name_mangling=True)) for i in range(4)]
    rwd = Ring(p, f"{tag}_wd", 4)
    s_wc = p.ctr(f"{tag}_wc")
    wc_base = s_wc.n
    s_mm = p.ctr(f"{tag}_mm")
    mm_done = {}
    if psum_free is None:
        psum_free = {}
    pend = []

    def defer(fn):
        pend.append(fn)

    n_eg = n_e // 2
    for eg in range(n_eg):
        for fn in pend:
            fn()
        pend = []
        for d in range(n_d):
            idx = eg * n_d + d
            p.wait("sync", s_wc, wc_base + idx - 3)
            rwd.dma(p, "sync", idx, wsf[idx % 4][:],
                    w_dram[d * 128:(d + 1) * 128, eg * 256:(eg + 1) * 256])
            rwd.wait(p, "vector", idx)
            p.wait("vector", s_mm, mm_done.get(idx - 4, 0))
            p.op("vector", lambda e, a=wsb[idx % 4], b=wsf[idx % 4]:
                 e.tensor_copy(a[:], b[:]), s_wc)
            p.wait("tensor", s_wc, wc_base + idx + 1)
            if d == 0:
                if eg == 0 and pre_mm_waits:
                    for (ctr, val) in pre_mm_waits:
                        p.wait("tensor", ctr, val)
                for e2 in range(2):
                    ec = eg * 2 + e2
                    if pf_base + ec - nrot >= 0:
                        p.waitf("tensor", psum_free[pf_base + ec - nrot])
            for e2 in range(2):
                ec = eg * 2 + e2
                psten = psq[ec % nrot]
                for t in range(2):
                    fn = (lambda e, psten=psten, w=wsb[idx % 4], e2=e2, d=d, t=t:
                          e.matmul(psten[:, t * 512:(t + 1) * 512],
                                   w[:, e2 * 128:(e2 + 1) * 128],
                                   xb[:, d * 1024 + t * 512: d * 1024 + (t + 1) * 512],
                                   start=(d == 0),
                                   stop=(d == n_d - 1 and extra_mm_cb is None)))
                    p.op("tensor", fn, s_mm if (e2 == 1 and t == 1) else None)
            mm_done[idx] = s_mm.n
        for e2 in range(2):
            ec = eg * 2 + e2
            if extra_mm_cb is not None:
                extra_mm_cb(ec, psq[ec % nrot], s_mm)
            psum_free[pf_base + ec] = evac_cb(ec, psq[ec % nrot], defer, s_mm.n, s_mm)
    for fn in pend:
        fn()
    return psum_free, s_mm


def emit_qkv(nc, p, ctx, tag, rope, xin, wq, wk, wv, qout, kout, vout,
             cos_in=None, sin_in=None):
    if rope:
        rsign = np.zeros((128, 128), dtype=BF)
        for i in range(64):
            rsign[i + 64, i] = -1.0
            rsign[i, i + 64] = 1.0
        rsign_d = nc.inline_tensor(rsign, f"rsign{tag}")
    ident_d = nc.inline_tensor(np.eye(128, dtype=BF), f"ident{tag}")
    sb = lambda n, s, d: ctx.enter_context(nc.sbuf_tensor(n, s, d, allow_name_mangling=True))
    psa = lambda n, s, d=F32: ctx.enter_context(nc.psum_tensor(n, s, d, allow_name_mangling=True))

    xb = sb("xb", [128, NC * 1024], BF16)
    psq = [psa(f"psq{i}", [128, 1024]) for i in range(3)]
    ptr_bf = psa("ptr_bf", [128, 1024], BF16)
    prot = psa("prot", [128, 512])
    ev_sb = [sb(f"ev{i}", [128, 1024], BF16) for i in range(3)]
    ob_sb = [sb(f"ob{i}", [128, 1024], BF16) for i in range(2)]
    tmp1 = [sb(f"tmp1{i}", [128, 1024], BF16) for i in range(2)]
    tmp2 = [sb(f"tmp2{i}", [128, 1024], BF16) for i in range(2)]
    ident_sb = sb("ident_sb", [128, 128], BF16)
    if rope:
        cosb = sb("cosb", [128, 1024], BF16)
        sinb = sb("sinb", [128, 1024], BF16)
        cosf = sb("cosf", [128, 1024], F32)
        rs_sb = sb("rs_sb", [128, 128], BF16)

    s_aux = p.ctr("aux")
    s_rot = p.ctr("rot")
    s_rop = p.ctr("rope")
    rst_ob = Ring(p, "st_ob", 2)
    rst_eb = Ring(p, "st_eb", 3)
    ob_sti = [0]
    eb_sti = [0]

    aux0 = s_aux.n
    p.dma("sync", ident_sb[:], ident_d[:], s_aux)
    if rope:
        p.dma("sync", cosf[:], cos_in[:], s_aux)
        p.dma("sync", rs_sb[:], rsign_d[:], s_aux)
        p.wait("vector", s_aux, aux0 + 48)
        p.op("vector", lambda e: e.tensor_copy(cosb[:], cosf[:]), s_aux)
        p.wait("sync", s_aux, aux0 + 49)
        p.dma("sync", cosf[:], sin_in[:], s_aux)
        p.wait("vector", s_aux, aux0 + 65)
        p.op("vector", lambda e: e.tensor_copy(sinb[:], cosf[:]), s_aux)
    aux_ready = s_aux.n

    xb_ready = emit_load_xb(p, "x", xin, xb)

    G = {"n": 0}
    eb_free = {}
    ob_free = {}
    rot_free = {}
    tmp_free = {}

    s_evq = p.ctr("evq")

    def make_evac(kind, out_dram):
        def evac(ec, psten, defer, mm_need, s_mm):
            gec = G["n"]; G["n"] += 1
            eb = ev_sb[gec % 3]
            ob = ob_sb[gec % 2]
            t1, t2 = tmp1[gec % 2], tmp2[gec % 2]
            p.wait("scalar", s_mm, mm_need)
            p.waitf("scalar", eb_free.get(gec - 3))
            p.op("scalar", lambda e: e.activation(
                eb[:], psten[:], AF.Copy, scale=1.0), s_evq)
            ev_now = s_evq.n

            if kind == "plain":
                i = eb_sti[0]; eb_sti[0] += 1
                assert i % 3 == gec % 3
                p.wait("sync", s_evq, ev_now)
                rst_eb.dma(p, "sync", i, out_dram[ec * 128:(ec + 1) * 128, :], eb[:])
                eb_free[gec] = rst_eb.free(i)
                return (s_evq, ev_now)

            def emit_def():
                if kind == "rope":
                    p.wait("tensor", s_evq, ev_now)
                    p.waitf("tensor", rot_free.get(gec - 1))
                    p.op("tensor", lambda e: e.matmul(
                        prot[:], rs_sb[:], eb[:, 0:512], start=True, stop=True), s_rot)
                    r1 = s_rot.n
                    p.wait("vector", s_rot, r1)
                    p.waitf("vector", tmp_free.get(gec - 2))
                    p.op("vector", lambda e: e.tensor_tensor(
                        t2[:, 0:512], prot[:], sinb[:, 0:512], ALU.mult), s_rop)
                    v1 = s_rop.n
                    p.wait("tensor", s_rop, v1)
                    p.op("tensor", lambda e: e.matmul(
                        prot[:], rs_sb[:], eb[:, 512:1024], start=True, stop=True), s_rot)
                    r2 = s_rot.n
                    rot_free[gec] = (s_rop, v1 + 1)
                    p.wait("vector", s_rot, r2)
                    p.op("vector", lambda e: e.tensor_tensor(
                        t2[:, 512:1024], prot[:], sinb[:, 512:1024], ALU.mult), s_rop)
                    p.op("vector", lambda e: e.tensor_mul(t1[:], eb[:], cosb[:]), s_rop)
                    v2 = s_rop.n
                    eb_free[gec] = (s_rop, v2)
                    p.waitf("vector", ob_free.get(gec - 2))
                    p.wait("vector", s_rop, v2)
                    p.op("vector", lambda e: e.tensor_add(ob[:], t1[:], t2[:]), s_rop)
                    tmp_free[gec] = (s_rop, s_rop.n)
                    i = ob_sti[0]; ob_sti[0] += 1
                    assert i % 2 == gec % 2
                    p.wait("sync", s_rop, s_rop.n)
                    rst_ob.dma(p, "sync", i, out_dram[ec * 128:(ec + 1) * 128, :], ob[:])
                    ob_free[gec] = rst_ob.free(i)
                else:
                    p.wait("tensor", s_evq, ev_now)
                    p.waitf("tensor", rot_free.get(gec - 1))
                    def pe_tr(e):
                        inst = None
                        for i in range(8):
                            inst = e.transpose(ptr_bf[:, i * 128:(i + 1) * 128],
                                               eb[:, i * 128:(i + 1) * 128], ident_sb[:])
                        return inst
                    p.op("tensor", pe_tr, s_rot)
                    eb_free[gec] = (s_rot, s_rot.n)
                    p.waitf("scalar", ob_free.get(gec - 2))
                    p.wait("scalar", s_rot, s_rot.n)
                    p.op("scalar", lambda e: e.activation(
                        ob[:], ptr_bf[:], AF.Copy, scale=1.0), s_rop)
                    rot_free[gec] = (s_rop, s_rop.n)
                    i = ob_sti[0]; ob_sti[0] += 1
                    assert i % 2 == gec % 2
                    p.wait("sync", s_rop, s_rop.n)
                    dst = out_dram.ap().rearrange("(b p) e -> p b e", p=128)[:, :, ec * 128:(ec + 1) * 128]
                    srcap = ob[:].rearrange("p (b d) -> p b d", d=128)
                    rst_ob.dma(p, "sync", i, dst, srcap)
                    ob_free[gec] = rst_ob.free(i)
            defer(emit_def)
            return (s_evq, ev_now)
        return evac

    kind_q = "rope" if rope else "plain"
    pf = {}
    emit_proj(p, "q", wq, NC, xb, psq, make_evac(kind_q, qout),
              pre_mm_waits=[(xb_ready, xb_ready.n), (s_aux, aux_ready)],
              psum_free=pf, pf_base=0)
    emit_proj(p, "k", wk, NC, xb, psq, make_evac(kind_q, kout),
              psum_free=pf, pf_base=32)
    emit_proj(p, "v", wv, NC, xb, psq, make_evac("v", vout),
              psum_free=pf, pf_base=64)

    rst_ob.wait_all(p, "gpsimd")
    rst_eb.wait_all(p, "gpsimd")


def build_qkv(rope):
    nc = mk_nc()
    xin = nc.dram_tensor("xin", [D, TL], F32, kind="ExternalInput")
    wq = nc.dram_tensor("wq", [D, D], F32, kind="ExternalInput")
    wk = nc.dram_tensor("wk", [D, D], F32, kind="ExternalInput")
    wv = nc.dram_tensor("wv", [D, D], F32, kind="ExternalInput")
    qout = nc.dram_tensor("qout", [D, TL], BF16, kind="ExternalOutput")
    kout = nc.dram_tensor("kout", [D, TL], BF16, kind="ExternalOutput")
    vout = nc.dram_tensor("vout", [TL, D], BF16, kind="ExternalOutput")
    cos_in = sin_in = None
    if rope:
        cos_in = nc.dram_tensor("cosin", [128, TL], F32, kind="ExternalInput")
        sin_in = nc.dram_tensor("sinin", [128, TL], F32, kind="ExternalInput")
    with ExitStack() as ctx:
        ctx.enter_context(nc.allow_low_precision(reason="bf16 compute per stated tolerance"))
        p = P(nc, ctx)
        emit_qkv(nc, p, ctx, "", rope, xin, wq, wk, wv, qout, kout, vout, cos_in, sin_in)
        p.flush("qkv")
    return nc



def emit_attn(nc, p, ctx, causal, scale, qin, kag, vag, att, msk_in=None):
    sb = lambda n, s, d: ctx.enter_context(nc.sbuf_tensor(n, s, d, allow_name_mangling=True))
    psa = lambda n, s, d=F32: ctx.enter_context(nc.psum_tensor(n, s, d, allow_name_mangling=True))

    ksb = [sb(f"ksb{i}", [128, 64 * 128], BF16) for i in range(2)]
    vsb = [sb(f"vsb{i}", [128, 64 * 128], BF16) for i in range(2)]
    qsb = [sb(f"qsb{i}", [128, TL], BF16) for i in range(2)]
    esb = [sb(f"esb{i}", [128, 1024], BF16) for i in range(3)]
    dsb = [sb(f"dsb{i}", [128, 1024], BF16) for i in range(2)]
    rbf = sb("rbf", [1, 1024], BF16)
    bcs = sb("bcs", [128, 512], F32)
    att_sb = [sb(f"attsb{i}", [128, TL], BF16) for i in range(2)]
    onec = sb("onec", [128, 1], BF16)
    oner = sb("oner", [1, 128], BF16)
    if causal:
        msk_sb = sb("msk_sb", [128, 8 * 128], BF16)

    pst = [psa(f"pst{i}", [128, 1024]) for i in range(2)]
    ps_pv = [psa(f"ps_pv{i}", [128, 512]) for i in range(2)]
    ps_den = psa("ps_den", [1, 512])
    ps_bc = psa("ps_bc", [128, 512])

    rkv = [p.ctr("rkv0"), p.ctr("rkv1")]
    rst = Ring(p, "rst", 2)
    a_s = p.ctr("a_s")
    a_x = p.ctr("a_x")
    a_e = p.ctr("a_e")
    a_p = p.ctr("a_p")
    a_v = p.ctr("a_v")
    a_x2 = p.ctr("a_x2")
    s_aux = p.ctr("aux")

    p.op("vector", lambda e: e.memset(onec[:], 1.0), s_aux)
    p.op("vector", lambda e: e.memset(oner[:], 1.0), s_aux)
    if causal:
        p.dma("sync", msk_sb[:].rearrange("p (r q) -> p r q", q=128),
              msk_in.ap().rearrange("(r p) q -> p r q", p=128), s_aux)
        p.wait("vector", s_aux, s_aux.n)

    kv_ready = {}

    def loads(h):
        hb = h % 2
        s = rkv[hb]
        for k8 in range(8):
            p.dma("sync", ksb[hb][:, k8 * 1024:(k8 + 1) * 1024].rearrange("p (c t) -> p c t", t=128),
                  kag.ap().rearrange("(c x) t -> x c t", c=8)[h * 128:(h + 1) * 128, :, k8 * 128:(k8 + 1) * 128], s)
            p.dma("sync", vsb[hb][:, k8 * 1024:(k8 + 1) * 1024].rearrange("p (c t) -> p c t", t=128),
                  vag.ap().rearrange("(c y) e -> y c e", c=8)[k8 * 128:(k8 + 1) * 128, :, h * 128:(h + 1) * 128], s)
        p.dma("sync", qsb[hb][:], qin[h * 128:(h + 1) * 128, :], s)
        kv_ready[h] = s.n

    gj0 = [None]
    scount = {}
    xcount = {}
    ecount = {}
    pcount = {}
    div_done = {}
    dme_done = {}

    loads(0)
    loads(1)

    nB = 64
    npair = 32 if causal else 64

    for h in range(NH):
        hb = h % 2
        p.wait("tensor", rkv[hb], kv_ready[h])

        p.waitf("vector", div_done.get(h - 2))
        p.op("vector", lambda e: e.memset(dsb[0][:], 0.0), a_v)
        p.op("vector", lambda e: e.memset(dsb[1][:], 0.0), a_v)
        dmn = a_v.n

        if gj0[0] is None:
            gj0[0] = 0
        gj_head = gj0[0]

        def colA(j):
            return (j // 8) * 128 if causal else 0

        def colB(j):
            if causal and j >= 32:
                return (j // 8 - 4) * 128
            return 0

        for j in range(nB):
            gj = gj0[0]; gj0[0] += 1
            paired = j < npair
            c0a = colA(j)
            c0b = colB(j)
            lo = c0a if paired else 512 + c0b
            p.waitf("tensor", ecount.get(gj - 2))
            if j == 0:
                p.waitf("tensor", div_done.get(h - 2))
            if paired:
                p.op("tensor", lambda e, j=j, gj=gj, c0a=c0a, hb=hb: e.matmul(
                    pst[gj % 2][:, c0a:512],
                    ksb[hb][:, j * 128:(j + 1) * 128],
                    qsb[hb][:, c0a:512], start=True, stop=True), None)
            p.op("tensor", lambda e, j=j, gj=gj, c0b=c0b, hb=hb: e.matmul(
                pst[gj % 2][:, 512 + c0b:1024],
                ksb[hb][:, j * 128:(j + 1) * 128],
                qsb[hb][:, 512 + c0b:1024], start=True, stop=True), a_s)
            scount[gj] = a_s.n
            if j >= 2:
                jj = j - 2
                gjj = gj - 2
                p.waitf("tensor", ecount.get(gjj))
                if jj < npair:
                    p.op("tensor", lambda e, jj=jj, gjj=gjj, hb=hb, c0=colA(jj): e.matmul(
                        ps_pv[0][:, c0:512],
                        vsb[hb][:, jj * 128:(jj + 1) * 128],
                        esb[gjj % 3][:, c0:512],
                        start=(jj == 0), stop=(jj == npair - 1)), None)
                p.op("tensor", lambda e, jj=jj, gjj=gjj, hb=hb, c0=colB(jj): e.matmul(
                    ps_pv[1][:, c0:512],
                    vsb[hb][:, jj * 128:(jj + 1) * 128],
                    esb[gjj % 3][:, 512 + c0:1024],
                    start=(jj == 0), stop=(jj == nB - 1)), a_p)
                pcount[gjj] = (a_p, a_p.n)
            p.wait("scalar", a_s, scount[gj])
            p.waitf("scalar", pcount.get(gj - 3))
            p.waitf("scalar", ecount.get(gj - 3))
            p.op("scalar", lambda e, gj=gj, lo=lo: e.activation(
                esb[gj % 3][:, lo:1024], pst[gj % 2][:, lo:1024], AF.Exp,
                scale=scale), a_x)
            xcount[gj] = a_x.n
            p.wait("vector", a_x, xcount[gj])
            if j <= 1:
                p.wait("vector", a_v, dmn)
            else:
                p.waitf("vector", ecount.get(gj - 2))
            mr = j % 8
            if causal and paired:
                p.op("vector", lambda e, gj=gj, c0a=c0a, mr=mr: e.tensor_tensor(
                    esb[gj % 3][:, c0a:c0a + 128], esb[gj % 3][:, c0a:c0a + 128],
                    msk_sb[:, mr * 128:(mr + 1) * 128], ALU.mult), a_v)
                p.wait("vector", a_v, a_v.n)
            if causal and j >= 32:
                p.op("vector", lambda e, gj=gj, c0b=c0b, mr=mr: e.tensor_tensor(
                    esb[gj % 3][:, 512 + c0b:512 + c0b + 128], esb[gj % 3][:, 512 + c0b:512 + c0b + 128],
                    msk_sb[:, mr * 128:(mr + 1) * 128], ALU.mult), a_v)
                p.wait("vector", a_v, a_v.n)
            p.op("vector", lambda e, gj=gj, lo=lo: e.tensor_tensor(
                dsb[gj % 2][:, lo:1024], dsb[gj % 2][:, lo:1024],
                esb[gj % 3][:, lo:1024], ALU.add), a_e)
            ecount[gj] = (a_e, a_e.n)

        for jj in (nB - 2, nB - 1):
            gjj = gj_head + jj
            p.waitf("tensor", ecount.get(gjj))
            if jj < npair:
                p.op("tensor", lambda e, jj=jj, gjj=gjj, hb=hb, c0=colA(jj): e.matmul(
                    ps_pv[0][:, c0:512],
                    vsb[hb][:, jj * 128:(jj + 1) * 128],
                    esb[gjj % 3][:, c0:512],
                    start=(jj == 0), stop=(jj == npair - 1)), None)
            p.op("tensor", lambda e, jj=jj, gjj=gjj, hb=hb, c0=colB(jj): e.matmul(
                ps_pv[1][:, c0:512],
                vsb[hb][:, jj * 128:(jj + 1) * 128],
                esb[gjj % 3][:, 512 + c0:1024],
                start=(jj == 0), stop=(jj == nB - 1)), a_p)
            pcount[gjj] = (a_p, a_p.n)
        hp_end = a_p.n

        p.waitf("vector", ecount[gj0[0] - 1])
        p.op("vector", lambda e: e.tensor_add(dsb[0][:], dsb[0][:], dsb[1][:]), a_v)
        castn = a_v.n
        div_done[h] = None
        for half, (pvp, off) in enumerate(((0, 0), (1, 512))):
            p.wait("tensor", a_v, castn if half == 0 else a_v.n)
            if half == 0 and h >= 1:
                p.waitf("tensor", div_done.get(h - 1))
            p.op("tensor", lambda e, off=off: e.matmul(
                ps_den[:], onec[:], dsb[0][:, off:off + 512], start=True, stop=True), a_s)
            on_n = a_s.n
            p.wait("vector", a_s, on_n)
            p.op("vector", lambda e, off=off: e.reciprocal(rbf[:, off:off + 512], ps_den[:]), a_v)
            rc_n = a_v.n
            p.wait("tensor", a_v, rc_n)
            p.op("tensor", lambda e, off=off: e.matmul(
                ps_bc[:], oner[:], rbf[:, off:off + 512], start=True, stop=True), a_s)
            bc_n = a_s.n
            p.wait("scalar", a_s, bc_n)
            p.op("scalar", lambda e: e.activation(bcs[:], ps_bc[:], AF.Copy, scale=1.0), a_x2)
            bce_n = a_x2.n
            p.wait("vector", a_x2, bce_n)
            p.wait("vector", a_p, hp_end)
            p.waitf("vector", rst.free(h - 2) if h >= 2 else None)
            p.op("vector", lambda e, pvp=pvp, off=off, hb=hb: e.tensor_tensor(
                att_sb[hb][:, off:off + 512], ps_pv[pvp][:], bcs[:], ALU.mult), a_v)
        div_done[h] = (a_v, a_v.n)

        p.wait("sync", a_v, div_done[h][1])
        rst.dma(p, "sync", h, att[h * 128:(h + 1) * 128, :], att_sb[hb][:])
        if h + 2 < NH:
            p.wait("sync", a_p, hp_end)
            loads(h + 2)

    rst.wait_all(p, "gpsimd")


def build_attn(causal, scale):
    nc = mk_nc()
    qin = nc.dram_tensor("qin", [D, TL], BF16, kind="ExternalInput")
    kag = nc.dram_tensor("kag", [NCORE * D, TL], BF16, kind="ExternalInput")
    vag = nc.dram_tensor("vag", [NCORE * TL, D], BF16, kind="ExternalInput")
    att = nc.dram_tensor("att", [D, TL], BF16, kind="ExternalOutput")
    msk_in = None
    if causal:
        msk_in = nc.dram_tensor("msk", [8 * 128, 128], BF16, kind="ExternalInput")
    with ExitStack() as ctx:
        ctx.enter_context(nc.allow_low_precision(reason="bf16 compute per stated tolerance"))
        p = P(nc, ctx)
        emit_attn(nc, p, ctx, causal, scale, qin, kag, vag, att, msk_in)
        p.flush("attn")
    return nc



def emit_post(nc, p, ctx, tag, layer, attin, wo, xprev, xout, xr_dram, x1_dram,
              rw=None, rbias=None, eb64=None,
              wbase=None, bb_in=None, wr=None, astk=None, bstk=None):
    if layer == 0:
        NE = 64
        KK = 8
    else:
        ex8 = np.zeros((8, 128), dtype=BF)
        for e8 in range(8):
            ex8[e8, e8 * 16:(e8 + 1) * 16] = 1.0
        ex8_d = nc.inline_tensor(ex8, f"ex8{tag}")
        NE = 8
        KK = 2
    identb_d = nc.inline_tensor(np.eye(128, dtype=BF), f"identb{tag}")
    identf_d = nc.inline_tensor(np.eye(128, dtype=np.float32), f"identf{tag}")
    sb = lambda n, s, d: ctx.enter_context(nc.sbuf_tensor(n, s, d, allow_name_mangling=True))

    ab = sb("ab", [128, NC * 1024], BF16)
    xr_sb = [sb(f"xr{i}", [128, 1024], F32) for i in range(2)]
    xp_sb = [sb(f"xp{i}", [128, 1024], F32) for i in range(2)]
    xrb_sb = [sb(f"xrb{i}", [128, 1024], BF16) for i in range(2)]
    sq_sb = [sb(f"sq{i}", [128, 1024], BF16) for i in range(2)]
    x1c_sb = [sb(f"x1c{i}", [128, 1024], F32) for i in range(2)]
    onec = sb("onec", [128, 1], BF16)
    oner = sb("oner", [1, 128], BF16)
    epsb = sb("epsb", [1, 1], F32)
    identb = sb("identb_sb", [128, 128], BF16)
    identf = sb("identf_sb", [128, 128], F32)
    m_sb = sb("m_sb", [1, 1024], F32)
    v_sb = sb("v_sb", [1, 1024], F32)
    t1_sb = sb("t1_sb", [1, 1024], F32)
    rstd_sb = sb("rstd_sb", [1, 1024], F32)
    rstdb = sb("rstdb", [1, 1024], BF16)
    mrb = sb("mrb", [1, 1024], BF16)

    s_aux = p.ctr("aux")
    a_r = p.ctr("a_r")
    a_c = p.ctr("a_c")
    a_q = p.ctr("a_q")
    a_sm = p.ctr("a_sm")
    a_ln = p.ctr("a_ln")
    a_f = p.ctr("a_f")
    a_t = p.ctr("a_t")

    p.dma("sync", identb[:], identb_d[:], s_aux)
    p.dma("sync", identf[:], identf_d[:], s_aux)
    p.op("vector", lambda e: e.memset(onec[:], 1.0), s_aux)
    p.op("vector", lambda e: e.memset(oner[:], 1.0), s_aux)
    p.op("vector", lambda e: e.memset(epsb[:], EPS), s_aux)
    aux_n = s_aux.n

    ab_ready = emit_load_xb_bf(p, "ab", attin, ab)

    def ln_finalize_apply(stats, stats_total, src_dram, src_ring, dst_dram, cast_to_ab, tagn):
        ld = Ring(p, f"{tagn}_ld", 2)
        st = Ring(p, f"{tagn}_st", 2)
        src_ring.wait_all(p, "sync")
        for eng in ("sync", "vector", "scalar"):
            p.wait(eng, a_sm, stats_total)
            p.wait(eng, a_r, a_r.n)
            p.wait(eng, a_c, a_c.n)
            p.wait(eng, a_q, a_q.n)
        for hh in range(2):
            p.op("vector", lambda e, hh=hh: e.tensor_scalar(
                m_sb[:, hh * 512:(hh + 1) * 512], stats[hh][:], 1.0 / D, None, ALU.mult), a_ln)
            p.op("vector", lambda e, hh=hh: e.tensor_scalar(
                v_sb[:, hh * 512:(hh + 1) * 512], stats[2 + hh][:], 1.0 / D, None, ALU.mult), a_ln)
        p.wait("vector", a_ln, a_ln.n)
        p.op("vector", lambda e: e.tensor_mul(t1_sb[:], m_sb[:], m_sb[:]), a_ln)
        p.wait("vector", a_ln, a_ln.n)
        p.op("vector", lambda e: e.tensor_sub(v_sb[:], v_sb[:], t1_sb[:]), a_ln)
        vn = a_ln.n
        p.wait("scalar", a_ln, vn)
        p.op("scalar", lambda e: e.activation(t1_sb[:], v_sb[:], AF.Ln, bias=epsb[:], scale=1.0), a_ln)
        p.wait("scalar", a_ln, a_ln.n)
        p.op("scalar", lambda e: e.activation(rstd_sb[:], t1_sb[:], AF.Exp, scale=-0.5), a_ln)
        rn = a_ln.n
        p.wait("vector", a_ln, rn)
        p.op("vector", lambda e: e.tensor_mul(t1_sb[:], m_sb[:], rstd_sb[:]), a_ln)
        p.op("vector", lambda e: e.tensor_copy(rstdb[:], rstd_sb[:]), a_ln)
        p.wait("vector", a_ln, a_ln.n)
        p.op("vector", lambda e: e.tensor_copy(mrb[:], t1_sb[:]), a_ln)
        castn = a_ln.n
        with nc.psum_tensor(f"{tagn}_rsb", [128, 1024], F32, allow_name_mangling=True) as rs_bc, \
             nc.psum_tensor(f"{tagn}_mrb", [128, 1024], F32, allow_name_mangling=True) as mr_bc:
            p.wait("tensor", a_ln, castn)
            for hh in range(2):
                p.op("tensor", lambda e, hh=hh: e.matmul(
                    rs_bc[:, hh * 512:(hh + 1) * 512], oner[:],
                    rstdb[:, hh * 512:(hh + 1) * 512], start=True, stop=True), a_ln)
                p.op("tensor", lambda e, hh=hh: e.matmul(
                    mr_bc[:, hh * 512:(hh + 1) * 512], oner[:],
                    mrb[:, hh * 512:(hh + 1) * 512], start=True, stop=True), a_ln)
            bcn = a_ln.n
            apply_done = {}
            cast_done = {}
            for f in range(NC):
                p.waitf("sync", apply_done.get(f - 2))
                ld.dma(p, "sync", f, xr_sb[f % 2][:], src_dram[f * 128:(f + 1) * 128, :])
                ld.wait(p, "vector", f)
                p.wait("vector", a_ln, bcn)
                if f >= 2:
                    p.waitf("vector", apply_done.get(f - 2))
                    if cast_to_ab:
                        p.wait("vector", a_c, cast_done[f - 2])
                p.op("vector", lambda e, f=f: e.tensor_mul(x1c_sb[f % 2][:], xr_sb[f % 2][:], rs_bc[:]), a_ln)
                p.wait("vector", a_ln, a_ln.n)
                p.op("vector", lambda e, f=f: e.tensor_sub(x1c_sb[f % 2][:], x1c_sb[f % 2][:], mr_bc[:]), a_ln)
                here = a_ln.n
                if cast_to_ab:
                    p.wait("scalar", a_ln, here)
                    p.op("scalar", lambda e, f=f: e.activation(
                        ab[:, f * 1024:(f + 1) * 1024], x1c_sb[f % 2][:], AF.Copy, scale=1.0), a_c)
                    cast_done[f] = a_c.n
                p.wait("sync", a_ln, here)
                st.dma(p, "sync", f, dst_dram[f * 128:(f + 1) * 128, :], x1c_sb[f % 2][:])
                apply_done[f] = st.free(f)
            st.wait_all(p, "tensor")
        return st

    rxp1 = Ring(p, "rxp1", 2)
    rxr1 = Ring(p, "rxr1", 2)
    stats_pend = []

    def mk_evac(rxp, rxr, stats, a_sm_base, xr_src, resid_dram):
        def evac(ec, psten, defer, mm_need, s_mm):
            rxp.dma(p, "sync", ec, xp_sb[ec % 2][:], resid_dram[ec * 128:(ec + 1) * 128, :])
            p.wait("vector", s_mm, mm_need)
            rxp.wait(p, "vector", ec)
            if ec >= 2:
                p.waitf("vector", rxr.free(ec - 2))
            p.op("vector", lambda e, ec=ec, psten=psten: xr_src(e, ec, psten), a_r)
            rn = a_r.n
            p.wait("sync", a_r, rn)
            rxr.dma(p, "sync", ec, xr_dram[ec * 128:(ec + 1) * 128, :], xr_sb[ec % 2][:])
            p.wait("scalar", a_r, rn)
            if ec >= 2:
                p.wait("scalar", a_sm, a_sm_base + ec - 1)
            p.op("scalar", lambda e, ec=ec: e.activation(
                xrb_sb[ec % 2][:], xr_sb[ec % 2][:], AF.Copy, scale=1.0), a_c)
            cn = a_c.n
            p.wait("vector", a_c, cn)
            if ec >= 2:
                p.wait("vector", a_sm, a_sm_base + ec - 1)
            p.op("vector", lambda e, ec=ec: e.tensor_mul(
                sq_sb[ec % 2][:], xrb_sb[ec % 2][:], xrb_sb[ec % 2][:]), a_q)
            qn = a_q.n
            def emit_def(ec=ec, cn=cn, qn=qn):
                p.wait("tensor", a_c, cn)
                p.wait("tensor", a_q, qn)
                if ec == 0:
                    p.wait("tensor", s_aux, aux_n)
                for hh in range(2):
                    p.op("tensor", lambda e, ec=ec, hh=hh: e.matmul(
                        stats[hh][:], onec[:], xrb_sb[ec % 2][:, hh * 512:(hh + 1) * 512],
                        start=(ec == 0), stop=(ec == NC - 1), skip_group_check=True), None)
                    p.op("tensor", lambda e, ec=ec, hh=hh: e.matmul(
                        stats[2 + hh][:], onec[:], sq_sb[ec % 2][:, hh * 512:(hh + 1) * 512],
                        start=(ec == 0), stop=(ec == NC - 1), skip_group_check=True),
                        a_sm if hh == 1 else None)
            defer(emit_def)
            return (a_r, rn)
        return evac

    with ExitStack() as s1s:
        stats = [s1s.enter_context(nc.psum_tensor(f"st{i}", [1, 512], F32, allow_name_mangling=True)) for i in range(4)]
        a_r_base1 = a_r.n
        with ExitStack() as s1p:
            psq2 = [s1p.enter_context(nc.psum_tensor(f"psq{i}", [128, 1024], F32, allow_name_mangling=True)) for i in range(2)]
            ev1 = mk_evac(rxp1, rxr1, stats, 0,
                          lambda e, ec, psten: e.tensor_add(xr_sb[ec % 2][:], psten[:], xp_sb[ec % 2][:]),
                          xprev)
            def ev1g(ec, psten, defer, mm_need, s_mm):
                p.wait("sync", a_r, a_r_base1 + max(0, ec - 1))
                return ev1(ec, psten, defer, mm_need, s_mm)
            emit_proj(p, "wo", wo, NC, ab, psq2, ev1g,
                      pre_mm_waits=[(ab_ready, ab_ready.n)])
        st1 = ln_finalize_apply(stats, a_sm.n, xr_dram, rxr1, x1_dram, True, "ln1")
    ab2_ready = a_c.n

    lgf = sb("lgf", [64, 1024], F32)
    tokl2 = sb("tokl2", [128, 8 * NE], F32)
    ex_sb = sb("ex_t", [128, NE], F32)
    cur_sb = sb("cur_t", [128, NE], F32)
    w_sb = sb("w_t", [128, NE], F32)
    wb_sb = sb("wb_t", [128, NE], BF16)
    mx_sb = sb("mx_t", [128, 1], F32)
    nmx_sb = sb("nmx_t", [128, 1], F32)
    sm_sb = sb("sm_t", [128, 1], F32)
    rc_sb = sb("rc_t", [128, 1], F32)
    t8_sb = sb("t8_t", [128, 1], F32)
    wdt = sb("wdt", [NE, 1024], BF16)
    wsb_sb = sb("wsb_sb", [128, 1024], F32)
    ws1b = sb("ws1b", [1, 1024], BF16)

    if layer == 0:
        rwf = sb("rwf", [128, NC * 64], F32)
        rwb = sb("rwb", [128, NC * 64], BF16)
        ebf = sb("ebf", [64, D], F32)
        ebb = sb("ebb", [64, D], BF16)
        rb_sb = sb("rb_sb", [64, 1], F32)
        p.dma("sync", rwf[:].rearrange("p (d e) -> p d e", e=64),
              rw.ap().rearrange("(d p) e -> p d e", p=128), s_aux)
        p.dma("sync", ebf[:], eb64[:], s_aux)
        p.dma("sync", rb_sb[:], rbias[:], s_aux)
        p.wait("vector", s_aux, s_aux.n)
        p.op("vector", lambda e: e.tensor_copy(rwb[:], rwf[:]), a_f)
        p.op("vector", lambda e: e.tensor_copy(ebb[:], ebf[:]), a_f)
        wready = a_f.n
        wchunks = rwb
    else:
        bbf = sb("bbf", [128, 32], F32)
        wrf = sb("wrf", [128, NC * 8], F32)
        wrb = sb("wrb", [128, NC * 8], BF16)
        asf = sb("asf", [128, NC * 128], F32)
        asb = sb("asb", [128, NC * 128], BF16)
        bsf = sb("bsf", [128, D], F32)
        bsb = sb("bsb", [128, D], BF16)
        ex8_sb = sb("ex8_sb", [8, 128], BF16)
        hs_sb = sb("hs_sb", [128, 1024], BF16)
        cb_sb = sb("cb_sb", [128, 1024], F32)
        p.dma("sync", bbf[:].rearrange("p (f o) -> p f o", o=1),
              bb_in.ap().rearrange("(f p) o -> p f o", p=128), s_aux,
              allow_slow_non_contiguous=True)
        p.dma("sync", wrf[:].rearrange("p (d e) -> p d e", e=8),
              wr.ap().rearrange("(d p) e -> p d e", p=128), s_aux)
        p.dma("sync", asf[:].rearrange("p (d e) -> p d e", e=128),
              astk.ap().rearrange("(d p) e -> p d e", p=128), s_aux)
        p.dma("sync", bsf[:], bstk[:], s_aux)
        p.dma("sync", ex8_sb[:], ex8_d[:], s_aux)
        p.wait("vector", s_aux, s_aux.n)
        p.op("vector", lambda e: e.tensor_copy(wrb[:], wrf[:]), a_f)
        p.op("vector", lambda e: e.tensor_copy(asb[:], asf[:]), a_f)
        p.op("vector", lambda e: e.tensor_copy(bsb[:], bsf[:]), a_f)
        wready = a_f.n
        wchunks = wrb

    def emit_router_and_topk(n_e, kk):
        with ExitStack() as s2:
            pr = s2.enter_context(nc.psum_tensor("pr", [n_e, 1024], F32, allow_name_mangling=True))
            ptt = s2.enter_context(nc.psum_tensor("ptt", [128, n_e], F32, allow_name_mangling=True))
            ptw = s2.enter_context(nc.psum_tensor("ptw", [n_e, 128], BF16, allow_name_mangling=True))
            p.wait("tensor", a_f, wready)
            p.wait("tensor", a_c, ab2_ready)
            for dd in range(NC):
                for t in range(2):
                    p.op("tensor", lambda e, dd=dd, t=t: e.matmul(
                        pr[:, t * 512:(t + 1) * 512],
                        wchunks[:, dd * n_e:(dd + 1) * n_e],
                        ab[:, dd * 1024 + t * 512: dd * 1024 + (t + 1) * 512],
                        start=(dd == 0), stop=(dd == NC - 1)),
                        a_f if (dd == NC - 1 and t == 1) else None)
            prn = a_f.n
            p.wait("vector", a_f, prn)
            if layer == 0:
                p.op("vector", lambda e: e.tensor_scalar(
                    lgf[0:n_e, :], pr[:], rb_sb[:], None, ALU.add), a_f)
            else:
                p.op("vector", lambda e: e.tensor_copy(lgf[0:n_e, :], pr[:]), a_f)
            for tg in range(8):
                p.wait("tensor", a_f, a_f.n)
                p.op("tensor", lambda e, tg=tg: e.transpose(
                    ptt[:, 0:n_e], lgf[0:n_e, tg * 128:(tg + 1) * 128],
                    identf[0:n_e, 0:n_e]), a_f)
                p.wait("scalar", a_f, a_f.n)
                p.op("scalar", lambda e, tg=tg: e.activation(
                    tokl2[:, tg * n_e:(tg + 1) * n_e], ptt[:, 0:n_e], AF.Copy, scale=1.0), a_f)
            for tg in range(8):
                base = tokl2[:, tg * n_e:(tg + 1) * n_e]
                p.wait("vector", a_f, a_f.n)
                p.wait("vector", a_t, a_t.n)
                p.op("vector", lambda e, base=base: e.reduce_max(mx_sb[:], base, axis=AX.X), a_t)
                p.wait("vector", a_t, a_t.n)
                p.op("vector", lambda e: e.tensor_scalar(nmx_sb[:], mx_sb[:], -1.0, None, ALU.mult), a_t)
                nmn = a_t.n
                p.wait("scalar", a_t, nmn)
                p.op("scalar", lambda e, base=base: e.activation(
                    ex_sb[:], base, AF.Exp, bias=nmx_sb[:], scale=1.0), a_t)
                p.wait("vector", a_t, a_t.n)
                p.op("vector", lambda e: e.reduce_sum(sm_sb[:], ex_sb[:], axis=AX.X), a_t)
                p.wait("vector", a_t, a_t.n)
                p.op("vector", lambda e: e.reciprocal(rc_sb[:], sm_sb[:]), a_t)
                p.wait("vector", a_t, a_t.n)
                p.op("vector", lambda e: e.tensor_scalar(
                    ex_sb[:], ex_sb[:], rc_sb[:], None, ALU.mult), a_t)
                p.wait("vector", a_t, a_t.n)
                p.op("vector", lambda e: e.tensor_copy(cur_sb[:], ex_sb[:]), a_t)
                for ki in range(kk):
                    p.wait("vector", a_t, a_t.n)
                    p.op("vector", lambda e: e.reduce_max(t8_sb[:], cur_sb[:], axis=AX.X), a_t)
                    if ki < kk - 1:
                        p.wait("vector", a_t, a_t.n)
                        p.op("vector", lambda e: e.tensor_scalar(
                            w_sb[:], cur_sb[:], t8_sb[:], None, ALU.is_ge), a_t)
                        p.wait("vector", a_t, a_t.n)
                        p.op("vector", lambda e: e.scalar_tensor_tensor(
                            cur_sb[:], w_sb[:], -1e30, cur_sb[:], ALU.mult, ALU.add), a_t)
                p.wait("vector", a_t, a_t.n)
                p.op("vector", lambda e: e.tensor_scalar(
                    w_sb[:], ex_sb[:], t8_sb[:], None, ALU.is_ge), a_t)
                p.wait("vector", a_t, a_t.n)
                p.op("vector", lambda e: e.tensor_mul(w_sb[:], w_sb[:], ex_sb[:]), a_t)
                if layer == 1:
                    p.wait("vector", a_t, a_t.n)
                    p.op("vector", lambda e: e.reduce_sum(sm_sb[:], w_sb[:], axis=AX.X), a_t)
                    p.wait("vector", a_t, a_t.n)
                    p.op("vector", lambda e: e.reciprocal(rc_sb[:], sm_sb[:]), a_t)
                    p.wait("vector", a_t, a_t.n)
                    p.op("vector", lambda e: e.tensor_scalar(
                        w_sb[:], w_sb[:], rc_sb[:], None, ALU.mult), a_t)
                p.wait("vector", a_t, a_t.n)
                p.op("vector", lambda e: e.tensor_copy(wb_sb[:], w_sb[:]), a_t)
                wbn = a_t.n
                p.wait("tensor", a_t, wbn)
                p.wait("tensor", a_f, a_f.n)
                p.op("tensor", lambda e: e.transpose(
                    ptw[:, :], wb_sb[:, 0:n_e], identb[:]), a_f)
                p.wait("scalar", a_f, a_f.n)
                p.op("scalar", lambda e, tg=tg: e.activation(
                    wdt[:, tg * 128:(tg + 1) * 128], ptw[0:n_e, :], AF.Copy, scale=1.0), a_f)
            p.wait("tensor", a_f, a_f.n)

    emit_router_and_topk(NE, KK)

    if layer == 0:
        with ExitStack() as s3:
            ps_ws = [s3.enter_context(nc.psum_tensor(f"ps_ws{i}", [1, 512], F32, allow_name_mangling=True)) for i in range(2)]
            for hh in range(2):
                p.op("tensor", lambda e, hh=hh: e.matmul(
                    ps_ws[hh][:], onec[0:64, :], wdt[:, hh * 512:(hh + 1) * 512],
                    start=True, stop=True), a_f)
            wsn = a_f.n
            p.wait("vector", a_f, wsn)
            for hh in range(2):
                p.op("vector", lambda e, hh=hh: e.tensor_scalar(
                    ws1b[:, hh * 512:(hh + 1) * 512], ps_ws[hh][:], 1.0, None, ALU.add), a_f)
            p.wait("tensor", a_f, a_f.n)
            ps_wb = s3.enter_context(nc.psum_tensor("ps_wb", [128, 1024], F32, allow_name_mangling=True))
            for hh in range(2):
                p.op("tensor", lambda e, hh=hh: e.matmul(
                    ps_wb[:, hh * 512:(hh + 1) * 512], oner[:],
                    ws1b[:, hh * 512:(hh + 1) * 512], start=True, stop=True), a_f)
            p.wait("scalar", a_f, a_f.n)
            p.op("scalar", lambda e: e.activation(wsb_sb[:], ps_wb[:], AF.Copy, scale=1.0), a_f)
            p.wait("tensor", a_f, a_f.n)
        wsbn = a_f.n
        rx1b = Ring(p, "rx1b", 2)
        rxr2 = Ring(p, "rxr2", 2)
        st1.wait_all(p, "sync")
        p.wait("sync", a_r, a_r.n)
        with ExitStack() as s5s:
            stats2 = [s5s.enter_context(nc.psum_tensor(f"st2_{i}", [1, 512], F32, allow_name_mangling=True)) for i in range(4)]
            s5p = ExitStack()
            pbm = [s5p.enter_context(nc.psum_tensor(f"pbm{i}", [128, 1024], F32, allow_name_mangling=True)) for i in range(2)]
            stat_pend = []
            a_r_base = a_r.n
            a_sm_base = a_sm.n
            for ec in range(NC):
                for fn2 in stat_pend:
                    fn2()
                stat_pend = []
                if ec >= 2:
                    p.waitf("tensor", (a_r, a_r_base + ec - 1))
                else:
                    p.wait("tensor", a_f, wsbn)
                for t in range(2):
                    p.op("tensor", lambda e, ec=ec, t=t: e.matmul(
                        pbm[ec % 2][:, t * 512:(t + 1) * 512],
                        ebb[:, ec * 128:(ec + 1) * 128],
                        wdt[:, t * 512:(t + 1) * 512], start=True, stop=True),
                        a_f if t == 1 else None)
                bmn = a_f.n
                p.wait("sync", a_r, a_r_base + max(0, ec - 1))
                rx1b.dma(p, "sync", ec, xp_sb[ec % 2][:], x1_dram[ec * 128:(ec + 1) * 128, :])
                rx1b.wait(p, "vector", ec)
                p.wait("vector", a_f, bmn)
                if ec >= 2:
                    p.waitf("vector", rxr2.free(ec - 2))
                p.wait("vector", a_ln, a_ln.n)
                p.op("vector", lambda e, ec=ec: e.tensor_mul(
                    xr_sb[ec % 2][:], xp_sb[ec % 2][:], wsb_sb[:]), a_ln)
                p.wait("vector", a_ln, a_ln.n)
                p.op("vector", lambda e, ec=ec: e.tensor_add(
                    xr_sb[ec % 2][:], xr_sb[ec % 2][:], pbm[ec % 2][:]), a_r)
                rn = a_r.n
                p.wait("sync", a_r, rn)
                rxr2.dma(p, "sync", ec, xr_dram[ec * 128:(ec + 1) * 128, :], xr_sb[ec % 2][:])
                p.wait("scalar", a_r, rn)
                if ec >= 2:
                    p.wait("scalar", a_sm, a_sm_base + ec - 1)
                p.op("scalar", lambda e, ec=ec: e.activation(
                    xrb_sb[ec % 2][:], xr_sb[ec % 2][:], AF.Copy, scale=1.0), a_c)
                cn = a_c.n
                p.wait("vector", a_c, cn)
                if ec >= 2:
                    p.wait("vector", a_sm, a_sm_base + ec - 1)
                p.op("vector", lambda e, ec=ec: e.tensor_mul(
                    sq_sb[ec % 2][:], xrb_sb[ec % 2][:], xrb_sb[ec % 2][:]), a_q)
                qn = a_q.n
                def emit_def(ec=ec, cn=cn, qn=qn):
                    p.wait("tensor", a_c, cn)
                    p.wait("tensor", a_q, qn)
                    for hh in range(2):
                        p.op("tensor", lambda e, ec=ec, hh=hh: e.matmul(
                            stats2[hh][:], onec[:], xrb_sb[ec % 2][:, hh * 512:(hh + 1) * 512],
                            start=(ec == 0), stop=(ec == NC - 1), skip_group_check=True), None)
                        p.op("tensor", lambda e, ec=ec, hh=hh: e.matmul(
                            stats2[2 + hh][:], onec[:], sq_sb[ec % 2][:, hh * 512:(hh + 1) * 512],
                            start=(ec == 0), stop=(ec == NC - 1), skip_group_check=True),
                            a_sm if hh == 1 else None)
                stat_pend.append(emit_def)
            for fn2 in stat_pend:
                fn2()
            s5p.close()
            stf = ln_finalize_apply(stats2, a_sm.n, xr_dram, rxr2, xout, False, "ln2")
    else:
        with ExitStack() as s3:
            ps_h = s3.enter_context(nc.psum_tensor("ps_h", [128, 1024], F32, allow_name_mangling=True))
            ps_cb = s3.enter_context(nc.psum_tensor("ps_cb", [128, 1024], F32, allow_name_mangling=True))
            p.wait("tensor", a_f, a_f.n)
            for t in range(2):
                p.op("tensor", lambda e, t=t: e.matmul(
                    ps_cb[:, t * 512:(t + 1) * 512], ex8_sb[:],
                    wdt[:, t * 512:(t + 1) * 512], start=True, stop=True), a_f)
            cbn = a_f.n
            p.wait("scalar", a_f, cbn)
            p.op("scalar", lambda e: e.activation(cb_sb[:], ps_cb[:], AF.Copy, scale=1.0), a_f)
            p.wait("tensor", a_f, a_f.n)
            for dd in range(NC):
                for t in range(2):
                    p.op("tensor", lambda e, dd=dd, t=t: e.matmul(
                        ps_h[:, t * 512:(t + 1) * 512],
                        asb[:, dd * 128:(dd + 1) * 128],
                        ab[:, dd * 1024 + t * 512: dd * 1024 + (t + 1) * 512],
                        start=(dd == 0), stop=(dd == NC - 1)),
                        a_f if (dd == NC - 1 and t == 1) else None)
            hn = a_f.n
            p.wait("vector", a_f, hn)
            p.op("vector", lambda e: e.scalar_tensor_tensor(
                hs_sb[:], ps_h[:], 2.0, cb_sb[:], ALU.mult, ALU.mult), a_f)
            p.wait("tensor", a_f, a_f.n)
        hs_n = a_f.n
        rxp2 = Ring(p, "rxp2", 2)
        rxr2 = Ring(p, "rxr2", 2)
        p.wait("sync", a_r, a_r.n)
        with ExitStack() as s6s:
            stats2 = [s6s.enter_context(nc.psum_tensor(f"st2_{i}", [1, 512], F32, allow_name_mangling=True)) for i in range(4)]
            s6p = ExitStack()
            psq2b = [s6p.enter_context(nc.psum_tensor(f"psqb{i}", [128, 1024], F32, allow_name_mangling=True)) for i in range(2)]

            def extra1(ec, psten, s_mm):
                p.wait("tensor", a_f, hs_n)
                for t in range(2):
                    p.op("tensor", lambda e, ec=ec, t=t, psten=psten: e.matmul(
                        psten[:, t * 512:(t + 1) * 512],
                        bsb[:, ec * 128:(ec + 1) * 128],
                        hs_sb[:, t * 512:(t + 1) * 512], start=False, stop=True),
                        s_mm if t == 1 else None)

            a_r_base = a_r.n
            st1.wait_all(p, "sync")
            ev2 = mk_evac(rxp2, rxr2, stats2, a_sm.n,
                          lambda e, ec, psten: e.scalar_tensor_tensor(
                              xr_sb[ec % 2][:], psten[:], bbf[:, ec:ec + 1],
                              xp_sb[ec % 2][:], ALU.add, ALU.add),
                          x1_dram)
            def ev2g(ec, psten, defer, mm_need, s_mm):
                p.wait("sync", a_r, a_r_base + max(0, ec - 1))
                return ev2(ec, psten, defer, mm_need, s_mm)
            emit_proj(p, "wb", wbase, NC, ab, psq2b, ev2g,
                      pre_mm_waits=[(a_f, hs_n), (a_c, ab2_ready)], extra_mm_cb=extra1)
            s6p.close()
            stf = ln_finalize_apply(stats2, a_sm.n, xr_dram, rxr2, xout, False, "ln2")

    stf.wait_all(p, "gpsimd")


def build_post(layer):
    nc = mk_nc()
    attin = nc.dram_tensor("attin", [D, TL], BF16, kind="ExternalInput")
    wo = nc.dram_tensor("wo", [D, D], F32, kind="ExternalInput")
    xprev = nc.dram_tensor("xprev", [D, TL], F32, kind="ExternalInput")
    xout = nc.dram_tensor("xout", [D, TL], F32, kind="ExternalOutput")
    kw = {}
    if layer == 0:
        kw["rw"] = nc.dram_tensor("rw", [D, 64], F32, kind="ExternalInput")
        kw["rbias"] = nc.dram_tensor("rbias", [64, 1], F32, kind="ExternalInput")
        kw["eb64"] = nc.dram_tensor("eb64", [64, D], F32, kind="ExternalInput")
    else:
        kw["wbase"] = nc.dram_tensor("wbase", [D, D], F32, kind="ExternalInput")
        kw["bb_in"] = nc.dram_tensor("bb", [D, 1], F32, kind="ExternalInput")
        kw["wr"] = nc.dram_tensor("wr", [D, 8], F32, kind="ExternalInput")
        kw["astk"] = nc.dram_tensor("astk", [D, 128], F32, kind="ExternalInput")
        kw["bstk"] = nc.dram_tensor("bstk", [128, D], F32, kind="ExternalInput")
    xr_dram = nc.dram_tensor("xr_scr", [D, TL], F32)
    x1_dram = nc.dram_tensor("x1_scr", [D, TL], F32)
    with ExitStack() as ctx:
        ctx.enter_context(nc.allow_low_precision(reason="bf16 compute per stated tolerance"))
        p = P(nc, ctx)
        emit_post(nc, p, ctx, "", layer, attin, wo, xprev, xout, xr_dram, x1_dram, **kw)
        p.flush("post")
    return nc



def emit_load_xb_bf(p, tag, src_dram, xb):
    s_l = p.ctr(f"{tag}_l")
    for f in range(NC):
        p.dma("sync", xb[:, f * 1024:(f + 1) * 1024], src_dram[f * 128:(f + 1) * 128, :], s_l)
    return s_l


def owned_cols(c):
    return np.concatenate([np.arange((8 * k + c) * 128, (8 * k + c) * 128 + 128)
                           for k in range(8)])


def rope_tables():
    inv = 1.0 / (10000.0 ** (np.arange(0, 128, 2, dtype=np.float64) / 128.0))
    t = np.arange(SEQ, dtype=np.float64)
    fr = np.outer(t, inv)
    emb = np.concatenate([fr, fr], axis=-1)
    return np.cos(emb).astype(np.float32), np.sin(emb).astype(np.float32)


def build_fused():
    nc = mk_nc()
    xin = nc.dram_tensor("xin", [D, TL], F32, kind="ExternalInput")
    cos_in = nc.dram_tensor("cosin", [128, TL], F32, kind="ExternalInput")
    sin_in = nc.dram_tensor("sinin", [128, TL], F32, kind="ExternalInput")
    wq0 = nc.dram_tensor("wq0", [D, D], F32, kind="ExternalInput")
    wk0 = nc.dram_tensor("wk0", [D, D], F32, kind="ExternalInput")
    wv0 = nc.dram_tensor("wv0", [D, D], F32, kind="ExternalInput")
    wo0 = nc.dram_tensor("wo0", [D, D], F32, kind="ExternalInput")
    rw = nc.dram_tensor("rw", [D, 64], F32, kind="ExternalInput")
    rbias = nc.dram_tensor("rbias", [64, 1], F32, kind="ExternalInput")
    eb64 = nc.dram_tensor("eb64", [64, D], F32, kind="ExternalInput")
    wq1 = nc.dram_tensor("wq1", [D, D], F32, kind="ExternalInput")
    wk1 = nc.dram_tensor("wk1", [D, D], F32, kind="ExternalInput")
    wv1 = nc.dram_tensor("wv1", [D, D], F32, kind="ExternalInput")
    wo1 = nc.dram_tensor("wo1", [D, D], F32, kind="ExternalInput")
    wbase = nc.dram_tensor("wbase", [D, D], F32, kind="ExternalInput")
    bb_in = nc.dram_tensor("bb", [D, 1], F32, kind="ExternalInput")
    wr = nc.dram_tensor("wr", [D, 8], F32, kind="ExternalInput")
    astk = nc.dram_tensor("astk", [D, 128], F32, kind="ExternalInput")
    bstk = nc.dram_tensor("bstk", [128, D], F32, kind="ExternalInput")
    msk_in = nc.dram_tensor("msk", [8 * 128, 128], BF16, kind="ExternalInput")
    xout = nc.dram_tensor("xout", [D, TL], F32, kind="ExternalOutput")
    q0l = nc.dram_tensor("q0l", [D, TL], BF16)
    k0l = nc.dram_tensor("k0l", [D, TL], BF16)
    v0l = nc.dram_tensor("v0l", [TL, D], BF16)
    kag0 = nc.dram_tensor("kag0", [NCORE * D, TL], BF16, addr_space="Shared")
    vag0 = nc.dram_tensor("vag0", [NCORE * TL, D], BF16, addr_space="Shared")
    att0 = nc.dram_tensor("att0", [D, TL], BF16)
    x2d = nc.dram_tensor("x2d", [D, TL], F32)
    q1l = nc.dram_tensor("q1l", [D, TL], BF16)
    k1l = nc.dram_tensor("k1l", [D, TL], BF16)
    v1l = nc.dram_tensor("v1l", [TL, D], BF16)
    kag1 = nc.dram_tensor("kag1", [NCORE * D, TL], BF16, addr_space="Shared")
    vag1 = nc.dram_tensor("vag1", [NCORE * TL, D], BF16, addr_space="Shared")
    att1 = nc.dram_tensor("att1", [D, TL], BF16)
    xr_dram = nc.dram_tensor("xr_scr", [D, TL], F32)
    x1_dram = nc.dram_tensor("x1_scr", [D, TL], F32)

    with ExitStack() as octx:
        octx.enter_context(nc.allow_low_precision(reason="bf16 compute per stated tolerance"))
        p = P(nc, octx)

        def ag_phase(name, pairs):
            cc = p.ctr("cc")
            for (s, d) in pairs:
                p.op("gpsimd", lambda e, s=s, d=d: e.collective_compute(
                    "AllGather", ALU.bypass, replica_groups=[list(range(NCORE))],
                    ins=[s[:]], outs=[d[:]]), cc)
            p.wait("gpsimd", cc, cc.n)
            p.flush(name)
            nc.all_engine_barrier()

        with ExitStack() as c:
            p.actx = c
            emit_qkv(nc, p, c, "0", True, xin, wq0, wk0, wv0, q0l, k0l, v0l, cos_in, sin_in)
        p.flush("phA")
        nc.all_engine_barrier()
        ag_phase("ag0", [(k0l, kag0), (v0l, vag0)])
        with ExitStack() as c:
            p.actx = c
            emit_attn(nc, p, c, False, L0_SCALE, q0l, kag0, vag0, att0)
        p.flush("phB")
        nc.all_engine_barrier()
        with ExitStack() as c:
            p.actx = c
            emit_post(nc, p, c, "0", 0, att0, wo0, xin, x2d, xr_dram, x1_dram,
                      rw=rw, rbias=rbias, eb64=eb64)
        p.flush("phC")
        nc.all_engine_barrier()
        with ExitStack() as c:
            p.actx = c
            emit_qkv(nc, p, c, "1", False, x2d, wq1, wk1, wv1, q1l, k1l, v1l)
        p.flush("phD")
        nc.all_engine_barrier()
        ag_phase("ag1", [(k1l, kag1), (v1l, vag1)])
        with ExitStack() as c:
            p.actx = c
            emit_attn(nc, p, c, True, L1_SCALE, q1l, kag1, vag1, att1, msk_in)
        p.flush("phE")
        nc.all_engine_barrier()
        with ExitStack() as c:
            p.actx = c
            emit_post(nc, p, c, "1", 1, att1, wo1, x2d, xout, xr_dram, x1_dram,
                      wbase=wbase, bb_in=bb_in, wr=wr, astk=astk, bstk=bstk)
        p.flush("phF")
    return nc


def kernel(**inputs):
    g = {k: np.asarray(v) for k, v in inputs.items()}
    x = np.ascontiguousarray(g["L0_attn_hidden_states"][0].T)
    wqT = np.ascontiguousarray(g["L0_attn_Wq"].T.astype(np.float32, copy=False))
    wkT = np.ascontiguousarray(g["L0_attn_Wk"].T.astype(np.float32, copy=False))
    wvT = np.ascontiguousarray(g["L0_attn_Wv"].T.astype(np.float32, copy=False))
    woT = np.ascontiguousarray(g["L0_attn_Wo"].T.astype(np.float32, copy=False))
    rwT = np.ascontiguousarray(g["L0_ffn_router_weight"].T.astype(np.float32, copy=False))
    rbias = np.ascontiguousarray(g["L0_ffn_router_bias"].reshape(64, 1).astype(np.float32, copy=False))
    eb64v = np.ascontiguousarray(g["L0_ffn_expert_bias"].astype(np.float32, copy=False))
    wq1f = g["L1_attn_wq"].astype(np.float32, copy=False)
    wq1u = np.ascontiguousarray(
        np.concatenate([wq1f[:, h * 192:h * 192 + 128] for h in range(32)], axis=1))
    wkv1 = g["L1_attn_wkv"].astype(np.float32, copy=False)
    wk1v = np.ascontiguousarray(wkv1[:, 0:4096])
    wv1v = np.ascontiguousarray(wkv1[:, 4096:8192])
    wo1v = np.ascontiguousarray(g["L1_attn_wo"].astype(np.float32, copy=False))
    wbT = np.ascontiguousarray(g["L1_ffn_W_base"].T.astype(np.float32, copy=False))
    bbv = np.ascontiguousarray(g["L1_ffn_b_base"].reshape(4096, 1).astype(np.float32, copy=False))
    wrT = np.ascontiguousarray(g["L1_ffn_W_router"].T.astype(np.float32, copy=False))
    astkv = np.ascontiguousarray(
        g["L1_ffn_A_stack"].astype(np.float32, copy=False).reshape(128, 4096).T)
    bstkv = np.ascontiguousarray(
        g["L1_ffn_B_stack"].astype(np.float32, copy=False).transpose(0, 2, 1).reshape(128, 4096))

    cols = [owned_cols(c) for c in range(NCORE)]
    cos, sin = rope_tables()
    nc = build_fused()
    in_maps = []
    for c in range(NCORE):
        in_maps.append({
            "xin": np.ascontiguousarray(x[:, cols[c]]),
            "cosin": np.ascontiguousarray(cos[cols[c]].T),
            "sinin": np.ascontiguousarray(sin[cols[c]].T),
            "wq0": wqT, "wk0": wkT, "wv0": wvT, "wo0": woT,
            "rw": rwT, "rbias": rbias, "eb64": eb64v,
            "wq1": wq1u, "wk1": wk1v, "wv1": wv1v, "wo1": wo1v,
            "wbase": wbT, "bb": bbv, "wr": wrT, "astk": astkv, "bstk": bstkv,
            "msk": build_masks(c),
        })
    res = run_bass_kernel_spmd(nc, in_maps, core_ids=list(range(NCORE))).results
    out = np.empty((D, SEQ), dtype=np.float32)
    for c in range(NCORE):
        out[:, cols[c]] = res[c]["xout"]
    return np.ascontiguousarray(out.T).reshape(1, SEQ, D)


def run_qkv_launch(rope, x_full_T, wq_de, wk_de, wv_de):
    nc = build_qkv(rope)
    in_maps = []
    if rope:
        cos, sin = rope_tables()
    for c in range(NCORE):
        cols = owned_cols(c)
        m = {"xin": np.ascontiguousarray(x_full_T[:, cols]),
             "wq": wq_de, "wk": wk_de, "wv": wv_de}
        if rope:
            m["cosin"] = np.ascontiguousarray(cos[cols].T)
            m["sinin"] = np.ascontiguousarray(sin[cols].T)
        in_maps.append(m)
    res = run_bass_kernel_spmd(nc, in_maps, core_ids=list(range(NCORE)))
    return res.results


def build_masks(c):
    m = np.zeros((8, 128, 128), dtype=np.float32)
    for r in range(8):
        if r < c:
            m[r] = 1.0
        elif r == c:
            m[r] = np.triu(np.ones((128, 128), dtype=np.float32))
    return np.ascontiguousarray(m.reshape(8 * 128, 128).astype(BF))


def run_qkv(rope, xT_cores, wq_de, wk_de, wv_de, extra=None):
    nc = build_qkv(rope)
    in_maps = []
    for c in range(NCORE):
        m = {"xin": xT_cores[c], "wq": wq_de, "wk": wk_de, "wv": wv_de}
        if rope:
            m.update(extra[c])
        in_maps.append(m)
    return run_bass_kernel_spmd(nc, in_maps, core_ids=list(range(NCORE))).results


def kernel_unfused(**inputs):
    g = {k: np.asarray(v) for k, v in inputs.items()}
    x = np.ascontiguousarray(g["L0_attn_hidden_states"][0].T)
    wqT = np.ascontiguousarray(g["L0_attn_Wq"].T.astype(np.float32, copy=False))
    wkT = np.ascontiguousarray(g["L0_attn_Wk"].T.astype(np.float32, copy=False))
    wvT = np.ascontiguousarray(g["L0_attn_Wv"].T.astype(np.float32, copy=False))
    woT = np.ascontiguousarray(g["L0_attn_Wo"].T.astype(np.float32, copy=False))
    rwT = np.ascontiguousarray(g["L0_ffn_router_weight"].T.astype(np.float32, copy=False))
    rbias = np.ascontiguousarray(g["L0_ffn_router_bias"].reshape(64, 1).astype(np.float32, copy=False))
    eb64 = np.ascontiguousarray(g["L0_ffn_expert_bias"].astype(np.float32, copy=False))
    wq1 = g["L1_attn_wq"].astype(np.float32, copy=False)
    wq1u = np.ascontiguousarray(
        np.concatenate([wq1[:, h * 192:h * 192 + 128] for h in range(32)], axis=1))
    wkv1 = g["L1_attn_wkv"].astype(np.float32, copy=False)
    wk1 = np.ascontiguousarray(wkv1[:, 0:4096])
    wv1 = np.ascontiguousarray(wkv1[:, 4096:8192])
    wo1 = np.ascontiguousarray(g["L1_attn_wo"].astype(np.float32, copy=False))
    wbT = np.ascontiguousarray(g["L1_ffn_W_base"].T.astype(np.float32, copy=False))
    bb = np.ascontiguousarray(g["L1_ffn_b_base"].reshape(4096, 1).astype(np.float32, copy=False))
    wrT = np.ascontiguousarray(g["L1_ffn_W_router"].T.astype(np.float32, copy=False))
    astk = np.ascontiguousarray(
        g["L1_ffn_A_stack"].astype(np.float32, copy=False).reshape(128, 4096).T)
    bstk = np.ascontiguousarray(
        g["L1_ffn_B_stack"].astype(np.float32, copy=False).transpose(0, 2, 1).reshape(128, 4096))

    cols = [owned_cols(c) for c in range(NCORE)]
    xT_cores = [np.ascontiguousarray(x[:, cols[c]]) for c in range(NCORE)]
    cos, sin = rope_tables()
    rope_extra = [{"cosin": np.ascontiguousarray(cos[cols[c]].T),
                   "sinin": np.ascontiguousarray(sin[cols[c]].T)} for c in range(NCORE)]

    resA = run_qkv(True, xT_cores, wqT, wkT, wvT, rope_extra)
    kag0 = np.concatenate([r["kout"] for r in resA], axis=0)
    vag0 = np.concatenate([r["vout"] for r in resA], axis=0)

    ncB = build_attn(False, L0_SCALE)
    in_maps = [{"qin": resA[c]["qout"], "kag": kag0, "vag": vag0} for c in range(NCORE)]
    resB = run_bass_kernel_spmd(ncB, in_maps, core_ids=list(range(NCORE))).results

    ncC = build_post(0)
    in_maps = [{"attin": resB[c]["att"], "wo": woT, "xprev": xT_cores[c],
                "rw": rwT, "rbias": rbias, "eb64": eb64} for c in range(NCORE)]
    resC = run_bass_kernel_spmd(ncC, in_maps, core_ids=list(range(NCORE))).results

    x2_cores = [np.ascontiguousarray(resC[c]["xout"]) for c in range(NCORE)]
    resD = run_qkv(False, x2_cores, wq1u, wk1, wv1)
    kag1 = np.concatenate([r["kout"] for r in resD], axis=0)
    vag1 = np.concatenate([r["vout"] for r in resD], axis=0)

    ncE = build_attn(True, L1_SCALE)
    in_maps = [{"qin": resD[c]["qout"], "kag": kag1, "vag": vag1,
                "msk": build_masks(c)} for c in range(NCORE)]
    resE = run_bass_kernel_spmd(ncE, in_maps, core_ids=list(range(NCORE))).results

    ncF = build_post(1)
    in_maps = [{"attin": resE[c]["att"], "wo": wo1, "xprev": x2_cores[c],
                "wbase": wbT, "bb": bb, "wr": wrT, "astk": astk, "bstk": bstk}
               for c in range(NCORE)]
    resF = run_bass_kernel_spmd(ncF, in_maps, core_ids=list(range(NCORE))).results

    out = np.empty((D, SEQ), dtype=np.float32)
    for c in range(NCORE):
        out[:, cols[c]] = resF[c]["xout"]
    return np.ascontiguousarray(out.T).reshape(1, SEQ, D)


# version 07
# speedup vs baseline: 1.2232x; 1.0233x over previous
import numpy as np
import ml_dtypes
from contextlib import ExitStack
import concourse.bass as bass
import concourse.mybir as mybir
from concourse.bass_utils import run_bass_kernel_spmd

F32 = mybir.dt.float32
BF16 = mybir.dt.bfloat16
AF = mybir.ActivationFunctionType
ALU = mybir.AluOpType
AX = mybir.AxisListType
BF = ml_dtypes.bfloat16

NCORE = 8
D = 4096
NC = 32
TL = 1024
NH = 32
SEQ = 8192
L0_SCALE = float(1.0 / np.sqrt(np.float32(128.0)))
L1_SCALE = float(np.float64(192.0) ** (-0.5))
EPS = 1e-5


class Ctr:
    def __init__(self, ctx, nc, name):
        self.h = ctx.enter_context(nc.semaphore(name))
        self.n = 0


class Ring:
    def __init__(self, p, name, n):
        self.sems = [p.ctr(f"{name}_r{i}") for i in range(n)]
        self.n = n
        self.rec = {}

    def dma(self, p, eng, i, out, in_, **kw):
        s = self.sems[i % self.n]
        p.dma(eng, out, in_, s, **kw)
        self.rec[i] = (s, s.n)

    def wait(self, p, eng, i):
        p.waitf(eng, self.rec[i])

    def free(self, i):
        return self.rec[i]

    def wait_all(self, p, eng):
        for s in self.sems:
            p.wait(eng, s, s.n)


class P:
    ENGS = ("sync", "gpsimd", "tensor", "vector", "scalar")

    def __init__(self, nc, ctx):
        self.nc = nc
        self.ctx = ctx
        self.actx = ctx
        self.s = {e: [] for e in P.ENGS}
        self._ctrs = {}

    def ctr(self, name):
        if name not in self._ctrs:
            self._ctrs[name] = Ctr(self.ctx, self.nc, name)
        return self._ctrs[name]

    def wait(self, eng, ctr, val):
        if val > 0:
            h = ctr.h
            self.s[eng].append(lambda e: e.wait_ge(h, val))

    def waitf(self, eng, free):
        if free is not None:
            self.wait(eng, free[0], free[1])

    def op(self, eng, fn, inc=None, k=1):
        if inc is None:
            self.s[eng].append(fn)
            return None
        h = inc.h
        self.s[eng].append(lambda e: fn(e).then_inc(h, k))
        inc.n += k
        return inc.n

    def dma(self, eng, out, in_, inc, **kw):
        h = inc.h
        self.s[eng].append(lambda e: e.dma_start(out=out, in_=in_, **kw).then_inc(h, 16))
        inc.n += 16
        return inc.n

    def flush(self, name):
        with self.nc.Block(name) as blk:
            for eng in P.ENGS:
                ops = self.s[eng]
                if not ops:
                    continue
                def runner(e, ops=ops):
                    for f in ops:
                        f(e)
                getattr(blk, eng)(runner)
        self.s = {e: [] for e in P.ENGS}


def mk_nc():
    return bass.Bass("TRN2", target_bir_lowering=False, num_devices=NCORE)


def emit_load_xb(p, tag, x_dram, xb):
    nc, ctx = p.nc, p.actx
    xf = [ctx.enter_context(nc.sbuf_tensor(f"{tag}_xf{i}", [128, 1024], F32, allow_name_mangling=True)) for i in range(2)]
    rl = Ring(p, f"{tag}_xl", 2)
    s_c = p.ctr(f"{tag}_xc")
    c_base = s_c.n
    for f in range(NC):
        p.wait("sync", s_c, c_base + f - 1)
        rl.dma(p, "sync", f, xf[f % 2][:], x_dram[f * 128:(f + 1) * 128, :])
        rl.wait(p, "scalar", f)
        p.op("scalar", lambda e, f=f: e.activation(
            xb[:, f * 1024:(f + 1) * 1024], xf[f % 2][:], AF.Copy, scale=1.0), s_c)
    return s_c


def emit_proj(p, tag, w_dram, n_d, xb, psq, evac_cb, pre_mm_waits=None, extra_mm_cb=None,
              psum_free=None, pf_base=0):
    nc, ctx = p.nc, p.actx
    n_e = w_dram.shape[1] // 128
    nrot = len(psq)
    wsf = [ctx.enter_context(nc.sbuf_tensor(f"{tag}_wsf{i}", [128, 256], F32, allow_name_mangling=True)) for i in range(4)]
    wsb = [ctx.enter_context(nc.sbuf_tensor(f"{tag}_wsb{i}", [128, 256], BF16, allow_name_mangling=True)) for i in range(4)]
    rwd = Ring(p, f"{tag}_wd", 4)
    s_wc = p.ctr(f"{tag}_wc")
    wc_base = s_wc.n
    s_mm = p.ctr(f"{tag}_mm")
    mm_done = {}
    if psum_free is None:
        psum_free = {}
    pend = []

    def defer(fn):
        pend.append(fn)

    n_eg = n_e // 2
    for eg in range(n_eg):
        for fn in pend:
            fn()
        pend = []
        for d in range(n_d):
            idx = eg * n_d + d
            p.wait("sync", s_wc, wc_base + idx - 3)
            rwd.dma(p, "sync", idx, wsf[idx % 4][:],
                    w_dram[d * 128:(d + 1) * 128, eg * 256:(eg + 1) * 256])
            rwd.wait(p, "vector", idx)
            p.wait("vector", s_mm, mm_done.get(idx - 4, 0))
            p.op("vector", lambda e, a=wsb[idx % 4], b=wsf[idx % 4]:
                 e.tensor_copy(a[:], b[:]), s_wc)
            p.wait("tensor", s_wc, wc_base + idx + 1)
            if d == 0:
                if eg == 0 and pre_mm_waits:
                    for (ctr, val) in pre_mm_waits:
                        p.wait("tensor", ctr, val)
                for e2 in range(2):
                    ec = eg * 2 + e2
                    if pf_base + ec - nrot >= 0:
                        p.waitf("tensor", psum_free[pf_base + ec - nrot])
            for e2 in range(2):
                ec = eg * 2 + e2
                psten = psq[ec % nrot]
                for t in range(2):
                    fn = (lambda e, psten=psten, w=wsb[idx % 4], e2=e2, d=d, t=t:
                          e.matmul(psten[:, t * 512:(t + 1) * 512],
                                   w[:, e2 * 128:(e2 + 1) * 128],
                                   xb[:, d * 1024 + t * 512: d * 1024 + (t + 1) * 512],
                                   start=(d == 0),
                                   stop=(d == n_d - 1 and extra_mm_cb is None)))
                    p.op("tensor", fn, s_mm if (e2 == 1 and t == 1) else None)
            mm_done[idx] = s_mm.n
        for e2 in range(2):
            ec = eg * 2 + e2
            if extra_mm_cb is not None:
                extra_mm_cb(ec, psq[ec % nrot], s_mm)
            psum_free[pf_base + ec] = evac_cb(ec, psq[ec % nrot], defer, s_mm.n, s_mm)
    for fn in pend:
        fn()
    return psum_free, s_mm


def emit_qkv(nc, p, ctx, tag, rope, xin, wq, wk, wv, qout, kout, vout,
             cos_in=None, sin_in=None, ag_cb=None):
    if rope:
        rsign = np.zeros((128, 128), dtype=BF)
        for i in range(64):
            rsign[i + 64, i] = -1.0
            rsign[i, i + 64] = 1.0
        rsign_d = nc.inline_tensor(rsign, f"rsign{tag}")
    ident_d = nc.inline_tensor(np.eye(128, dtype=BF), f"ident{tag}")
    sb = lambda n, s, d: ctx.enter_context(nc.sbuf_tensor(n, s, d, allow_name_mangling=True))
    psa = lambda n, s, d=F32: ctx.enter_context(nc.psum_tensor(n, s, d, allow_name_mangling=True))

    xb = sb("xb", [128, NC * 1024], BF16)
    psq = [psa(f"psq{i}", [128, 1024]) for i in range(3)]
    ptr_bf = psa("ptr_bf", [128, 1024], BF16)
    prot = psa("prot", [128, 512])
    ev_sb = [sb(f"ev{i}", [128, 1024], BF16) for i in range(3)]
    ob_sb = [sb(f"ob{i}", [128, 1024], BF16) for i in range(2)]
    tmp1 = [sb(f"tmp1{i}", [128, 1024], BF16) for i in range(2)]
    tmp2 = [sb(f"tmp2{i}", [128, 1024], BF16) for i in range(2)]
    ident_sb = sb("ident_sb", [128, 128], BF16)
    if rope:
        cosb = sb("cosb", [128, 1024], BF16)
        sinb = sb("sinb", [128, 1024], BF16)
        cosf = sb("cosf", [128, 1024], F32)
        rs_sb = sb("rs_sb", [128, 128], BF16)

    s_aux = p.ctr("aux")
    s_rot = p.ctr("rot")
    s_rop = p.ctr("rope")
    rst_ob = Ring(p, "st_ob", 2)
    rst_eb = Ring(p, "st_eb", 3)
    ob_sti = [0]
    eb_sti = [0]

    aux0 = s_aux.n
    p.dma("sync", ident_sb[:], ident_d[:], s_aux)
    if rope:
        p.dma("sync", cosf[:], cos_in[:], s_aux)
        p.dma("sync", rs_sb[:], rsign_d[:], s_aux)
        p.wait("vector", s_aux, aux0 + 48)
        p.op("vector", lambda e: e.tensor_copy(cosb[:], cosf[:]), s_aux)
        p.wait("sync", s_aux, aux0 + 49)
        p.dma("sync", cosf[:], sin_in[:], s_aux)
        p.wait("vector", s_aux, aux0 + 65)
        p.op("vector", lambda e: e.tensor_copy(sinb[:], cosf[:]), s_aux)
    aux_ready = s_aux.n

    xb_ready = emit_load_xb(p, "x", xin, xb)

    G = {"n": 0}
    eb_free = {}
    ob_free = {}
    rot_free = {}
    tmp_free = {}

    s_evq = p.ctr("evq")

    def make_evac(kind, out_dram):
        def evac(ec, psten, defer, mm_need, s_mm):
            gec = G["n"]; G["n"] += 1
            eb = ev_sb[gec % 3]
            ob = ob_sb[gec % 2]
            t1, t2 = tmp1[gec % 2], tmp2[gec % 2]
            p.wait("scalar", s_mm, mm_need)
            p.waitf("scalar", eb_free.get(gec - 3))
            p.op("scalar", lambda e: e.activation(
                eb[:], psten[:], AF.Copy, scale=1.0), s_evq)
            ev_now = s_evq.n

            if kind == "plain":
                p.wait("sync", s_evq, ev_now)
                rst_eb.dma(p, "sync", gec, out_dram[ec * 128:(ec + 1) * 128, :], eb[:])
                eb_free[gec] = rst_eb.free(gec)
                return (s_evq, ev_now)

            def emit_def():
                if kind == "rope":
                    p.wait("tensor", s_evq, ev_now)
                    p.waitf("tensor", rot_free.get(gec - 1))
                    p.op("tensor", lambda e: e.matmul(
                        prot[:], rs_sb[:], eb[:, 0:512], start=True, stop=True), s_rot)
                    r1 = s_rot.n
                    p.wait("vector", s_rot, r1)
                    p.waitf("vector", tmp_free.get(gec - 2))
                    p.op("vector", lambda e: e.tensor_tensor(
                        t2[:, 0:512], prot[:], sinb[:, 0:512], ALU.mult), s_rop)
                    v1 = s_rop.n
                    p.wait("tensor", s_rop, v1)
                    p.op("tensor", lambda e: e.matmul(
                        prot[:], rs_sb[:], eb[:, 512:1024], start=True, stop=True), s_rot)
                    r2 = s_rot.n
                    rot_free[gec] = (s_rop, v1 + 1)
                    p.wait("vector", s_rot, r2)
                    p.op("vector", lambda e: e.tensor_tensor(
                        t2[:, 512:1024], prot[:], sinb[:, 512:1024], ALU.mult), s_rop)
                    p.op("vector", lambda e: e.tensor_mul(t1[:], eb[:], cosb[:]), s_rop)
                    v2 = s_rop.n
                    eb_free[gec] = (s_rop, v2)
                    p.waitf("vector", ob_free.get(gec - 2))
                    p.wait("vector", s_rop, v2)
                    p.op("vector", lambda e: e.tensor_add(ob[:], t1[:], t2[:]), s_rop)
                    tmp_free[gec] = (s_rop, s_rop.n)
                    p.wait("sync", s_rop, s_rop.n)
                    rst_ob.dma(p, "sync", gec, out_dram[ec * 128:(ec + 1) * 128, :], ob[:])
                    ob_free[gec] = rst_ob.free(gec)
                else:
                    p.wait("tensor", s_evq, ev_now)
                    p.waitf("tensor", rot_free.get(gec - 1))
                    def pe_tr(e):
                        inst = None
                        for i in range(8):
                            inst = e.transpose(ptr_bf[:, i * 128:(i + 1) * 128],
                                               eb[:, i * 128:(i + 1) * 128], ident_sb[:])
                        return inst
                    p.op("tensor", pe_tr, s_rot)
                    eb_free[gec] = (s_rot, s_rot.n)
                    p.waitf("scalar", ob_free.get(gec - 2))
                    p.wait("scalar", s_rot, s_rot.n)
                    p.op("scalar", lambda e: e.activation(
                        ob[:], ptr_bf[:], AF.Copy, scale=1.0), s_rop)
                    rot_free[gec] = (s_rop, s_rop.n)
                    p.wait("sync", s_rop, s_rop.n)
                    dst = out_dram.ap().rearrange("(b p) e -> p b e", p=128)[:, :, ec * 128:(ec + 1) * 128]
                    srcap = ob[:].rearrange("p (b d) -> p b d", d=128)
                    rst_ob.dma(p, "sync", gec, dst, srcap)
                    ob_free[gec] = rst_ob.free(gec)
            defer(emit_def)
            return (s_evq, ev_now)
        return evac

    kind_q = "rope" if rope else "plain"
    pf = {}

    def ring_snap():
        return [(s, s.n) for s in rst_ob.sems + rst_eb.sems]

    emit_proj(p, "k", wk, NC, xb, psq, make_evac(kind_q, kout),
              pre_mm_waits=[(xb_ready, xb_ready.n), (s_aux, aux_ready)],
              psum_free=pf, pf_base=0)
    if ag_cb is not None:
        ag_cb("k", ring_snap())
    emit_proj(p, "v", wv, NC, xb, psq, make_evac("v", vout),
              psum_free=pf, pf_base=32)
    if ag_cb is not None:
        ag_cb("v", ring_snap())
    emit_proj(p, "q", wq, NC, xb, psq, make_evac(kind_q, qout),
              psum_free=pf, pf_base=64)

    rst_ob.wait_all(p, "gpsimd")
    rst_eb.wait_all(p, "gpsimd")


def build_qkv(rope):
    nc = mk_nc()
    xin = nc.dram_tensor("xin", [D, TL], F32, kind="ExternalInput")
    wq = nc.dram_tensor("wq", [D, D], F32, kind="ExternalInput")
    wk = nc.dram_tensor("wk", [D, D], F32, kind="ExternalInput")
    wv = nc.dram_tensor("wv", [D, D], F32, kind="ExternalInput")
    qout = nc.dram_tensor("qout", [D, TL], BF16, kind="ExternalOutput")
    kout = nc.dram_tensor("kout", [D, TL], BF16, kind="ExternalOutput")
    vout = nc.dram_tensor("vout", [TL, D], BF16, kind="ExternalOutput")
    cos_in = sin_in = None
    if rope:
        cos_in = nc.dram_tensor("cosin", [128, TL], F32, kind="ExternalInput")
        sin_in = nc.dram_tensor("sinin", [128, TL], F32, kind="ExternalInput")
    with ExitStack() as ctx:
        ctx.enter_context(nc.allow_low_precision(reason="bf16 compute per stated tolerance"))
        p = P(nc, ctx)
        emit_qkv(nc, p, ctx, "", rope, xin, wq, wk, wv, qout, kout, vout, cos_in, sin_in)
        p.flush("qkv")
    return nc



def emit_attn(nc, p, ctx, causal, scale, qin, kag, vag, att, msk_in=None, pre_waits=None):
    sb = lambda n, s, d: ctx.enter_context(nc.sbuf_tensor(n, s, d, allow_name_mangling=True))
    psa = lambda n, s, d=F32: ctx.enter_context(nc.psum_tensor(n, s, d, allow_name_mangling=True))

    ksb = [sb(f"ksb{i}", [128, 64 * 128], BF16) for i in range(2)]
    vsb = [sb(f"vsb{i}", [128, 64 * 128], BF16) for i in range(2)]
    qsb = [sb(f"qsb{i}", [128, TL], BF16) for i in range(2)]
    esb = [sb(f"esb{i}", [128, 1024], BF16) for i in range(3)]
    dsb2 = [[sb(f"dsb{s}_{i}", [128, 1024], BF16) for i in range(2)] for s in range(2)]
    rbf = sb("rbf", [1, 1024], BF16)
    bcs2 = [sb(f"bcs{i}", [128, 512], F32) for i in range(2)]
    att_sb = [sb(f"attsb{i}", [128, TL], BF16) for i in range(2)]
    onec = sb("onec", [128, 1], BF16)
    oner = sb("oner", [1, 128], BF16)
    if causal:
        msk_sb = sb("msk_sb", [128, 8 * 128], BF16)

    pst = [psa(f"pst{i}", [128, 1024]) for i in range(2)]
    ps_pv = [psa(f"ps_pv{i}", [128, 512]) for i in range(2)]
    ps_den = psa("ps_den", [1, 512])
    ps_bc = psa("ps_bc", [128, 512])

    rkv = [p.ctr("rkv0"), p.ctr("rkv1")]
    rst = Ring(p, "rst", 2)
    a_s = p.ctr("a_s")
    a_x = p.ctr("a_x")
    a_e = p.ctr("a_e")
    a_p = p.ctr("a_p")
    a_v = p.ctr("a_v")
    a_x2 = p.ctr("a_x2")
    s_aux = p.ctr("aux")

    p.op("vector", lambda e: e.memset(onec[:], 1.0), s_aux)
    p.op("vector", lambda e: e.memset(oner[:], 1.0), s_aux)
    if causal:
        p.dma("sync", msk_sb[:].rearrange("p (r q) -> p r q", q=128),
              msk_in.ap().rearrange("(r p) q -> p r q", p=128), s_aux)
        p.wait("vector", s_aux, s_aux.n)

    kv_ready = {}

    def loads(h):
        hb = h % 2
        s = rkv[hb]
        for k8 in range(8):
            p.dma("sync", ksb[hb][:, k8 * 1024:(k8 + 1) * 1024].rearrange("p (c t) -> p c t", t=128),
                  kag.ap().rearrange("(c x) t -> x c t", c=8)[h * 128:(h + 1) * 128, :, k8 * 128:(k8 + 1) * 128], s)
            p.dma("sync", vsb[hb][:, k8 * 1024:(k8 + 1) * 1024].rearrange("p (c t) -> p c t", t=128),
                  vag.ap().rearrange("(c y) e -> y c e", c=8)[k8 * 128:(k8 + 1) * 128, :, h * 128:(h + 1) * 128], s)
        p.dma("sync", qsb[hb][:], qin[h * 128:(h + 1) * 128, :], s)
        kv_ready[h] = s.n

    gj0 = [None]
    scount = {}
    xcount = {}
    ecount = {}
    pcount = {}
    div_done = {}
    dme_done = {}

    if pre_waits:
        for (cw, vw) in pre_waits:
            p.wait("sync", cw, vw)
    loads(0)
    loads(1)

    nB = 64
    npair = 32 if causal else 64
    pend_div = []

    for h in range(NH):
        hb = h % 2
        p.wait("tensor", rkv[hb], kv_ready[h])

        dsb = dsb2[h % 2]
        p.waitf("vector", div_done.get(h - 2))
        p.op("vector", lambda e, dsb=dsb: e.memset(dsb[0][:], 0.0), a_v)
        p.op("vector", lambda e, dsb=dsb: e.memset(dsb[1][:], 0.0), a_v)
        dmn = a_v.n

        if gj0[0] is None:
            gj0[0] = 0
        gj_head = gj0[0]

        def colA(j):
            return (j // 8) * 128 if causal else 0

        def colB(j):
            if causal and j >= 32:
                return (j // 8 - 4) * 128
            return 0

        for j in range(nB):
            if j <= 2 and pend_div:
                pend_div.pop(0)()
            gj = gj0[0]; gj0[0] += 1
            paired = j < npair
            c0a = colA(j)
            c0b = colB(j)
            lo = c0a if paired else 512 + c0b
            p.waitf("tensor", ecount.get(gj - 2))
            if paired:
                p.op("tensor", lambda e, j=j, gj=gj, c0a=c0a, hb=hb: e.matmul(
                    pst[gj % 2][:, c0a:512],
                    ksb[hb][:, j * 128:(j + 1) * 128],
                    qsb[hb][:, c0a:512], start=True, stop=True), None)
            p.op("tensor", lambda e, j=j, gj=gj, c0b=c0b, hb=hb: e.matmul(
                pst[gj % 2][:, 512 + c0b:1024],
                ksb[hb][:, j * 128:(j + 1) * 128],
                qsb[hb][:, 512 + c0b:1024], start=True, stop=True), a_s)
            scount[gj] = a_s.n
            if j >= 2:
                jj = j - 2
                gjj = gj - 2
                p.waitf("tensor", ecount.get(gjj))
                if jj == 0:
                    p.waitf("tensor", div_done.get(h - 1))
                if jj < npair:
                    p.op("tensor", lambda e, jj=jj, gjj=gjj, hb=hb, c0=colA(jj): e.matmul(
                        ps_pv[0][:, c0:512],
                        vsb[hb][:, jj * 128:(jj + 1) * 128],
                        esb[gjj % 3][:, c0:512],
                        start=(jj == 0), stop=(jj == npair - 1)), None)
                p.op("tensor", lambda e, jj=jj, gjj=gjj, hb=hb, c0=colB(jj): e.matmul(
                    ps_pv[1][:, c0:512],
                    vsb[hb][:, jj * 128:(jj + 1) * 128],
                    esb[gjj % 3][:, 512 + c0:1024],
                    start=(jj == 0), stop=(jj == nB - 1)), a_p)
                pcount[gjj] = (a_p, a_p.n)
            p.wait("scalar", a_s, scount[gj])
            p.waitf("scalar", pcount.get(gj - 3))
            p.waitf("scalar", ecount.get(gj - 3))
            p.op("scalar", lambda e, gj=gj, lo=lo: e.activation(
                esb[gj % 3][:, lo:1024], pst[gj % 2][:, lo:1024], AF.Exp,
                scale=scale), a_x)
            xcount[gj] = a_x.n
            p.wait("vector", a_x, xcount[gj])
            if j <= 1:
                p.wait("vector", a_v, dmn)
            else:
                p.waitf("vector", ecount.get(gj - 2))
            mr = j % 8
            if causal and paired:
                p.op("vector", lambda e, gj=gj, c0a=c0a, mr=mr: e.tensor_tensor(
                    esb[gj % 3][:, c0a:c0a + 128], esb[gj % 3][:, c0a:c0a + 128],
                    msk_sb[:, mr * 128:(mr + 1) * 128], ALU.mult), a_v)
                p.wait("vector", a_v, a_v.n)
            if causal and j >= 32:
                p.op("vector", lambda e, gj=gj, c0b=c0b, mr=mr: e.tensor_tensor(
                    esb[gj % 3][:, 512 + c0b:512 + c0b + 128], esb[gj % 3][:, 512 + c0b:512 + c0b + 128],
                    msk_sb[:, mr * 128:(mr + 1) * 128], ALU.mult), a_v)
                p.wait("vector", a_v, a_v.n)
            p.op("vector", lambda e, gj=gj, lo=lo, dsb=dsb: e.tensor_tensor(
                dsb[gj % 2][:, lo:1024], dsb[gj % 2][:, lo:1024],
                esb[gj % 3][:, lo:1024], ALU.add), a_e)
            ecount[gj] = (a_e, a_e.n)

        for jj in (nB - 2, nB - 1):
            gjj = gj_head + jj
            p.waitf("tensor", ecount.get(gjj))
            if jj < npair:
                p.op("tensor", lambda e, jj=jj, gjj=gjj, hb=hb, c0=colA(jj): e.matmul(
                    ps_pv[0][:, c0:512],
                    vsb[hb][:, jj * 128:(jj + 1) * 128],
                    esb[gjj % 3][:, c0:512],
                    start=(jj == 0), stop=(jj == npair - 1)), None)
            p.op("tensor", lambda e, jj=jj, gjj=gjj, hb=hb, c0=colB(jj): e.matmul(
                ps_pv[1][:, c0:512],
                vsb[hb][:, jj * 128:(jj + 1) * 128],
                esb[gjj % 3][:, 512 + c0:1024],
                start=(jj == 0), stop=(jj == nB - 1)), a_p)
            pcount[gjj] = (a_p, a_p.n)
        hp_end = a_p.n

        p.waitf("vector", ecount[gj0[0] - 1])
        p.op("vector", lambda e, dsb=dsb: e.tensor_add(dsb[0][:], dsb[0][:], dsb[1][:]), a_v)
        castn = a_v.n

        def make_div(h=h, hb=hb, hp_end=hp_end, castn=castn, dsb=dsb):
            st = {}
            def stage1():
                p.wait("tensor", a_v, castn)
                if h >= 1:
                    p.waitf("tensor", div_done.get(h - 1))
                for off in (0, 512):
                    p.op("tensor", lambda e, off=off, dsb=dsb: e.matmul(
                        ps_den[:], onec[:], dsb[0][:, off:off + 512], start=True, stop=True), a_s)
                    on_n = a_s.n
                    p.wait("vector", a_s, on_n)
                    p.op("vector", lambda e, off=off: e.reciprocal(rbf[:, off:off + 512], ps_den[:]), a_v)
                    if off == 0:
                        p.wait("tensor", a_v, a_v.n)
                st["recip"] = a_v.n
            def stage2():
                p.wait("tensor", a_v, st["recip"])
                for i2, off in enumerate((0, 512)):
                    if i2 == 1:
                        p.wait("tensor", a_x2, a_x2.n)
                    p.op("tensor", lambda e, off=off: e.matmul(
                        ps_bc[:], oner[:], rbf[:, off:off + 512], start=True, stop=True), a_s)
                    bc_n = a_s.n
                    p.wait("scalar", a_s, bc_n)
                    p.op("scalar", lambda e, i2=i2: e.activation(
                        bcs2[i2][:], ps_bc[:], AF.Copy, scale=1.0), a_x2)
                st["bce"] = a_x2.n
            def stage3():
                p.wait("vector", a_x2, st["bce"])
                p.wait("vector", a_p, hp_end)
                p.waitf("vector", rst.free(h - 2) if h >= 2 else None)
                for i2, (pvp, off) in enumerate(((0, 0), (1, 512))):
                    p.op("vector", lambda e, pvp=pvp, off=off, hb=hb, i2=i2: e.tensor_tensor(
                        att_sb[hb][:, off:off + 512], ps_pv[pvp][:], bcs2[i2][:], ALU.mult), a_v)
                div_done[h] = (a_v, a_v.n)
                p.wait("sync", a_v, div_done[h][1])
                rst.dma(p, "sync", h, att[h * 128:(h + 1) * 128, :], att_sb[hb][:])
                if h + 2 < NH:
                    p.wait("sync", a_p, hp_end)
                    loads(h + 2)
            return [stage1, stage2, stage3]
        pend_div = make_div()
        if h == NH - 1:
            for fn in pend_div:
                fn()
            pend_div = []

    rst.wait_all(p, "gpsimd")


def build_attn(causal, scale):
    nc = mk_nc()
    qin = nc.dram_tensor("qin", [D, TL], BF16, kind="ExternalInput")
    kag = nc.dram_tensor("kag", [NCORE * D, TL], BF16, kind="ExternalInput")
    vag = nc.dram_tensor("vag", [NCORE * TL, D], BF16, kind="ExternalInput")
    att = nc.dram_tensor("att", [D, TL], BF16, kind="ExternalOutput")
    msk_in = None
    if causal:
        msk_in = nc.dram_tensor("msk", [8 * 128, 128], BF16, kind="ExternalInput")
    with ExitStack() as ctx:
        ctx.enter_context(nc.allow_low_precision(reason="bf16 compute per stated tolerance"))
        p = P(nc, ctx)
        emit_attn(nc, p, ctx, causal, scale, qin, kag, vag, att, msk_in)
        p.flush("attn")
    return nc



def emit_post(nc, p, ctx, tag, layer, attin, wo, xprev, xout, xr_dram, x1_dram,
              rw=None, rbias=None, eb64=None,
              wbase=None, bb_in=None, wr=None, astk=None, bstk=None):
    if layer == 0:
        NE = 64
        KK = 8
    else:
        ex8 = np.zeros((8, 128), dtype=BF)
        for e8 in range(8):
            ex8[e8, e8 * 16:(e8 + 1) * 16] = 1.0
        ex8_d = nc.inline_tensor(ex8, f"ex8{tag}")
        NE = 8
        KK = 2
    identb_d = nc.inline_tensor(np.eye(128, dtype=BF), f"identb{tag}")
    identf_d = nc.inline_tensor(np.eye(128, dtype=np.float32), f"identf{tag}")
    sb = lambda n, s, d: ctx.enter_context(nc.sbuf_tensor(n, s, d, allow_name_mangling=True))

    ab = sb("ab", [128, NC * 1024], BF16)
    xr_sb = [sb(f"xr{i}", [128, 1024], F32) for i in range(2)]
    xp_sb = [sb(f"xp{i}", [128, 1024], F32) for i in range(2)]
    xrb_sb = [sb(f"xrb{i}", [128, 1024], BF16) for i in range(2)]
    sq_sb = [sb(f"sq{i}", [128, 1024], BF16) for i in range(2)]
    x1c_sb = [sb(f"x1c{i}", [128, 1024], F32) for i in range(2)]
    onec = sb("onec", [128, 1], BF16)
    oner = sb("oner", [1, 128], BF16)
    epsb = sb("epsb", [1, 1], F32)
    identb = sb("identb_sb", [128, 128], BF16)
    identf = sb("identf_sb", [128, 128], F32)
    m_sb = sb("m_sb", [1, 1024], F32)
    v_sb = sb("v_sb", [1, 1024], F32)
    t1_sb = sb("t1_sb", [1, 1024], F32)
    rstd_sb = sb("rstd_sb", [1, 1024], F32)
    rstdb = sb("rstdb", [1, 1024], BF16)
    mrb = sb("mrb", [1, 1024], BF16)

    s_aux = p.ctr("aux")
    a_r = p.ctr("a_r")
    a_c = p.ctr("a_c")
    a_q = p.ctr("a_q")
    a_sm = p.ctr("a_sm")
    a_ln = p.ctr("a_ln")
    a_f = p.ctr("a_f")
    a_t = p.ctr("a_t")

    p.dma("sync", identb[:], identb_d[:], s_aux)
    p.dma("sync", identf[:], identf_d[:], s_aux)
    p.op("vector", lambda e: e.memset(onec[:], 1.0), s_aux)
    p.op("vector", lambda e: e.memset(oner[:], 1.0), s_aux)
    p.op("vector", lambda e: e.memset(epsb[:], EPS), s_aux)
    aux_n = s_aux.n

    ab_ready = emit_load_xb_bf(p, "ab", attin, ab)

    def ln_finalize_apply(stats, stats_total, src_dram, src_ring, dst_dram, cast_to_ab, tagn):
        ld = Ring(p, f"{tagn}_ld", 2)
        st = Ring(p, f"{tagn}_st", 2)
        src_ring.wait_all(p, "sync")
        for eng in ("sync", "vector", "scalar"):
            p.wait(eng, a_sm, stats_total)
            p.wait(eng, a_r, a_r.n)
            p.wait(eng, a_c, a_c.n)
            p.wait(eng, a_q, a_q.n)
        for hh in range(2):
            p.op("vector", lambda e, hh=hh: e.tensor_scalar(
                m_sb[:, hh * 512:(hh + 1) * 512], stats[hh][:], 1.0 / D, None, ALU.mult), a_ln)
            p.op("vector", lambda e, hh=hh: e.tensor_scalar(
                v_sb[:, hh * 512:(hh + 1) * 512], stats[2 + hh][:], 1.0 / D, None, ALU.mult), a_ln)
        p.wait("vector", a_ln, a_ln.n)
        p.op("vector", lambda e: e.tensor_mul(t1_sb[:], m_sb[:], m_sb[:]), a_ln)
        p.wait("vector", a_ln, a_ln.n)
        p.op("vector", lambda e: e.tensor_sub(v_sb[:], v_sb[:], t1_sb[:]), a_ln)
        vn = a_ln.n
        p.wait("scalar", a_ln, vn)
        p.op("scalar", lambda e: e.activation(t1_sb[:], v_sb[:], AF.Ln, bias=epsb[:], scale=1.0), a_ln)
        p.wait("scalar", a_ln, a_ln.n)
        p.op("scalar", lambda e: e.activation(rstd_sb[:], t1_sb[:], AF.Exp, scale=-0.5), a_ln)
        rn = a_ln.n
        p.wait("vector", a_ln, rn)
        p.op("vector", lambda e: e.tensor_mul(t1_sb[:], m_sb[:], rstd_sb[:]), a_ln)
        p.op("vector", lambda e: e.tensor_copy(rstdb[:], rstd_sb[:]), a_ln)
        p.wait("vector", a_ln, a_ln.n)
        p.op("vector", lambda e: e.tensor_copy(mrb[:], t1_sb[:]), a_ln)
        castn = a_ln.n
        with nc.psum_tensor(f"{tagn}_rsb", [128, 1024], F32, allow_name_mangling=True) as rs_bc, \
             nc.psum_tensor(f"{tagn}_mrb", [128, 1024], F32, allow_name_mangling=True) as mr_bc:
            p.wait("tensor", a_ln, castn)
            for hh in range(2):
                p.op("tensor", lambda e, hh=hh: e.matmul(
                    rs_bc[:, hh * 512:(hh + 1) * 512], oner[:],
                    rstdb[:, hh * 512:(hh + 1) * 512], start=True, stop=True), a_ln)
                p.op("tensor", lambda e, hh=hh: e.matmul(
                    mr_bc[:, hh * 512:(hh + 1) * 512], oner[:],
                    mrb[:, hh * 512:(hh + 1) * 512], start=True, stop=True), a_ln)
            bcn = a_ln.n
            apply_done = {}
            cast_done = {}
            for f in range(NC):
                p.waitf("sync", apply_done.get(f - 2))
                ld.dma(p, "sync", f, xr_sb[f % 2][:], src_dram[f * 128:(f + 1) * 128, :])
                ld.wait(p, "vector", f)
                p.wait("vector", a_ln, bcn)
                if f >= 2:
                    p.waitf("vector", apply_done.get(f - 2))
                    if cast_to_ab:
                        p.wait("vector", a_c, cast_done[f - 2])
                p.op("vector", lambda e, f=f: e.tensor_mul(x1c_sb[f % 2][:], xr_sb[f % 2][:], rs_bc[:]), a_ln)
                p.wait("vector", a_ln, a_ln.n)
                p.op("vector", lambda e, f=f: e.tensor_sub(x1c_sb[f % 2][:], x1c_sb[f % 2][:], mr_bc[:]), a_ln)
                here = a_ln.n
                if cast_to_ab:
                    p.wait("scalar", a_ln, here)
                    p.op("scalar", lambda e, f=f: e.activation(
                        ab[:, f * 1024:(f + 1) * 1024], x1c_sb[f % 2][:], AF.Copy, scale=1.0), a_c)
                    cast_done[f] = a_c.n
                p.wait("sync", a_ln, here)
                st.dma(p, "sync", f, dst_dram[f * 128:(f + 1) * 128, :], x1c_sb[f % 2][:])
                apply_done[f] = st.free(f)
            st.wait_all(p, "tensor")
        return st

    rxp1 = Ring(p, "rxp1", 2)
    rxr1 = Ring(p, "rxr1", 2)
    stats_pend = []

    def mk_evac(rxp, rxr, stats, a_sm_base, xr_src, resid_dram):
        def evac(ec, psten, defer, mm_need, s_mm):
            rxp.dma(p, "sync", ec, xp_sb[ec % 2][:], resid_dram[ec * 128:(ec + 1) * 128, :])
            p.wait("vector", s_mm, mm_need)
            rxp.wait(p, "vector", ec)
            if ec >= 2:
                p.waitf("vector", rxr.free(ec - 2))
            p.op("vector", lambda e, ec=ec, psten=psten: xr_src(e, ec, psten), a_r)
            rn = a_r.n
            p.wait("sync", a_r, rn)
            rxr.dma(p, "sync", ec, xr_dram[ec * 128:(ec + 1) * 128, :], xr_sb[ec % 2][:])
            p.wait("scalar", a_r, rn)
            if ec >= 2:
                p.wait("scalar", a_sm, a_sm_base + ec - 1)
            p.op("scalar", lambda e, ec=ec: e.activation(
                xrb_sb[ec % 2][:], xr_sb[ec % 2][:], AF.Copy, scale=1.0), a_c)
            cn = a_c.n
            p.wait("vector", a_c, cn)
            if ec >= 2:
                p.wait("vector", a_sm, a_sm_base + ec - 1)
            p.op("vector", lambda e, ec=ec: e.tensor_mul(
                sq_sb[ec % 2][:], xrb_sb[ec % 2][:], xrb_sb[ec % 2][:]), a_q)
            qn = a_q.n
            def emit_def(ec=ec, cn=cn, qn=qn):
                p.wait("tensor", a_c, cn)
                p.wait("tensor", a_q, qn)
                if ec == 0:
                    p.wait("tensor", s_aux, aux_n)
                for hh in range(2):
                    p.op("tensor", lambda e, ec=ec, hh=hh: e.matmul(
                        stats[hh][:], onec[:], xrb_sb[ec % 2][:, hh * 512:(hh + 1) * 512],
                        start=(ec == 0), stop=(ec == NC - 1), skip_group_check=True), None)
                    p.op("tensor", lambda e, ec=ec, hh=hh: e.matmul(
                        stats[2 + hh][:], onec[:], sq_sb[ec % 2][:, hh * 512:(hh + 1) * 512],
                        start=(ec == 0), stop=(ec == NC - 1), skip_group_check=True),
                        a_sm if hh == 1 else None)
            defer(emit_def)
            return (a_r, rn)
        return evac

    with ExitStack() as s1s:
        stats = [s1s.enter_context(nc.psum_tensor(f"st{i}", [1, 512], F32, allow_name_mangling=True)) for i in range(4)]
        a_r_base1 = a_r.n
        with ExitStack() as s1p:
            psq2 = [s1p.enter_context(nc.psum_tensor(f"psq{i}", [128, 1024], F32, allow_name_mangling=True)) for i in range(2)]
            ev1 = mk_evac(rxp1, rxr1, stats, 0,
                          lambda e, ec, psten: e.tensor_add(xr_sb[ec % 2][:], psten[:], xp_sb[ec % 2][:]),
                          xprev)
            def ev1g(ec, psten, defer, mm_need, s_mm):
                p.wait("sync", a_r, a_r_base1 + max(0, ec - 1))
                return ev1(ec, psten, defer, mm_need, s_mm)
            emit_proj(p, "wo", wo, NC, ab, psq2, ev1g,
                      pre_mm_waits=[(ab_ready, ab_ready.n)])
        st1 = ln_finalize_apply(stats, a_sm.n, xr_dram, rxr1, x1_dram, True, "ln1")
    ab2_ready = a_c.n

    lgf = sb("lgf", [64, 1024], F32)
    tokl2 = sb("tokl2", [128, 8 * NE], F32)
    ex_sb = sb("ex_t", [128, NE], F32)
    cur_sb = sb("cur_t", [128, NE], F32)
    w_sb = sb("w_t", [128, NE], F32)
    wb_sb = sb("wb_t", [128, NE], BF16)
    mx_sb = sb("mx_t", [128, 1], F32)
    nmx_sb = sb("nmx_t", [128, 1], F32)
    sm_sb = sb("sm_t", [128, 1], F32)
    rc_sb = sb("rc_t", [128, 1], F32)
    t8_sb = sb("t8_t", [128, 1], F32)
    wdt = sb("wdt", [NE, 1024], BF16)
    wsb_sb = sb("wsb_sb", [128, 1024], F32)
    ws1b = sb("ws1b", [1, 1024], BF16)

    if layer == 0:
        rwf = sb("rwf", [128, NC * 64], F32)
        rwb = sb("rwb", [128, NC * 64], BF16)
        ebf = sb("ebf", [64, D], F32)
        ebb = sb("ebb", [64, D], BF16)
        rb_sb = sb("rb_sb", [64, 1], F32)
        p.dma("sync", rwf[:].rearrange("p (d e) -> p d e", e=64),
              rw.ap().rearrange("(d p) e -> p d e", p=128), s_aux)
        p.dma("sync", ebf[:], eb64[:], s_aux)
        p.dma("sync", rb_sb[:], rbias[:], s_aux)
        p.wait("vector", s_aux, s_aux.n)
        p.op("vector", lambda e: e.tensor_copy(rwb[:], rwf[:]), a_f)
        p.op("vector", lambda e: e.tensor_copy(ebb[:], ebf[:]), a_f)
        wready = a_f.n
        wchunks = rwb
    else:
        bbf = sb("bbf", [128, 32], F32)
        wrf = sb("wrf", [128, NC * 8], F32)
        wrb = sb("wrb", [128, NC * 8], BF16)
        asf = sb("asf", [128, NC * 128], F32)
        asb = sb("asb", [128, NC * 128], BF16)
        bsf = sb("bsf", [128, D], F32)
        bsb = sb("bsb", [128, D], BF16)
        ex8_sb = sb("ex8_sb", [8, 128], BF16)
        hs_sb = sb("hs_sb", [128, 1024], BF16)
        cb_sb = sb("cb_sb", [128, 1024], F32)
        p.dma("sync", bbf[:].rearrange("p (f o) -> p f o", o=1),
              bb_in.ap().rearrange("(f p) o -> p f o", p=128), s_aux,
              allow_slow_non_contiguous=True)
        p.dma("sync", wrf[:].rearrange("p (d e) -> p d e", e=8),
              wr.ap().rearrange("(d p) e -> p d e", p=128), s_aux)
        p.dma("sync", asf[:].rearrange("p (d e) -> p d e", e=128),
              astk.ap().rearrange("(d p) e -> p d e", p=128), s_aux)
        p.dma("sync", bsf[:], bstk[:], s_aux)
        p.dma("sync", ex8_sb[:], ex8_d[:], s_aux)
        p.wait("vector", s_aux, s_aux.n)
        p.op("vector", lambda e: e.tensor_copy(wrb[:], wrf[:]), a_f)
        p.op("vector", lambda e: e.tensor_copy(asb[:], asf[:]), a_f)
        p.op("vector", lambda e: e.tensor_copy(bsb[:], bsf[:]), a_f)
        wready = a_f.n
        wchunks = wrb

    def emit_router_and_topk(n_e, kk):
        with ExitStack() as s2:
            pr = s2.enter_context(nc.psum_tensor("pr", [n_e, 1024], F32, allow_name_mangling=True))
            ptt = s2.enter_context(nc.psum_tensor("ptt", [128, n_e], F32, allow_name_mangling=True))
            ptw = s2.enter_context(nc.psum_tensor("ptw", [n_e, 128], BF16, allow_name_mangling=True))
            p.wait("tensor", a_f, wready)
            p.wait("tensor", a_c, ab2_ready)
            for dd in range(NC):
                for t in range(2):
                    p.op("tensor", lambda e, dd=dd, t=t: e.matmul(
                        pr[:, t * 512:(t + 1) * 512],
                        wchunks[:, dd * n_e:(dd + 1) * n_e],
                        ab[:, dd * 1024 + t * 512: dd * 1024 + (t + 1) * 512],
                        start=(dd == 0), stop=(dd == NC - 1)),
                        a_f if (dd == NC - 1 and t == 1) else None)
            prn = a_f.n
            p.wait("vector", a_f, prn)
            if layer == 0:
                p.op("vector", lambda e: e.tensor_scalar(
                    lgf[0:n_e, :], pr[:], rb_sb[:], None, ALU.add), a_f)
            else:
                p.op("vector", lambda e: e.tensor_copy(lgf[0:n_e, :], pr[:]), a_f)
            for tg in range(8):
                p.wait("tensor", a_f, a_f.n)
                p.op("tensor", lambda e, tg=tg: e.transpose(
                    ptt[:, 0:n_e], lgf[0:n_e, tg * 128:(tg + 1) * 128],
                    identf[0:n_e, 0:n_e]), a_f)
                p.wait("scalar", a_f, a_f.n)
                p.op("scalar", lambda e, tg=tg: e.activation(
                    tokl2[:, tg * n_e:(tg + 1) * n_e], ptt[:, 0:n_e], AF.Copy, scale=1.0), a_f)
            for tg in range(8):
                base = tokl2[:, tg * n_e:(tg + 1) * n_e]
                p.wait("vector", a_f, a_f.n)
                p.wait("vector", a_t, a_t.n)
                p.op("vector", lambda e, base=base: e.reduce_max(mx_sb[:], base, axis=AX.X), a_t)
                p.wait("vector", a_t, a_t.n)
                p.op("vector", lambda e: e.tensor_scalar(nmx_sb[:], mx_sb[:], -1.0, None, ALU.mult), a_t)
                nmn = a_t.n
                p.wait("scalar", a_t, nmn)
                p.op("scalar", lambda e, base=base: e.activation(
                    ex_sb[:], base, AF.Exp, bias=nmx_sb[:], scale=1.0), a_t)
                p.wait("vector", a_t, a_t.n)
                p.op("vector", lambda e: e.reduce_sum(sm_sb[:], ex_sb[:], axis=AX.X), a_t)
                p.wait("vector", a_t, a_t.n)
                p.op("vector", lambda e: e.reciprocal(rc_sb[:], sm_sb[:]), a_t)
                p.wait("vector", a_t, a_t.n)
                p.op("vector", lambda e: e.tensor_scalar(
                    ex_sb[:], ex_sb[:], rc_sb[:], None, ALU.mult), a_t)
                p.wait("vector", a_t, a_t.n)
                p.op("vector", lambda e: e.tensor_copy(cur_sb[:], ex_sb[:]), a_t)
                for ki in range(kk):
                    p.wait("vector", a_t, a_t.n)
                    p.op("vector", lambda e: e.reduce_max(t8_sb[:], cur_sb[:], axis=AX.X), a_t)
                    if ki < kk - 1:
                        p.wait("vector", a_t, a_t.n)
                        p.op("vector", lambda e: e.tensor_scalar(
                            w_sb[:], cur_sb[:], t8_sb[:], None, ALU.is_ge), a_t)
                        p.wait("vector", a_t, a_t.n)
                        p.op("vector", lambda e: e.scalar_tensor_tensor(
                            cur_sb[:], w_sb[:], -1e30, cur_sb[:], ALU.mult, ALU.add), a_t)
                p.wait("vector", a_t, a_t.n)
                p.op("vector", lambda e: e.tensor_scalar(
                    w_sb[:], ex_sb[:], t8_sb[:], None, ALU.is_ge), a_t)
                p.wait("vector", a_t, a_t.n)
                p.op("vector", lambda e: e.tensor_mul(w_sb[:], w_sb[:], ex_sb[:]), a_t)
                if layer == 1:
                    p.wait("vector", a_t, a_t.n)
                    p.op("vector", lambda e: e.reduce_sum(sm_sb[:], w_sb[:], axis=AX.X), a_t)
                    p.wait("vector", a_t, a_t.n)
                    p.op("vector", lambda e: e.reciprocal(rc_sb[:], sm_sb[:]), a_t)
                    p.wait("vector", a_t, a_t.n)
                    p.op("vector", lambda e: e.tensor_scalar(
                        w_sb[:], w_sb[:], rc_sb[:], None, ALU.mult), a_t)
                p.wait("vector", a_t, a_t.n)
                p.op("vector", lambda e: e.tensor_copy(wb_sb[:], w_sb[:]), a_t)
                wbn = a_t.n
                p.wait("tensor", a_t, wbn)
                p.wait("tensor", a_f, a_f.n)
                p.op("tensor", lambda e: e.transpose(
                    ptw[:, :], wb_sb[:, 0:n_e], identb[:]), a_f)
                p.wait("scalar", a_f, a_f.n)
                p.op("scalar", lambda e, tg=tg: e.activation(
                    wdt[:, tg * 128:(tg + 1) * 128], ptw[0:n_e, :], AF.Copy, scale=1.0), a_f)
            p.wait("tensor", a_f, a_f.n)

    emit_router_and_topk(NE, KK)

    if layer == 0:
        with ExitStack() as s3:
            ps_ws = [s3.enter_context(nc.psum_tensor(f"ps_ws{i}", [1, 512], F32, allow_name_mangling=True)) for i in range(2)]
            for hh in range(2):
                p.op("tensor", lambda e, hh=hh: e.matmul(
                    ps_ws[hh][:], onec[0:64, :], wdt[:, hh * 512:(hh + 1) * 512],
                    start=True, stop=True), a_f)
            wsn = a_f.n
            p.wait("vector", a_f, wsn)
            for hh in range(2):
                p.op("vector", lambda e, hh=hh: e.tensor_scalar(
                    ws1b[:, hh * 512:(hh + 1) * 512], ps_ws[hh][:], 1.0, None, ALU.add), a_f)
            p.wait("tensor", a_f, a_f.n)
            ps_wb = s3.enter_context(nc.psum_tensor("ps_wb", [128, 1024], F32, allow_name_mangling=True))
            for hh in range(2):
                p.op("tensor", lambda e, hh=hh: e.matmul(
                    ps_wb[:, hh * 512:(hh + 1) * 512], oner[:],
                    ws1b[:, hh * 512:(hh + 1) * 512], start=True, stop=True), a_f)
            p.wait("scalar", a_f, a_f.n)
            p.op("scalar", lambda e: e.activation(wsb_sb[:], ps_wb[:], AF.Copy, scale=1.0), a_f)
            p.wait("tensor", a_f, a_f.n)
        wsbn = a_f.n
        rx1b = Ring(p, "rx1b", 2)
        rxr2 = Ring(p, "rxr2", 2)
        st1.wait_all(p, "sync")
        p.wait("sync", a_r, a_r.n)
        with ExitStack() as s5s:
            stats2 = [s5s.enter_context(nc.psum_tensor(f"st2_{i}", [1, 512], F32, allow_name_mangling=True)) for i in range(4)]
            s5p = ExitStack()
            pbm = [s5p.enter_context(nc.psum_tensor(f"pbm{i}", [128, 1024], F32, allow_name_mangling=True)) for i in range(2)]
            stat_pend = []
            a_r_base = a_r.n
            a_sm_base = a_sm.n
            for ec in range(NC):
                for fn2 in stat_pend:
                    fn2()
                stat_pend = []
                if ec >= 2:
                    p.waitf("tensor", (a_r, a_r_base + ec - 1))
                else:
                    p.wait("tensor", a_f, wsbn)
                for t in range(2):
                    p.op("tensor", lambda e, ec=ec, t=t: e.matmul(
                        pbm[ec % 2][:, t * 512:(t + 1) * 512],
                        ebb[:, ec * 128:(ec + 1) * 128],
                        wdt[:, t * 512:(t + 1) * 512], start=True, stop=True),
                        a_f if t == 1 else None)
                bmn = a_f.n
                p.wait("sync", a_r, a_r_base + max(0, ec - 1))
                rx1b.dma(p, "sync", ec, xp_sb[ec % 2][:], x1_dram[ec * 128:(ec + 1) * 128, :])
                rx1b.wait(p, "vector", ec)
                p.wait("vector", a_f, bmn)
                if ec >= 2:
                    p.waitf("vector", rxr2.free(ec - 2))
                p.wait("vector", a_ln, a_ln.n)
                p.op("vector", lambda e, ec=ec: e.tensor_mul(
                    xr_sb[ec % 2][:], xp_sb[ec % 2][:], wsb_sb[:]), a_ln)
                p.wait("vector", a_ln, a_ln.n)
                p.op("vector", lambda e, ec=ec: e.tensor_add(
                    xr_sb[ec % 2][:], xr_sb[ec % 2][:], pbm[ec % 2][:]), a_r)
                rn = a_r.n
                p.wait("sync", a_r, rn)
                rxr2.dma(p, "sync", ec, xr_dram[ec * 128:(ec + 1) * 128, :], xr_sb[ec % 2][:])
                p.wait("scalar", a_r, rn)
                if ec >= 2:
                    p.wait("scalar", a_sm, a_sm_base + ec - 1)
                p.op("scalar", lambda e, ec=ec: e.activation(
                    xrb_sb[ec % 2][:], xr_sb[ec % 2][:], AF.Copy, scale=1.0), a_c)
                cn = a_c.n
                p.wait("vector", a_c, cn)
                if ec >= 2:
                    p.wait("vector", a_sm, a_sm_base + ec - 1)
                p.op("vector", lambda e, ec=ec: e.tensor_mul(
                    sq_sb[ec % 2][:], xrb_sb[ec % 2][:], xrb_sb[ec % 2][:]), a_q)
                qn = a_q.n
                def emit_def(ec=ec, cn=cn, qn=qn):
                    p.wait("tensor", a_c, cn)
                    p.wait("tensor", a_q, qn)
                    for hh in range(2):
                        p.op("tensor", lambda e, ec=ec, hh=hh: e.matmul(
                            stats2[hh][:], onec[:], xrb_sb[ec % 2][:, hh * 512:(hh + 1) * 512],
                            start=(ec == 0), stop=(ec == NC - 1), skip_group_check=True), None)
                        p.op("tensor", lambda e, ec=ec, hh=hh: e.matmul(
                            stats2[2 + hh][:], onec[:], sq_sb[ec % 2][:, hh * 512:(hh + 1) * 512],
                            start=(ec == 0), stop=(ec == NC - 1), skip_group_check=True),
                            a_sm if hh == 1 else None)
                stat_pend.append(emit_def)
            for fn2 in stat_pend:
                fn2()
            s5p.close()
            stf = ln_finalize_apply(stats2, a_sm.n, xr_dram, rxr2, xout, False, "ln2")
    else:
        with ExitStack() as s3:
            ps_h = s3.enter_context(nc.psum_tensor("ps_h", [128, 1024], F32, allow_name_mangling=True))
            ps_cb = s3.enter_context(nc.psum_tensor("ps_cb", [128, 1024], F32, allow_name_mangling=True))
            p.wait("tensor", a_f, a_f.n)
            for t in range(2):
                p.op("tensor", lambda e, t=t: e.matmul(
                    ps_cb[:, t * 512:(t + 1) * 512], ex8_sb[:],
                    wdt[:, t * 512:(t + 1) * 512], start=True, stop=True), a_f)
            cbn = a_f.n
            p.wait("scalar", a_f, cbn)
            p.op("scalar", lambda e: e.activation(cb_sb[:], ps_cb[:], AF.Copy, scale=1.0), a_f)
            p.wait("tensor", a_f, a_f.n)
            for dd in range(NC):
                for t in range(2):
                    p.op("tensor", lambda e, dd=dd, t=t: e.matmul(
                        ps_h[:, t * 512:(t + 1) * 512],
                        asb[:, dd * 128:(dd + 1) * 128],
                        ab[:, dd * 1024 + t * 512: dd * 1024 + (t + 1) * 512],
                        start=(dd == 0), stop=(dd == NC - 1)),
                        a_f if (dd == NC - 1 and t == 1) else None)
            hn = a_f.n
            p.wait("vector", a_f, hn)
            p.op("vector", lambda e: e.scalar_tensor_tensor(
                hs_sb[:], ps_h[:], 2.0, cb_sb[:], ALU.mult, ALU.mult), a_f)
            p.wait("tensor", a_f, a_f.n)
        hs_n = a_f.n
        rxp2 = Ring(p, "rxp2", 2)
        rxr2 = Ring(p, "rxr2", 2)
        p.wait("sync", a_r, a_r.n)
        with ExitStack() as s6s:
            stats2 = [s6s.enter_context(nc.psum_tensor(f"st2_{i}", [1, 512], F32, allow_name_mangling=True)) for i in range(4)]
            s6p = ExitStack()
            psq2b = [s6p.enter_context(nc.psum_tensor(f"psqb{i}", [128, 1024], F32, allow_name_mangling=True)) for i in range(2)]

            def extra1(ec, psten, s_mm):
                p.wait("tensor", a_f, hs_n)
                for t in range(2):
                    p.op("tensor", lambda e, ec=ec, t=t, psten=psten: e.matmul(
                        psten[:, t * 512:(t + 1) * 512],
                        bsb[:, ec * 128:(ec + 1) * 128],
                        hs_sb[:, t * 512:(t + 1) * 512], start=False, stop=True),
                        s_mm if t == 1 else None)

            a_r_base = a_r.n
            st1.wait_all(p, "sync")
            ev2 = mk_evac(rxp2, rxr2, stats2, a_sm.n,
                          lambda e, ec, psten: e.scalar_tensor_tensor(
                              xr_sb[ec % 2][:], psten[:], bbf[:, ec:ec + 1],
                              xp_sb[ec % 2][:], ALU.add, ALU.add),
                          x1_dram)
            def ev2g(ec, psten, defer, mm_need, s_mm):
                p.wait("sync", a_r, a_r_base + max(0, ec - 1))
                return ev2(ec, psten, defer, mm_need, s_mm)
            emit_proj(p, "wb", wbase, NC, ab, psq2b, ev2g,
                      pre_mm_waits=[(a_f, hs_n), (a_c, ab2_ready)], extra_mm_cb=extra1)
            s6p.close()
            stf = ln_finalize_apply(stats2, a_sm.n, xr_dram, rxr2, xout, False, "ln2")

    stf.wait_all(p, "gpsimd")


def build_post(layer):
    nc = mk_nc()
    attin = nc.dram_tensor("attin", [D, TL], BF16, kind="ExternalInput")
    wo = nc.dram_tensor("wo", [D, D], F32, kind="ExternalInput")
    xprev = nc.dram_tensor("xprev", [D, TL], F32, kind="ExternalInput")
    xout = nc.dram_tensor("xout", [D, TL], F32, kind="ExternalOutput")
    kw = {}
    if layer == 0:
        kw["rw"] = nc.dram_tensor("rw", [D, 64], F32, kind="ExternalInput")
        kw["rbias"] = nc.dram_tensor("rbias", [64, 1], F32, kind="ExternalInput")
        kw["eb64"] = nc.dram_tensor("eb64", [64, D], F32, kind="ExternalInput")
    else:
        kw["wbase"] = nc.dram_tensor("wbase", [D, D], F32, kind="ExternalInput")
        kw["bb_in"] = nc.dram_tensor("bb", [D, 1], F32, kind="ExternalInput")
        kw["wr"] = nc.dram_tensor("wr", [D, 8], F32, kind="ExternalInput")
        kw["astk"] = nc.dram_tensor("astk", [D, 128], F32, kind="ExternalInput")
        kw["bstk"] = nc.dram_tensor("bstk", [128, D], F32, kind="ExternalInput")
    xr_dram = nc.dram_tensor("xr_scr", [D, TL], F32)
    x1_dram = nc.dram_tensor("x1_scr", [D, TL], F32)
    with ExitStack() as ctx:
        ctx.enter_context(nc.allow_low_precision(reason="bf16 compute per stated tolerance"))
        p = P(nc, ctx)
        emit_post(nc, p, ctx, "", layer, attin, wo, xprev, xout, xr_dram, x1_dram, **kw)
        p.flush("post")
    return nc



def emit_load_xb_bf(p, tag, src_dram, xb):
    s_l = p.ctr(f"{tag}_l")
    for f in range(NC):
        p.dma("sync", xb[:, f * 1024:(f + 1) * 1024], src_dram[f * 128:(f + 1) * 128, :], s_l)
    return s_l


def owned_cols(c):
    return np.concatenate([np.arange((8 * k + c) * 128, (8 * k + c) * 128 + 128)
                           for k in range(8)])


def rope_tables():
    inv = 1.0 / (10000.0 ** (np.arange(0, 128, 2, dtype=np.float64) / 128.0))
    t = np.arange(SEQ, dtype=np.float64)
    fr = np.outer(t, inv)
    emb = np.concatenate([fr, fr], axis=-1)
    return np.cos(emb).astype(np.float32), np.sin(emb).astype(np.float32)


def build_fused():
    nc = mk_nc()
    xin = nc.dram_tensor("xin", [D, TL], F32, kind="ExternalInput")
    cos_in = nc.dram_tensor("cosin", [128, TL], F32, kind="ExternalInput")
    sin_in = nc.dram_tensor("sinin", [128, TL], F32, kind="ExternalInput")
    wq0 = nc.dram_tensor("wq0", [D, D], F32, kind="ExternalInput")
    wk0 = nc.dram_tensor("wk0", [D, D], F32, kind="ExternalInput")
    wv0 = nc.dram_tensor("wv0", [D, D], F32, kind="ExternalInput")
    wo0 = nc.dram_tensor("wo0", [D, D], F32, kind="ExternalInput")
    rw = nc.dram_tensor("rw", [D, 64], F32, kind="ExternalInput")
    rbias = nc.dram_tensor("rbias", [64, 1], F32, kind="ExternalInput")
    eb64 = nc.dram_tensor("eb64", [64, D], F32, kind="ExternalInput")
    wq1 = nc.dram_tensor("wq1", [D, D], F32, kind="ExternalInput")
    wk1 = nc.dram_tensor("wk1", [D, D], F32, kind="ExternalInput")
    wv1 = nc.dram_tensor("wv1", [D, D], F32, kind="ExternalInput")
    wo1 = nc.dram_tensor("wo1", [D, D], F32, kind="ExternalInput")
    wbase = nc.dram_tensor("wbase", [D, D], F32, kind="ExternalInput")
    bb_in = nc.dram_tensor("bb", [D, 1], F32, kind="ExternalInput")
    wr = nc.dram_tensor("wr", [D, 8], F32, kind="ExternalInput")
    astk = nc.dram_tensor("astk", [D, 128], F32, kind="ExternalInput")
    bstk = nc.dram_tensor("bstk", [128, D], F32, kind="ExternalInput")
    msk_in = nc.dram_tensor("msk", [8 * 128, 128], BF16, kind="ExternalInput")
    xout = nc.dram_tensor("xout", [D, TL], F32, kind="ExternalOutput")
    q0l = nc.dram_tensor("q0l", [D, TL], BF16)
    k0l = nc.dram_tensor("k0l", [D, TL], BF16)
    v0l = nc.dram_tensor("v0l", [TL, D], BF16)
    kag0 = nc.dram_tensor("kag0", [NCORE * D, TL], BF16, addr_space="Shared")
    vag0 = nc.dram_tensor("vag0", [NCORE * TL, D], BF16, addr_space="Shared")
    att0 = nc.dram_tensor("att0", [D, TL], BF16)
    x2d = nc.dram_tensor("x2d", [D, TL], F32)
    q1l = nc.dram_tensor("q1l", [D, TL], BF16)
    k1l = nc.dram_tensor("k1l", [D, TL], BF16)
    v1l = nc.dram_tensor("v1l", [TL, D], BF16)
    kag1 = nc.dram_tensor("kag1", [NCORE * D, TL], BF16, addr_space="Shared")
    vag1 = nc.dram_tensor("vag1", [NCORE * TL, D], BF16, addr_space="Shared")
    att1 = nc.dram_tensor("att1", [D, TL], BF16)
    xr_dram = nc.dram_tensor("xr_scr", [D, TL], F32)
    x1_dram = nc.dram_tensor("x1_scr", [D, TL], F32)

    with ExitStack() as octx:
        octx.enter_context(nc.allow_low_precision(reason="bf16 compute per stated tolerance"))
        p = P(nc, octx)

        def mk_ag_cb(pairs_map, cc):
            def cb(name, snaps):
                if name in pairs_map:
                    src_d, dst_d = pairs_map[name]
                    for (sw, vw) in snaps:
                        p.wait("gpsimd", sw, vw)
                    p.op("gpsimd", lambda e, src_d=src_d, dst_d=dst_d: e.collective_compute(
                        "AllGather", ALU.bypass, replica_groups=[list(range(NCORE))],
                        ins=[src_d[:]], outs=[dst_d[:]]), cc)
            return cb

        cc0 = p.ctr("cc0")
        with ExitStack() as c:
            p.actx = c
            emit_qkv(nc, p, c, "0", True, xin, wq0, wk0, wv0, q0l, k0l, v0l, cos_in, sin_in,
                     ag_cb=mk_ag_cb({"k": (k0l, kag0), "v": (v0l, vag0)}, cc0))
        p.flush("phA")
        nc.all_engine_barrier()
        with ExitStack() as c:
            p.actx = c
            emit_attn(nc, p, c, False, L0_SCALE, q0l, kag0, vag0, att0,
                      pre_waits=[(cc0, 2)])
        p.flush("phB")
        nc.all_engine_barrier()
        with ExitStack() as c:
            p.actx = c
            emit_post(nc, p, c, "0", 0, att0, wo0, xin, x2d, xr_dram, x1_dram,
                      rw=rw, rbias=rbias, eb64=eb64)
        p.flush("phC")
        nc.all_engine_barrier()
        cc1 = p.ctr("cc1")
        with ExitStack() as c:
            p.actx = c
            emit_qkv(nc, p, c, "1", False, x2d, wq1, wk1, wv1, q1l, k1l, v1l,
                     ag_cb=mk_ag_cb({"k": (k1l, kag1), "v": (v1l, vag1)}, cc1))
        p.flush("phD")
        nc.all_engine_barrier()
        with ExitStack() as c:
            p.actx = c
            emit_attn(nc, p, c, True, L1_SCALE, q1l, kag1, vag1, att1, msk_in,
                      pre_waits=[(cc1, 2)])
        p.flush("phE")
        nc.all_engine_barrier()
        with ExitStack() as c:
            p.actx = c
            emit_post(nc, p, c, "1", 1, att1, wo1, x2d, xout, xr_dram, x1_dram,
                      wbase=wbase, bb_in=bb_in, wr=wr, astk=astk, bstk=bstk)
        p.flush("phF")
    return nc


def kernel(**inputs):
    g = {k: np.asarray(v) for k, v in inputs.items()}
    x = np.ascontiguousarray(g["L0_attn_hidden_states"][0].T)
    wqT = np.ascontiguousarray(g["L0_attn_Wq"].T.astype(np.float32, copy=False))
    wkT = np.ascontiguousarray(g["L0_attn_Wk"].T.astype(np.float32, copy=False))
    wvT = np.ascontiguousarray(g["L0_attn_Wv"].T.astype(np.float32, copy=False))
    woT = np.ascontiguousarray(g["L0_attn_Wo"].T.astype(np.float32, copy=False))
    rwT = np.ascontiguousarray(g["L0_ffn_router_weight"].T.astype(np.float32, copy=False))
    rbias = np.ascontiguousarray(g["L0_ffn_router_bias"].reshape(64, 1).astype(np.float32, copy=False))
    eb64v = np.ascontiguousarray(g["L0_ffn_expert_bias"].astype(np.float32, copy=False))
    wq1f = g["L1_attn_wq"].astype(np.float32, copy=False)
    wq1u = np.ascontiguousarray(
        np.concatenate([wq1f[:, h * 192:h * 192 + 128] for h in range(32)], axis=1))
    wkv1 = g["L1_attn_wkv"].astype(np.float32, copy=False)
    wk1v = np.ascontiguousarray(wkv1[:, 0:4096])
    wv1v = np.ascontiguousarray(wkv1[:, 4096:8192])
    wo1v = np.ascontiguousarray(g["L1_attn_wo"].astype(np.float32, copy=False))
    wbT = np.ascontiguousarray(g["L1_ffn_W_base"].T.astype(np.float32, copy=False))
    bbv = np.ascontiguousarray(g["L1_ffn_b_base"].reshape(4096, 1).astype(np.float32, copy=False))
    wrT = np.ascontiguousarray(g["L1_ffn_W_router"].T.astype(np.float32, copy=False))
    astkv = np.ascontiguousarray(
        g["L1_ffn_A_stack"].astype(np.float32, copy=False).reshape(128, 4096).T)
    bstkv = np.ascontiguousarray(
        g["L1_ffn_B_stack"].astype(np.float32, copy=False).transpose(0, 2, 1).reshape(128, 4096))

    cols = [owned_cols(c) for c in range(NCORE)]
    cos, sin = rope_tables()
    nc = build_fused()
    in_maps = []
    for c in range(NCORE):
        in_maps.append({
            "xin": np.ascontiguousarray(x[:, cols[c]]),
            "cosin": np.ascontiguousarray(cos[cols[c]].T),
            "sinin": np.ascontiguousarray(sin[cols[c]].T),
            "wq0": wqT, "wk0": wkT, "wv0": wvT, "wo0": woT,
            "rw": rwT, "rbias": rbias, "eb64": eb64v,
            "wq1": wq1u, "wk1": wk1v, "wv1": wv1v, "wo1": wo1v,
            "wbase": wbT, "bb": bbv, "wr": wrT, "astk": astkv, "bstk": bstkv,
            "msk": build_masks(c),
        })
    res = run_bass_kernel_spmd(nc, in_maps, core_ids=list(range(NCORE))).results
    out = np.empty((D, SEQ), dtype=np.float32)
    for c in range(NCORE):
        out[:, cols[c]] = res[c]["xout"]
    return np.ascontiguousarray(out.T).reshape(1, SEQ, D)


def run_qkv_launch(rope, x_full_T, wq_de, wk_de, wv_de):
    nc = build_qkv(rope)
    in_maps = []
    if rope:
        cos, sin = rope_tables()
    for c in range(NCORE):
        cols = owned_cols(c)
        m = {"xin": np.ascontiguousarray(x_full_T[:, cols]),
             "wq": wq_de, "wk": wk_de, "wv": wv_de}
        if rope:
            m["cosin"] = np.ascontiguousarray(cos[cols].T)
            m["sinin"] = np.ascontiguousarray(sin[cols].T)
        in_maps.append(m)
    res = run_bass_kernel_spmd(nc, in_maps, core_ids=list(range(NCORE)))
    return res.results


def build_masks(c):
    m = np.zeros((8, 128, 128), dtype=np.float32)
    for r in range(8):
        if r < c:
            m[r] = 1.0
        elif r == c:
            m[r] = np.triu(np.ones((128, 128), dtype=np.float32))
    return np.ascontiguousarray(m.reshape(8 * 128, 128).astype(BF))


def run_qkv(rope, xT_cores, wq_de, wk_de, wv_de, extra=None):
    nc = build_qkv(rope)
    in_maps = []
    for c in range(NCORE):
        m = {"xin": xT_cores[c], "wq": wq_de, "wk": wk_de, "wv": wv_de}
        if rope:
            m.update(extra[c])
        in_maps.append(m)
    return run_bass_kernel_spmd(nc, in_maps, core_ids=list(range(NCORE))).results


def kernel_unfused(**inputs):
    g = {k: np.asarray(v) for k, v in inputs.items()}
    x = np.ascontiguousarray(g["L0_attn_hidden_states"][0].T)
    wqT = np.ascontiguousarray(g["L0_attn_Wq"].T.astype(np.float32, copy=False))
    wkT = np.ascontiguousarray(g["L0_attn_Wk"].T.astype(np.float32, copy=False))
    wvT = np.ascontiguousarray(g["L0_attn_Wv"].T.astype(np.float32, copy=False))
    woT = np.ascontiguousarray(g["L0_attn_Wo"].T.astype(np.float32, copy=False))
    rwT = np.ascontiguousarray(g["L0_ffn_router_weight"].T.astype(np.float32, copy=False))
    rbias = np.ascontiguousarray(g["L0_ffn_router_bias"].reshape(64, 1).astype(np.float32, copy=False))
    eb64 = np.ascontiguousarray(g["L0_ffn_expert_bias"].astype(np.float32, copy=False))
    wq1 = g["L1_attn_wq"].astype(np.float32, copy=False)
    wq1u = np.ascontiguousarray(
        np.concatenate([wq1[:, h * 192:h * 192 + 128] for h in range(32)], axis=1))
    wkv1 = g["L1_attn_wkv"].astype(np.float32, copy=False)
    wk1 = np.ascontiguousarray(wkv1[:, 0:4096])
    wv1 = np.ascontiguousarray(wkv1[:, 4096:8192])
    wo1 = np.ascontiguousarray(g["L1_attn_wo"].astype(np.float32, copy=False))
    wbT = np.ascontiguousarray(g["L1_ffn_W_base"].T.astype(np.float32, copy=False))
    bb = np.ascontiguousarray(g["L1_ffn_b_base"].reshape(4096, 1).astype(np.float32, copy=False))
    wrT = np.ascontiguousarray(g["L1_ffn_W_router"].T.astype(np.float32, copy=False))
    astk = np.ascontiguousarray(
        g["L1_ffn_A_stack"].astype(np.float32, copy=False).reshape(128, 4096).T)
    bstk = np.ascontiguousarray(
        g["L1_ffn_B_stack"].astype(np.float32, copy=False).transpose(0, 2, 1).reshape(128, 4096))

    cols = [owned_cols(c) for c in range(NCORE)]
    xT_cores = [np.ascontiguousarray(x[:, cols[c]]) for c in range(NCORE)]
    cos, sin = rope_tables()
    rope_extra = [{"cosin": np.ascontiguousarray(cos[cols[c]].T),
                   "sinin": np.ascontiguousarray(sin[cols[c]].T)} for c in range(NCORE)]

    resA = run_qkv(True, xT_cores, wqT, wkT, wvT, rope_extra)
    kag0 = np.concatenate([r["kout"] for r in resA], axis=0)
    vag0 = np.concatenate([r["vout"] for r in resA], axis=0)

    ncB = build_attn(False, L0_SCALE)
    in_maps = [{"qin": resA[c]["qout"], "kag": kag0, "vag": vag0} for c in range(NCORE)]
    resB = run_bass_kernel_spmd(ncB, in_maps, core_ids=list(range(NCORE))).results

    ncC = build_post(0)
    in_maps = [{"attin": resB[c]["att"], "wo": woT, "xprev": xT_cores[c],
                "rw": rwT, "rbias": rbias, "eb64": eb64} for c in range(NCORE)]
    resC = run_bass_kernel_spmd(ncC, in_maps, core_ids=list(range(NCORE))).results

    x2_cores = [np.ascontiguousarray(resC[c]["xout"]) for c in range(NCORE)]
    resD = run_qkv(False, x2_cores, wq1u, wk1, wv1)
    kag1 = np.concatenate([r["kout"] for r in resD], axis=0)
    vag1 = np.concatenate([r["vout"] for r in resD], axis=0)

    ncE = build_attn(True, L1_SCALE)
    in_maps = [{"qin": resD[c]["qout"], "kag": kag1, "vag": vag1,
                "msk": build_masks(c)} for c in range(NCORE)]
    resE = run_bass_kernel_spmd(ncE, in_maps, core_ids=list(range(NCORE))).results

    ncF = build_post(1)
    in_maps = [{"attin": resE[c]["att"], "wo": wo1, "xprev": x2_cores[c],
                "wbase": wbT, "bb": bb, "wr": wrT, "astk": astk, "bstk": bstk}
               for c in range(NCORE)]
    resF = run_bass_kernel_spmd(ncF, in_maps, core_ids=list(range(NCORE))).results

    out = np.empty((D, SEQ), dtype=np.float32)
    for c in range(NCORE):
        out[:, cols[c]] = resF[c]["xout"]
    return np.ascontiguousarray(out.T).reshape(1, SEQ, D)


# version 08
# speedup vs baseline: 1.2605x; 1.0305x over previous
import numpy as np
import ml_dtypes
from contextlib import ExitStack
import concourse.bass as bass
import concourse.mybir as mybir
from concourse.bass_utils import run_bass_kernel_spmd

F32 = mybir.dt.float32
BF16 = mybir.dt.bfloat16
AF = mybir.ActivationFunctionType
ALU = mybir.AluOpType
AX = mybir.AxisListType
BF = ml_dtypes.bfloat16

NCORE = 8
D = 4096
NC = 32
TL = 1024
NH = 32
SEQ = 8192
L0_SCALE = float(1.0 / np.sqrt(np.float32(128.0)))
L1_SCALE = float(np.float64(192.0) ** (-0.5))
EPS = 1e-5


class Ctr:
    def __init__(self, ctx, nc, name):
        self.h = ctx.enter_context(nc.semaphore(name))
        self.n = 0


class Ring:
    def __init__(self, p, name, n):
        self.sems = [p.ctr(f"{name}_r{i}") for i in range(n)]
        self.n = n
        self.rec = {}

    def dma(self, p, eng, i, out, in_, **kw):
        s = self.sems[i % self.n]
        p.dma(eng, out, in_, s, **kw)
        self.rec[i] = (s, s.n)

    def wait(self, p, eng, i):
        p.waitf(eng, self.rec[i])

    def free(self, i):
        return self.rec[i]

    def wait_all(self, p, eng):
        for s in self.sems:
            p.wait(eng, s, s.n)


class P:
    ENGS = ("sync", "gpsimd", "tensor", "vector", "scalar")

    def __init__(self, nc, ctx):
        self.nc = nc
        self.ctx = ctx
        self.actx = ctx
        self.s = {e: [] for e in P.ENGS}
        self._ctrs = {}

    def ctr(self, name):
        if name not in self._ctrs:
            self._ctrs[name] = Ctr(self.ctx, self.nc, name)
        return self._ctrs[name]

    def wait(self, eng, ctr, val):
        if val > 0:
            h = ctr.h
            self.s[eng].append(lambda e: e.wait_ge(h, val))

    def waitf(self, eng, free):
        if free is not None:
            self.wait(eng, free[0], free[1])

    def op(self, eng, fn, inc=None, k=1):
        if inc is None:
            self.s[eng].append(fn)
            return None
        h = inc.h
        self.s[eng].append(lambda e: fn(e).then_inc(h, k))
        inc.n += k
        return inc.n

    def dma(self, eng, out, in_, inc, **kw):
        h = inc.h
        self.s[eng].append(lambda e: e.dma_start(out=out, in_=in_, **kw).then_inc(h, 16))
        inc.n += 16
        return inc.n

    def flush(self, name):
        with self.nc.Block(name) as blk:
            for eng in P.ENGS:
                ops = self.s[eng]
                if not ops:
                    continue
                def runner(e, ops=ops):
                    for f in ops:
                        f(e)
                getattr(blk, eng)(runner)
        self.s = {e: [] for e in P.ENGS}


def mk_nc():
    return bass.Bass("TRN2", target_bir_lowering=False, num_devices=NCORE)


def emit_load_xb(p, tag, x_dram, xb):
    nc, ctx = p.nc, p.actx
    xf = [ctx.enter_context(nc.sbuf_tensor(f"{tag}_xf{i}", [128, 1024], F32, allow_name_mangling=True)) for i in range(2)]
    rl = Ring(p, f"{tag}_xl", 2)
    s_c = p.ctr(f"{tag}_xc")
    c_base = s_c.n
    for f in range(NC):
        p.wait("sync", s_c, c_base + f - 1)
        rl.dma(p, "sync", f, xf[f % 2][:], x_dram[f * 128:(f + 1) * 128, :])
        rl.wait(p, "scalar", f)
        p.op("scalar", lambda e, f=f: e.activation(
            xb[:, f * 1024:(f + 1) * 1024], xf[f % 2][:], AF.Copy, scale=1.0), s_c)
    return s_c


def emit_proj(p, tag, w_dram, n_d, xb, psq, evac_cb, pre_mm_waits=None, extra_mm_cb=None,
              psum_free=None, pf_base=0, depth=8):
    nc, ctx = p.nc, p.actx
    n_e = w_dram.shape[1] // 128
    nrot = len(psq)
    wsf = [ctx.enter_context(nc.sbuf_tensor(f"{tag}_wsf{i}", [128, 256], F32, allow_name_mangling=True)) for i in range(depth)]
    wsb = [ctx.enter_context(nc.sbuf_tensor(f"{tag}_wsb{i}", [128, 256], BF16, allow_name_mangling=True)) for i in range(depth)]
    rwd = Ring(p, f"{tag}_wd", depth)
    s_wc = p.ctr(f"{tag}_wc")
    wc_base = s_wc.n
    s_mm = p.ctr(f"{tag}_mm")
    mm_done = {}
    if psum_free is None:
        psum_free = {}
    pend = []

    def defer(fn):
        pend.append(fn)

    n_eg = n_e // 2
    for eg in range(n_eg):
        for fn in pend:
            fn()
        pend = []
        for d in range(n_d):
            idx = eg * n_d + d
            p.wait("sync", s_wc, wc_base + idx - (depth - 1))
            rwd.dma(p, "sync", idx, wsf[idx % depth][:],
                    w_dram[d * 128:(d + 1) * 128, eg * 256:(eg + 1) * 256])
            rwd.wait(p, "vector", idx)
            p.wait("vector", s_mm, mm_done.get(idx - depth, 0))
            p.op("vector", lambda e, a=wsb[idx % depth], b=wsf[idx % depth]:
                 e.tensor_copy(a[:], b[:]), s_wc)
            p.wait("tensor", s_wc, wc_base + idx + 1)
            if d == 0:
                if eg == 0 and pre_mm_waits:
                    for (ctr, val) in pre_mm_waits:
                        p.wait("tensor", ctr, val)
                for e2 in range(2):
                    ec = eg * 2 + e2
                    if pf_base + ec - nrot >= 0:
                        p.waitf("tensor", psum_free[pf_base + ec - nrot])
            for e2 in range(2):
                ec = eg * 2 + e2
                psten = psq[ec % nrot]
                for t in range(2):
                    fn = (lambda e, psten=psten, w=wsb[idx % depth], e2=e2, d=d, t=t:
                          e.matmul(psten[:, t * 512:(t + 1) * 512],
                                   w[:, e2 * 128:(e2 + 1) * 128],
                                   xb[:, d * 1024 + t * 512: d * 1024 + (t + 1) * 512],
                                   start=(d == 0),
                                   stop=(d == n_d - 1 and extra_mm_cb is None)))
                    p.op("tensor", fn, s_mm if (e2 == 1 and t == 1) else None)
            mm_done[idx] = s_mm.n
        for e2 in range(2):
            ec = eg * 2 + e2
            if extra_mm_cb is not None:
                extra_mm_cb(ec, psq[ec % nrot], s_mm)
            psum_free[pf_base + ec] = evac_cb(ec, psq[ec % nrot], defer, s_mm.n, s_mm)
    for fn in pend:
        fn()
    return psum_free, s_mm


def emit_qkv(nc, p, ctx, tag, rope, xin, wq, wk, wv, qout, kout, vout,
             cos_in=None, sin_in=None, ag_cb=None):
    if rope:
        rsign = np.zeros((128, 128), dtype=BF)
        for i in range(64):
            rsign[i + 64, i] = -1.0
            rsign[i, i + 64] = 1.0
        rsign_d = nc.inline_tensor(rsign, f"rsign{tag}")
    ident_d = nc.inline_tensor(np.eye(128, dtype=BF), f"ident{tag}")
    sb = lambda n, s, d: ctx.enter_context(nc.sbuf_tensor(n, s, d, allow_name_mangling=True))
    psa = lambda n, s, d=F32: ctx.enter_context(nc.psum_tensor(n, s, d, allow_name_mangling=True))

    xb = sb("xb", [128, NC * 1024], BF16)
    psq = [psa(f"psq{i}", [128, 1024]) for i in range(3)]
    ptr_bf = psa("ptr_bf", [128, 1024], BF16)
    prot = psa("prot", [128, 512])
    ev_sb = [sb(f"ev{i}", [128, 1024], BF16) for i in range(3)]
    ob_sb = [sb(f"ob{i}", [128, 1024], BF16) for i in range(2)]
    tmp1 = [sb(f"tmp1{i}", [128, 1024], BF16) for i in range(2)]
    tmp2 = [sb(f"tmp2{i}", [128, 1024], BF16) for i in range(2)]
    ident_sb = sb("ident_sb", [128, 128], BF16)
    if rope:
        cosb = sb("cosb", [128, 1024], BF16)
        sinb = sb("sinb", [128, 1024], BF16)
        cosf = sb("cosf", [128, 1024], F32)
        rs_sb = sb("rs_sb", [128, 128], BF16)

    s_aux = p.ctr("aux")
    s_rot = p.ctr("rot")
    s_rop = p.ctr("rope")
    rst_ob = Ring(p, "st_ob", 2)
    rst_eb = Ring(p, "st_eb", 3)
    ob_sti = [0]
    eb_sti = [0]

    aux0 = s_aux.n
    p.dma("sync", ident_sb[:], ident_d[:], s_aux)
    if rope:
        p.dma("sync", cosf[:], cos_in[:], s_aux)
        p.dma("sync", rs_sb[:], rsign_d[:], s_aux)
        p.wait("vector", s_aux, aux0 + 48)
        p.op("vector", lambda e: e.tensor_copy(cosb[:], cosf[:]), s_aux)
        p.wait("sync", s_aux, aux0 + 49)
        p.dma("sync", cosf[:], sin_in[:], s_aux)
        p.wait("vector", s_aux, aux0 + 65)
        p.op("vector", lambda e: e.tensor_copy(sinb[:], cosf[:]), s_aux)
    aux_ready = s_aux.n

    xb_ready = emit_load_xb(p, "x", xin, xb)

    G = {"n": 0}
    eb_free = {}
    ob_free = {}
    rot_free = {}
    tmp_free = {}

    s_evq = p.ctr("evq")

    def make_evac(kind, out_dram):
        def evac(ec, psten, defer, mm_need, s_mm):
            gec = G["n"]; G["n"] += 1
            eb = ev_sb[gec % 3]
            ob = ob_sb[gec % 2]
            t1, t2 = tmp1[gec % 2], tmp2[gec % 2]
            p.wait("scalar", s_mm, mm_need)
            p.waitf("scalar", eb_free.get(gec - 3))
            p.op("scalar", lambda e: e.activation(
                eb[:], psten[:], AF.Copy, scale=1.0), s_evq)
            ev_now = s_evq.n

            if kind == "plain":
                p.wait("sync", s_evq, ev_now)
                rst_eb.dma(p, "sync", gec, out_dram[ec * 128:(ec + 1) * 128, :], eb[:])
                eb_free[gec] = rst_eb.free(gec)
                return (s_evq, ev_now)

            def emit_def():
                if kind == "rope":
                    p.wait("tensor", s_evq, ev_now)
                    p.waitf("tensor", rot_free.get(gec - 1))
                    p.op("tensor", lambda e: e.matmul(
                        prot[:], rs_sb[:], eb[:, 0:512], start=True, stop=True), s_rot)
                    r1 = s_rot.n
                    p.wait("vector", s_rot, r1)
                    p.waitf("vector", tmp_free.get(gec - 2))
                    p.op("vector", lambda e: e.tensor_tensor(
                        t2[:, 0:512], prot[:], sinb[:, 0:512], ALU.mult), s_rop)
                    v1 = s_rop.n
                    p.wait("tensor", s_rop, v1)
                    p.op("tensor", lambda e: e.matmul(
                        prot[:], rs_sb[:], eb[:, 512:1024], start=True, stop=True), s_rot)
                    r2 = s_rot.n
                    rot_free[gec] = (s_rop, v1 + 1)
                    p.wait("vector", s_rot, r2)
                    p.op("vector", lambda e: e.tensor_tensor(
                        t2[:, 512:1024], prot[:], sinb[:, 512:1024], ALU.mult), s_rop)
                    p.op("vector", lambda e: e.tensor_mul(t1[:], eb[:], cosb[:]), s_rop)
                    v2 = s_rop.n
                    eb_free[gec] = (s_rop, v2)
                    p.waitf("vector", ob_free.get(gec - 2))
                    p.wait("vector", s_rop, v2)
                    p.op("vector", lambda e: e.tensor_add(ob[:], t1[:], t2[:]), s_rop)
                    tmp_free[gec] = (s_rop, s_rop.n)
                    p.wait("sync", s_rop, s_rop.n)
                    rst_ob.dma(p, "sync", gec, out_dram[ec * 128:(ec + 1) * 128, :], ob[:])
                    ob_free[gec] = rst_ob.free(gec)
                else:
                    p.wait("tensor", s_evq, ev_now)
                    p.waitf("tensor", rot_free.get(gec - 1))
                    def pe_tr(e):
                        inst = None
                        for i in range(8):
                            inst = e.transpose(ptr_bf[:, i * 128:(i + 1) * 128],
                                               eb[:, i * 128:(i + 1) * 128], ident_sb[:])
                        return inst
                    p.op("tensor", pe_tr, s_rot)
                    eb_free[gec] = (s_rot, s_rot.n)
                    p.waitf("scalar", ob_free.get(gec - 2))
                    p.wait("scalar", s_rot, s_rot.n)
                    p.op("scalar", lambda e: e.activation(
                        ob[:], ptr_bf[:], AF.Copy, scale=1.0), s_rop)
                    rot_free[gec] = (s_rop, s_rop.n)
                    p.wait("sync", s_rop, s_rop.n)
                    dst = out_dram.ap().rearrange("(b p) e -> p b e", p=128)[:, :, ec * 128:(ec + 1) * 128]
                    srcap = ob[:].rearrange("p (b d) -> p b d", d=128)
                    rst_ob.dma(p, "sync", gec, dst, srcap)
                    ob_free[gec] = rst_ob.free(gec)
            defer(emit_def)
            return (s_evq, ev_now)
        return evac

    kind_q = "rope" if rope else "plain"
    pf = {}

    def ring_snap():
        return [(s, s.n) for s in rst_ob.sems + rst_eb.sems]

    emit_proj(p, "k", wk, NC, xb, psq, make_evac(kind_q, kout),
              pre_mm_waits=[(xb_ready, xb_ready.n), (s_aux, aux_ready)],
              psum_free=pf, pf_base=0)
    if ag_cb is not None:
        ag_cb("k", ring_snap())
    emit_proj(p, "v", wv, NC, xb, psq, make_evac("v", vout),
              psum_free=pf, pf_base=32)
    if ag_cb is not None:
        ag_cb("v", ring_snap())
    emit_proj(p, "q", wq, NC, xb, psq, make_evac(kind_q, qout),
              psum_free=pf, pf_base=64)

    rst_ob.wait_all(p, "gpsimd")
    rst_eb.wait_all(p, "gpsimd")


def build_qkv(rope):
    nc = mk_nc()
    xin = nc.dram_tensor("xin", [D, TL], F32, kind="ExternalInput")
    wq = nc.dram_tensor("wq", [D, D], F32, kind="ExternalInput")
    wk = nc.dram_tensor("wk", [D, D], F32, kind="ExternalInput")
    wv = nc.dram_tensor("wv", [D, D], F32, kind="ExternalInput")
    qout = nc.dram_tensor("qout", [D, TL], BF16, kind="ExternalOutput")
    kout = nc.dram_tensor("kout", [D, TL], BF16, kind="ExternalOutput")
    vout = nc.dram_tensor("vout", [TL, D], BF16, kind="ExternalOutput")
    cos_in = sin_in = None
    if rope:
        cos_in = nc.dram_tensor("cosin", [128, TL], F32, kind="ExternalInput")
        sin_in = nc.dram_tensor("sinin", [128, TL], F32, kind="ExternalInput")
    with ExitStack() as ctx:
        ctx.enter_context(nc.allow_low_precision(reason="bf16 compute per stated tolerance"))
        p = P(nc, ctx)
        emit_qkv(nc, p, ctx, "", rope, xin, wq, wk, wv, qout, kout, vout, cos_in, sin_in)
        p.flush("qkv")
    return nc



def emit_attn(nc, p, ctx, causal, scale, qin, kag, vag, att, msk_in=None, pre_waits=None):
    sb = lambda n, s, d: ctx.enter_context(nc.sbuf_tensor(n, s, d, allow_name_mangling=True))
    psa = lambda n, s, d=F32: ctx.enter_context(nc.psum_tensor(n, s, d, allow_name_mangling=True))

    ksb = [sb(f"ksb{i}", [128, 64 * 128], BF16) for i in range(2)]
    vsb = [sb(f"vsb{i}", [128, 64 * 128], BF16) for i in range(2)]
    qsb = [sb(f"qsb{i}", [128, TL], BF16) for i in range(2)]
    esb = [sb(f"esb{i}", [128, 1024], BF16) for i in range(3)]
    dsb2 = [[sb(f"dsb{s}_{i}", [128, 1024], BF16) for i in range(2)] for s in range(2)]
    rbf = sb("rbf", [1, 1024], BF16)
    bcs2 = [sb(f"bcs{i}", [128, 512], F32) for i in range(2)]
    att_sb = [sb(f"attsb{i}", [128, TL], BF16) for i in range(2)]
    onec = sb("onec", [128, 1], BF16)
    oner = sb("oner", [1, 128], BF16)
    if causal:
        msk_sb = sb("msk_sb", [128, 8 * 128], BF16)

    pst = [psa(f"pst{i}", [128, 1024]) for i in range(2)]
    ps_pv = [psa(f"ps_pv{i}", [128, 512]) for i in range(2)]
    ps_den = psa("ps_den", [1, 512])
    ps_bc = psa("ps_bc", [128, 512])

    rkv = [p.ctr("rkv0"), p.ctr("rkv1")]
    rst = Ring(p, "rst", 2)
    a_s = p.ctr("a_s")
    a_x = p.ctr("a_x")
    a_e = p.ctr("a_e")
    a_p = p.ctr("a_p")
    a_v = p.ctr("a_v")
    a_x2 = p.ctr("a_x2")
    s_aux = p.ctr("aux")

    p.op("vector", lambda e: e.memset(onec[:], 1.0), s_aux)
    p.op("vector", lambda e: e.memset(oner[:], 1.0), s_aux)
    if causal:
        p.dma("sync", msk_sb[:].rearrange("p (r q) -> p r q", q=128),
              msk_in.ap().rearrange("(r p) q -> p r q", p=128), s_aux)
        p.wait("vector", s_aux, s_aux.n)

    kv_ready = {}

    def loads(h):
        hb = h % 2
        s = rkv[hb]
        for k8 in range(8):
            p.dma("sync", ksb[hb][:, k8 * 1024:(k8 + 1) * 1024].rearrange("p (c t) -> p c t", t=128),
                  kag.ap().rearrange("(c x) t -> x c t", c=8)[h * 128:(h + 1) * 128, :, k8 * 128:(k8 + 1) * 128], s)
            p.dma("sync", vsb[hb][:, k8 * 1024:(k8 + 1) * 1024].rearrange("p (c t) -> p c t", t=128),
                  vag.ap().rearrange("(c y) e -> y c e", c=8)[k8 * 128:(k8 + 1) * 128, :, h * 128:(h + 1) * 128], s)
        p.dma("sync", qsb[hb][:], qin[h * 128:(h + 1) * 128, :], s)
        kv_ready[h] = s.n

    gj0 = [None]
    scount = {}
    xcount = {}
    ecount = {}
    pcount = {}
    div_done = {}
    xcount_r = {}
    pv_ok = {}
    dme_done = {}

    if pre_waits:
        for (cw, vw) in pre_waits:
            p.wait("sync", cw, vw)
    loads(0)
    loads(1)

    nB = 64
    npair = 32 if causal else 64
    pend_div = []

    for h in range(NH):
        hb = h % 2
        p.wait("tensor", rkv[hb], kv_ready[h])

        dsb = dsb2[h % 2]
        p.waitf("vector", div_done.get(h - 2))
        p.op("vector", lambda e, dsb=dsb: e.memset(dsb[0][:], 0.0), a_v)
        p.op("vector", lambda e, dsb=dsb: e.memset(dsb[1][:], 0.0), a_v)
        dmn = a_v.n

        if gj0[0] is None:
            gj0[0] = 0
        gj_head = gj0[0]

        def colA(j):
            return (j // 8) * 128 if causal else 0

        def colB(j):
            if causal and j >= 32:
                return (j // 8 - 4) * 128
            return 0

        for j in range(nB):
            if j <= 2 and pend_div:
                pend_div.pop(0)()
            gj = gj0[0]; gj0[0] += 1
            paired = j < npair
            c0a = colA(j)
            c0b = colB(j)
            lo = c0a if paired else 512 + c0b
            p.waitf("tensor", xcount_r.get(gj - 2))
            if paired:
                p.op("tensor", lambda e, j=j, gj=gj, c0a=c0a, hb=hb: e.matmul(
                    pst[gj % 2][:, c0a:512],
                    ksb[hb][:, j * 128:(j + 1) * 128],
                    qsb[hb][:, c0a:512], start=True, stop=True), None)
            p.op("tensor", lambda e, j=j, gj=gj, c0b=c0b, hb=hb: e.matmul(
                pst[gj % 2][:, 512 + c0b:1024],
                ksb[hb][:, j * 128:(j + 1) * 128],
                qsb[hb][:, 512 + c0b:1024], start=True, stop=True), a_s)
            scount[gj] = a_s.n
            if j >= 2:
                jj = j - 2
                gjj = gj - 2
                p.waitf("tensor", pv_ok.get(gjj))
                if jj == 0:
                    p.waitf("tensor", div_done.get(h - 1))
                if jj < npair:
                    p.op("tensor", lambda e, jj=jj, gjj=gjj, hb=hb, c0=colA(jj): e.matmul(
                        ps_pv[0][:, c0:512],
                        vsb[hb][:, jj * 128:(jj + 1) * 128],
                        esb[gjj % 3][:, c0:512],
                        start=(jj == 0), stop=(jj == npair - 1)), None)
                p.op("tensor", lambda e, jj=jj, gjj=gjj, hb=hb, c0=colB(jj): e.matmul(
                    ps_pv[1][:, c0:512],
                    vsb[hb][:, jj * 128:(jj + 1) * 128],
                    esb[gjj % 3][:, 512 + c0:1024],
                    start=(jj == 0), stop=(jj == nB - 1)), a_p)
                pcount[gjj] = (a_p, a_p.n)
            p.wait("scalar", a_s, scount[gj])
            p.waitf("scalar", pcount.get(gj - 3))
            p.waitf("scalar", ecount.get(gj - 3))
            p.op("scalar", lambda e, gj=gj, lo=lo: e.activation(
                esb[gj % 3][:, lo:1024], pst[gj % 2][:, lo:1024], AF.Exp,
                scale=scale), a_x)
            xcount[gj] = a_x.n
            xcount_r[gj] = (a_x, a_x.n)
            p.wait("vector", a_x, xcount[gj])
            if j <= 1:
                p.wait("vector", a_v, dmn)
            else:
                p.waitf("vector", ecount.get(gj - 2))
            mr = j % 8
            if causal and paired:
                p.op("vector", lambda e, gj=gj, c0a=c0a, mr=mr: e.tensor_tensor(
                    esb[gj % 3][:, c0a:c0a + 128], esb[gj % 3][:, c0a:c0a + 128],
                    msk_sb[:, mr * 128:(mr + 1) * 128], ALU.mult), a_v)
                p.wait("vector", a_v, a_v.n)
            if causal and j >= 32:
                p.op("vector", lambda e, gj=gj, c0b=c0b, mr=mr: e.tensor_tensor(
                    esb[gj % 3][:, 512 + c0b:512 + c0b + 128], esb[gj % 3][:, 512 + c0b:512 + c0b + 128],
                    msk_sb[:, mr * 128:(mr + 1) * 128], ALU.mult), a_v)
                p.wait("vector", a_v, a_v.n)
            if causal and (paired or j >= 32):
                pv_ok[gj] = (a_v, a_v.n)
            else:
                pv_ok[gj] = (a_x, xcount[gj])
            p.op("vector", lambda e, gj=gj, lo=lo, dsb=dsb: e.tensor_tensor(
                dsb[gj % 2][:, lo:1024], dsb[gj % 2][:, lo:1024],
                esb[gj % 3][:, lo:1024], ALU.add), a_e)
            ecount[gj] = (a_e, a_e.n)

        for jj in (nB - 2, nB - 1):
            gjj = gj_head + jj
            p.waitf("tensor", pv_ok.get(gjj))
            if jj < npair:
                p.op("tensor", lambda e, jj=jj, gjj=gjj, hb=hb, c0=colA(jj): e.matmul(
                    ps_pv[0][:, c0:512],
                    vsb[hb][:, jj * 128:(jj + 1) * 128],
                    esb[gjj % 3][:, c0:512],
                    start=(jj == 0), stop=(jj == npair - 1)), None)
            p.op("tensor", lambda e, jj=jj, gjj=gjj, hb=hb, c0=colB(jj): e.matmul(
                ps_pv[1][:, c0:512],
                vsb[hb][:, jj * 128:(jj + 1) * 128],
                esb[gjj % 3][:, 512 + c0:1024],
                start=(jj == 0), stop=(jj == nB - 1)), a_p)
            pcount[gjj] = (a_p, a_p.n)
        hp_end = a_p.n

        p.waitf("vector", ecount[gj0[0] - 1])
        p.op("vector", lambda e, dsb=dsb: e.tensor_add(dsb[0][:], dsb[0][:], dsb[1][:]), a_v)
        castn = a_v.n

        def make_div(h=h, hb=hb, hp_end=hp_end, castn=castn, dsb=dsb):
            st = {}
            def stage1():
                p.wait("tensor", a_v, castn)
                if h >= 1:
                    p.waitf("tensor", div_done.get(h - 1))
                for off in (0, 512):
                    p.op("tensor", lambda e, off=off, dsb=dsb: e.matmul(
                        ps_den[:], onec[:], dsb[0][:, off:off + 512], start=True, stop=True), a_s)
                    on_n = a_s.n
                    p.wait("vector", a_s, on_n)
                    p.op("vector", lambda e, off=off: e.reciprocal(rbf[:, off:off + 512], ps_den[:]), a_v)
                    if off == 0:
                        p.wait("tensor", a_v, a_v.n)
                st["recip"] = a_v.n
            def stage2():
                p.wait("tensor", a_v, st["recip"])
                for i2, off in enumerate((0, 512)):
                    if i2 == 1:
                        p.wait("tensor", a_x2, a_x2.n)
                    p.op("tensor", lambda e, off=off: e.matmul(
                        ps_bc[:], oner[:], rbf[:, off:off + 512], start=True, stop=True), a_s)
                    bc_n = a_s.n
                    p.wait("scalar", a_s, bc_n)
                    p.op("scalar", lambda e, i2=i2: e.activation(
                        bcs2[i2][:], ps_bc[:], AF.Copy, scale=1.0), a_x2)
                st["bce"] = a_x2.n
            def stage3():
                p.wait("vector", a_x2, st["bce"])
                p.wait("vector", a_p, hp_end)
                p.waitf("vector", rst.free(h - 2) if h >= 2 else None)
                for i2, (pvp, off) in enumerate(((0, 0), (1, 512))):
                    p.op("vector", lambda e, pvp=pvp, off=off, hb=hb, i2=i2: e.tensor_tensor(
                        att_sb[hb][:, off:off + 512], ps_pv[pvp][:], bcs2[i2][:], ALU.mult), a_v)
                div_done[h] = (a_v, a_v.n)
                p.wait("sync", a_v, div_done[h][1])
                rst.dma(p, "sync", h, att[h * 128:(h + 1) * 128, :], att_sb[hb][:])
                if h + 2 < NH:
                    p.wait("sync", a_p, hp_end)
                    loads(h + 2)
            return [stage1, stage2, stage3]
        pend_div = make_div()
        if h == NH - 1:
            for fn in pend_div:
                fn()
            pend_div = []

    rst.wait_all(p, "gpsimd")


def build_attn(causal, scale):
    nc = mk_nc()
    qin = nc.dram_tensor("qin", [D, TL], BF16, kind="ExternalInput")
    kag = nc.dram_tensor("kag", [NCORE * D, TL], BF16, kind="ExternalInput")
    vag = nc.dram_tensor("vag", [NCORE * TL, D], BF16, kind="ExternalInput")
    att = nc.dram_tensor("att", [D, TL], BF16, kind="ExternalOutput")
    msk_in = None
    if causal:
        msk_in = nc.dram_tensor("msk", [8 * 128, 128], BF16, kind="ExternalInput")
    with ExitStack() as ctx:
        ctx.enter_context(nc.allow_low_precision(reason="bf16 compute per stated tolerance"))
        p = P(nc, ctx)
        emit_attn(nc, p, ctx, causal, scale, qin, kag, vag, att, msk_in)
        p.flush("attn")
    return nc



def emit_post(nc, p, ctx, tag, layer, attin, wo, xprev, xout, xr_dram, x1_dram,
              rw=None, rbias=None, eb64=None,
              wbase=None, bb_in=None, wr=None, astk=None, bstk=None):
    if layer == 0:
        NE = 64
        KK = 8
    else:
        ex8 = np.zeros((8, 128), dtype=BF)
        for e8 in range(8):
            ex8[e8, e8 * 16:(e8 + 1) * 16] = 1.0
        ex8_d = nc.inline_tensor(ex8, f"ex8{tag}")
        NE = 8
        KK = 2
    identb_d = nc.inline_tensor(np.eye(128, dtype=BF), f"identb{tag}")
    identf_d = nc.inline_tensor(np.eye(128, dtype=np.float32), f"identf{tag}")
    sb = lambda n, s, d: ctx.enter_context(nc.sbuf_tensor(n, s, d, allow_name_mangling=True))

    ab = sb("ab", [128, NC * 1024], BF16)
    xr_sb = [sb(f"xr{i}", [128, 1024], F32) for i in range(2)]
    xp_sb = [sb(f"xp{i}", [128, 1024], F32) for i in range(2)]
    xrb_sb = [sb(f"xrb{i}", [128, 1024], BF16) for i in range(2)]
    sq_sb = [sb(f"sq{i}", [128, 1024], BF16) for i in range(2)]
    x1c_sb = [sb(f"x1c{i}", [128, 1024], F32) for i in range(2)]
    onec = sb("onec", [128, 1], BF16)
    oner = sb("oner", [1, 128], BF16)
    epsb = sb("epsb", [1, 1], F32)
    identb = sb("identb_sb", [128, 128], BF16)
    identf = sb("identf_sb", [128, 128], F32)
    m_sb = sb("m_sb", [1, 1024], F32)
    v_sb = sb("v_sb", [1, 1024], F32)
    t1_sb = sb("t1_sb", [1, 1024], F32)
    rstd_sb = sb("rstd_sb", [1, 1024], F32)
    rstdb = sb("rstdb", [1, 1024], BF16)
    mrb = sb("mrb", [1, 1024], BF16)

    s_aux = p.ctr("aux")
    a_r = p.ctr("a_r")
    a_c = p.ctr("a_c")
    a_q = p.ctr("a_q")
    a_sm = p.ctr("a_sm")
    a_ln = p.ctr("a_ln")
    a_f = p.ctr("a_f")
    a_t = p.ctr("a_t")

    p.dma("sync", identb[:], identb_d[:], s_aux)
    p.dma("sync", identf[:], identf_d[:], s_aux)
    p.op("vector", lambda e: e.memset(onec[:], 1.0), s_aux)
    p.op("vector", lambda e: e.memset(oner[:], 1.0), s_aux)
    p.op("vector", lambda e: e.memset(epsb[:], EPS), s_aux)
    aux_n = s_aux.n

    ab_ready = emit_load_xb_bf(p, "ab", attin, ab)

    def ln_finalize_apply(stats, stats_total, src_dram, src_ring, dst_dram, cast_to_ab, tagn):
        ld = Ring(p, f"{tagn}_ld", 2)
        st = Ring(p, f"{tagn}_st", 2)
        src_ring.wait_all(p, "sync")
        for eng in ("sync", "vector", "scalar"):
            p.wait(eng, a_sm, stats_total)
            p.wait(eng, a_r, a_r.n)
            p.wait(eng, a_c, a_c.n)
            p.wait(eng, a_q, a_q.n)
        for hh in range(2):
            p.op("vector", lambda e, hh=hh: e.tensor_scalar(
                m_sb[:, hh * 512:(hh + 1) * 512], stats[hh][:], 1.0 / D, None, ALU.mult), a_ln)
            p.op("vector", lambda e, hh=hh: e.tensor_scalar(
                v_sb[:, hh * 512:(hh + 1) * 512], stats[2 + hh][:], 1.0 / D, None, ALU.mult), a_ln)
        p.wait("vector", a_ln, a_ln.n)
        p.op("vector", lambda e: e.tensor_mul(t1_sb[:], m_sb[:], m_sb[:]), a_ln)
        p.wait("vector", a_ln, a_ln.n)
        p.op("vector", lambda e: e.tensor_sub(v_sb[:], v_sb[:], t1_sb[:]), a_ln)
        vn = a_ln.n
        p.wait("scalar", a_ln, vn)
        p.op("scalar", lambda e: e.activation(t1_sb[:], v_sb[:], AF.Ln, bias=epsb[:], scale=1.0), a_ln)
        p.wait("scalar", a_ln, a_ln.n)
        p.op("scalar", lambda e: e.activation(rstd_sb[:], t1_sb[:], AF.Exp, scale=-0.5), a_ln)
        rn = a_ln.n
        p.wait("vector", a_ln, rn)
        p.op("vector", lambda e: e.tensor_mul(t1_sb[:], m_sb[:], rstd_sb[:]), a_ln)
        p.op("vector", lambda e: e.tensor_copy(rstdb[:], rstd_sb[:]), a_ln)
        p.wait("vector", a_ln, a_ln.n)
        p.op("vector", lambda e: e.tensor_copy(mrb[:], t1_sb[:]), a_ln)
        castn = a_ln.n
        with nc.psum_tensor(f"{tagn}_rsb", [128, 1024], F32, allow_name_mangling=True) as rs_bc, \
             nc.psum_tensor(f"{tagn}_mrb", [128, 1024], F32, allow_name_mangling=True) as mr_bc:
            p.wait("tensor", a_ln, castn)
            for hh in range(2):
                p.op("tensor", lambda e, hh=hh: e.matmul(
                    rs_bc[:, hh * 512:(hh + 1) * 512], oner[:],
                    rstdb[:, hh * 512:(hh + 1) * 512], start=True, stop=True), a_ln)
                p.op("tensor", lambda e, hh=hh: e.matmul(
                    mr_bc[:, hh * 512:(hh + 1) * 512], oner[:],
                    mrb[:, hh * 512:(hh + 1) * 512], start=True, stop=True), a_ln)
            bcn = a_ln.n
            apply_done = {}
            cast_done = {}
            for f in range(NC):
                p.waitf("sync", apply_done.get(f - 2))
                ld.dma(p, "sync", f, xr_sb[f % 2][:], src_dram[f * 128:(f + 1) * 128, :])
                ld.wait(p, "vector", f)
                p.wait("vector", a_ln, bcn)
                if f >= 2:
                    p.waitf("vector", apply_done.get(f - 2))
                    if cast_to_ab:
                        p.wait("vector", a_c, cast_done[f - 2])
                p.op("vector", lambda e, f=f: e.tensor_mul(x1c_sb[f % 2][:], xr_sb[f % 2][:], rs_bc[:]), a_ln)
                p.wait("vector", a_ln, a_ln.n)
                p.op("vector", lambda e, f=f: e.tensor_sub(x1c_sb[f % 2][:], x1c_sb[f % 2][:], mr_bc[:]), a_ln)
                here = a_ln.n
                if cast_to_ab:
                    p.wait("scalar", a_ln, here)
                    p.op("scalar", lambda e, f=f: e.activation(
                        ab[:, f * 1024:(f + 1) * 1024], x1c_sb[f % 2][:], AF.Copy, scale=1.0), a_c)
                    cast_done[f] = a_c.n
                p.wait("sync", a_ln, here)
                st.dma(p, "sync", f, dst_dram[f * 128:(f + 1) * 128, :], x1c_sb[f % 2][:])
                apply_done[f] = st.free(f)
            st.wait_all(p, "tensor")
        return st

    rxp1 = Ring(p, "rxp1", 2)
    rxr1 = Ring(p, "rxr1", 2)
    stats_pend = []

    def mk_evac(rxp, rxr, stats, a_sm_base, xr_src, resid_dram):
        def evac(ec, psten, defer, mm_need, s_mm):
            rxp.dma(p, "sync", ec, xp_sb[ec % 2][:], resid_dram[ec * 128:(ec + 1) * 128, :])
            p.wait("vector", s_mm, mm_need)
            rxp.wait(p, "vector", ec)
            if ec >= 2:
                p.waitf("vector", rxr.free(ec - 2))
            p.op("vector", lambda e, ec=ec, psten=psten: xr_src(e, ec, psten), a_r)
            rn = a_r.n
            p.wait("sync", a_r, rn)
            rxr.dma(p, "sync", ec, xr_dram[ec * 128:(ec + 1) * 128, :], xr_sb[ec % 2][:])
            p.wait("scalar", a_r, rn)
            if ec >= 2:
                p.wait("scalar", a_sm, a_sm_base + ec - 1)
            p.op("scalar", lambda e, ec=ec: e.activation(
                xrb_sb[ec % 2][:], xr_sb[ec % 2][:], AF.Copy, scale=1.0), a_c)
            cn = a_c.n
            p.wait("vector", a_c, cn)
            if ec >= 2:
                p.wait("vector", a_sm, a_sm_base + ec - 1)
            p.op("vector", lambda e, ec=ec: e.tensor_mul(
                sq_sb[ec % 2][:], xrb_sb[ec % 2][:], xrb_sb[ec % 2][:]), a_q)
            qn = a_q.n
            def emit_def(ec=ec, cn=cn, qn=qn):
                p.wait("tensor", a_c, cn)
                p.wait("tensor", a_q, qn)
                if ec == 0:
                    p.wait("tensor", s_aux, aux_n)
                for hh in range(2):
                    p.op("tensor", lambda e, ec=ec, hh=hh: e.matmul(
                        stats[hh][:], onec[:], xrb_sb[ec % 2][:, hh * 512:(hh + 1) * 512],
                        start=(ec == 0), stop=(ec == NC - 1), skip_group_check=True), None)
                    p.op("tensor", lambda e, ec=ec, hh=hh: e.matmul(
                        stats[2 + hh][:], onec[:], sq_sb[ec % 2][:, hh * 512:(hh + 1) * 512],
                        start=(ec == 0), stop=(ec == NC - 1), skip_group_check=True),
                        a_sm if hh == 1 else None)
            defer(emit_def)
            return (a_r, rn)
        return evac

    with ExitStack() as s1s:
        stats = [s1s.enter_context(nc.psum_tensor(f"st{i}", [1, 512], F32, allow_name_mangling=True)) for i in range(4)]
        a_r_base1 = a_r.n
        with ExitStack() as s1p:
            psq2 = [s1p.enter_context(nc.psum_tensor(f"psq{i}", [128, 1024], F32, allow_name_mangling=True)) for i in range(2)]
            ev1 = mk_evac(rxp1, rxr1, stats, 0,
                          lambda e, ec, psten: e.tensor_add(xr_sb[ec % 2][:], psten[:], xp_sb[ec % 2][:]),
                          xprev)
            def ev1g(ec, psten, defer, mm_need, s_mm):
                p.wait("sync", a_r, a_r_base1 + max(0, ec - 1))
                return ev1(ec, psten, defer, mm_need, s_mm)
            emit_proj(p, "wo", wo, NC, ab, psq2, ev1g,
                      pre_mm_waits=[(ab_ready, ab_ready.n)])
        st1 = ln_finalize_apply(stats, a_sm.n, xr_dram, rxr1, x1_dram, True, "ln1")
    ab2_ready = a_c.n

    lgf = sb("lgf", [64, 1024], F32)
    tokl2 = sb("tokl2", [128, 8 * NE], F32)
    ex_sb = sb("ex_t", [128, NE], F32)
    cur_sb = sb("cur_t", [128, NE], F32)
    w_sb = sb("w_t", [128, NE], F32)
    wb_sb = sb("wb_t", [128, NE], BF16)
    mx_sb = sb("mx_t", [128, 1], F32)
    nmx_sb = sb("nmx_t", [128, 1], F32)
    sm_sb = sb("sm_t", [128, 1], F32)
    rc_sb = sb("rc_t", [128, 1], F32)
    t8_sb = sb("t8_t", [128, 1], F32)
    wdt = sb("wdt", [NE, 1024], BF16)
    wsb_sb = sb("wsb_sb", [128, 1024], F32)
    ws1b = sb("ws1b", [1, 1024], BF16)

    if layer == 0:
        rwf = sb("rwf", [128, NC * 64], F32)
        rwb = sb("rwb", [128, NC * 64], BF16)
        ebf = sb("ebf", [64, D], F32)
        ebb = sb("ebb", [64, D], BF16)
        rb_sb = sb("rb_sb", [64, 1], F32)
        p.dma("sync", rwf[:].rearrange("p (d e) -> p d e", e=64),
              rw.ap().rearrange("(d p) e -> p d e", p=128), s_aux)
        p.dma("sync", ebf[:], eb64[:], s_aux)
        p.dma("sync", rb_sb[:], rbias[:], s_aux)
        p.wait("vector", s_aux, s_aux.n)
        p.op("vector", lambda e: e.tensor_copy(rwb[:], rwf[:]), a_f)
        p.op("vector", lambda e: e.tensor_copy(ebb[:], ebf[:]), a_f)
        wready = a_f.n
        wchunks = rwb
    else:
        bbf = sb("bbf", [128, 32], F32)
        wrf = sb("wrf", [128, NC * 8], F32)
        wrb = sb("wrb", [128, NC * 8], BF16)
        asf = sb("asf", [128, NC * 128], F32)
        asb = sb("asb", [128, NC * 128], BF16)
        bsf = sb("bsf", [128, D], F32)
        bsb = sb("bsb", [128, D], BF16)
        ex8_sb = sb("ex8_sb", [8, 128], BF16)
        hs_sb = sb("hs_sb", [128, 1024], BF16)
        cb_sb = sb("cb_sb", [128, 1024], F32)
        p.dma("sync", bbf[:].rearrange("p (f o) -> p f o", o=1),
              bb_in.ap().rearrange("(f p) o -> p f o", p=128), s_aux,
              allow_slow_non_contiguous=True)
        p.dma("sync", wrf[:].rearrange("p (d e) -> p d e", e=8),
              wr.ap().rearrange("(d p) e -> p d e", p=128), s_aux)
        p.dma("sync", asf[:].rearrange("p (d e) -> p d e", e=128),
              astk.ap().rearrange("(d p) e -> p d e", p=128), s_aux)
        p.dma("sync", bsf[:], bstk[:], s_aux)
        p.dma("sync", ex8_sb[:], ex8_d[:], s_aux)
        p.wait("vector", s_aux, s_aux.n)
        p.op("vector", lambda e: e.tensor_copy(wrb[:], wrf[:]), a_f)
        p.op("vector", lambda e: e.tensor_copy(asb[:], asf[:]), a_f)
        p.op("vector", lambda e: e.tensor_copy(bsb[:], bsf[:]), a_f)
        wready = a_f.n
        wchunks = wrb

    def emit_router_and_topk(n_e, kk):
        with ExitStack() as s2:
            pr = s2.enter_context(nc.psum_tensor("pr", [n_e, 1024], F32, allow_name_mangling=True))
            ptt = s2.enter_context(nc.psum_tensor("ptt", [128, n_e], F32, allow_name_mangling=True))
            ptw = s2.enter_context(nc.psum_tensor("ptw", [n_e, 128], BF16, allow_name_mangling=True))
            p.wait("tensor", a_f, wready)
            p.wait("tensor", a_c, ab2_ready)
            for dd in range(NC):
                for t in range(2):
                    p.op("tensor", lambda e, dd=dd, t=t: e.matmul(
                        pr[:, t * 512:(t + 1) * 512],
                        wchunks[:, dd * n_e:(dd + 1) * n_e],
                        ab[:, dd * 1024 + t * 512: dd * 1024 + (t + 1) * 512],
                        start=(dd == 0), stop=(dd == NC - 1)),
                        a_f if (dd == NC - 1 and t == 1) else None)
            prn = a_f.n
            p.wait("vector", a_f, prn)
            if layer == 0:
                p.op("vector", lambda e: e.tensor_scalar(
                    lgf[0:n_e, :], pr[:], rb_sb[:], None, ALU.add), a_f)
            else:
                p.op("vector", lambda e: e.tensor_copy(lgf[0:n_e, :], pr[:]), a_f)
            for tg in range(8):
                p.wait("tensor", a_f, a_f.n)
                p.op("tensor", lambda e, tg=tg: e.transpose(
                    ptt[:, 0:n_e], lgf[0:n_e, tg * 128:(tg + 1) * 128],
                    identf[0:n_e, 0:n_e]), a_f)
                p.wait("scalar", a_f, a_f.n)
                p.op("scalar", lambda e, tg=tg: e.activation(
                    tokl2[:, tg * n_e:(tg + 1) * n_e], ptt[:, 0:n_e], AF.Copy, scale=1.0), a_f)
            for tg in range(8):
                base = tokl2[:, tg * n_e:(tg + 1) * n_e]
                p.wait("vector", a_f, a_f.n)
                p.wait("vector", a_t, a_t.n)
                p.op("vector", lambda e, base=base: e.reduce_max(mx_sb[:], base, axis=AX.X), a_t)
                p.wait("vector", a_t, a_t.n)
                p.op("vector", lambda e: e.tensor_scalar(nmx_sb[:], mx_sb[:], -1.0, None, ALU.mult), a_t)
                nmn = a_t.n
                p.wait("scalar", a_t, nmn)
                p.op("scalar", lambda e, base=base: e.activation(
                    ex_sb[:], base, AF.Exp, bias=nmx_sb[:], scale=1.0), a_t)
                p.wait("vector", a_t, a_t.n)
                p.op("vector", lambda e: e.reduce_sum(sm_sb[:], ex_sb[:], axis=AX.X), a_t)
                p.wait("vector", a_t, a_t.n)
                p.op("vector", lambda e: e.reciprocal(rc_sb[:], sm_sb[:]), a_t)
                p.wait("vector", a_t, a_t.n)
                p.op("vector", lambda e: e.tensor_scalar(
                    ex_sb[:], ex_sb[:], rc_sb[:], None, ALU.mult), a_t)
                p.wait("vector", a_t, a_t.n)
                p.op("vector", lambda e: e.tensor_copy(cur_sb[:], ex_sb[:]), a_t)
                for ki in range(kk):
                    p.wait("vector", a_t, a_t.n)
                    p.op("vector", lambda e: e.reduce_max(t8_sb[:], cur_sb[:], axis=AX.X), a_t)
                    if ki < kk - 1:
                        p.wait("vector", a_t, a_t.n)
                        p.op("vector", lambda e: e.tensor_scalar(
                            w_sb[:], cur_sb[:], t8_sb[:], None, ALU.is_ge), a_t)
                        p.wait("vector", a_t, a_t.n)
                        p.op("vector", lambda e: e.scalar_tensor_tensor(
                            cur_sb[:], w_sb[:], -1e30, cur_sb[:], ALU.mult, ALU.add), a_t)
                p.wait("vector", a_t, a_t.n)
                p.op("vector", lambda e: e.tensor_scalar(
                    w_sb[:], ex_sb[:], t8_sb[:], None, ALU.is_ge), a_t)
                p.wait("vector", a_t, a_t.n)
                p.op("vector", lambda e: e.tensor_mul(w_sb[:], w_sb[:], ex_sb[:]), a_t)
                if layer == 1:
                    p.wait("vector", a_t, a_t.n)
                    p.op("vector", lambda e: e.reduce_sum(sm_sb[:], w_sb[:], axis=AX.X), a_t)
                    p.wait("vector", a_t, a_t.n)
                    p.op("vector", lambda e: e.reciprocal(rc_sb[:], sm_sb[:]), a_t)
                    p.wait("vector", a_t, a_t.n)
                    p.op("vector", lambda e: e.tensor_scalar(
                        w_sb[:], w_sb[:], rc_sb[:], None, ALU.mult), a_t)
                p.wait("vector", a_t, a_t.n)
                p.op("vector", lambda e: e.tensor_copy(wb_sb[:], w_sb[:]), a_t)
                wbn = a_t.n
                p.wait("tensor", a_t, wbn)
                p.wait("tensor", a_f, a_f.n)
                p.op("tensor", lambda e: e.transpose(
                    ptw[:, :], wb_sb[:, 0:n_e], identb[:]), a_f)
                p.wait("scalar", a_f, a_f.n)
                p.op("scalar", lambda e, tg=tg: e.activation(
                    wdt[:, tg * 128:(tg + 1) * 128], ptw[0:n_e, :], AF.Copy, scale=1.0), a_f)
            p.wait("tensor", a_f, a_f.n)

    emit_router_and_topk(NE, KK)

    if layer == 0:
        with ExitStack() as s3:
            ps_ws = [s3.enter_context(nc.psum_tensor(f"ps_ws{i}", [1, 512], F32, allow_name_mangling=True)) for i in range(2)]
            for hh in range(2):
                p.op("tensor", lambda e, hh=hh: e.matmul(
                    ps_ws[hh][:], onec[0:64, :], wdt[:, hh * 512:(hh + 1) * 512],
                    start=True, stop=True), a_f)
            wsn = a_f.n
            p.wait("vector", a_f, wsn)
            for hh in range(2):
                p.op("vector", lambda e, hh=hh: e.tensor_scalar(
                    ws1b[:, hh * 512:(hh + 1) * 512], ps_ws[hh][:], 1.0, None, ALU.add), a_f)
            p.wait("tensor", a_f, a_f.n)
            ps_wb = s3.enter_context(nc.psum_tensor("ps_wb", [128, 1024], F32, allow_name_mangling=True))
            for hh in range(2):
                p.op("tensor", lambda e, hh=hh: e.matmul(
                    ps_wb[:, hh * 512:(hh + 1) * 512], oner[:],
                    ws1b[:, hh * 512:(hh + 1) * 512], start=True, stop=True), a_f)
            p.wait("scalar", a_f, a_f.n)
            p.op("scalar", lambda e: e.activation(wsb_sb[:], ps_wb[:], AF.Copy, scale=1.0), a_f)
            p.wait("tensor", a_f, a_f.n)
        wsbn = a_f.n
        rx1b = Ring(p, "rx1b", 2)
        rxr2 = Ring(p, "rxr2", 2)
        st1.wait_all(p, "sync")
        p.wait("sync", a_r, a_r.n)
        with ExitStack() as s5s:
            stats2 = [s5s.enter_context(nc.psum_tensor(f"st2_{i}", [1, 512], F32, allow_name_mangling=True)) for i in range(4)]
            s5p = ExitStack()
            pbm = [s5p.enter_context(nc.psum_tensor(f"pbm{i}", [128, 1024], F32, allow_name_mangling=True)) for i in range(2)]
            stat_pend = []
            a_r_base = a_r.n
            a_sm_base = a_sm.n
            for ec in range(NC):
                for fn2 in stat_pend:
                    fn2()
                stat_pend = []
                if ec >= 2:
                    p.waitf("tensor", (a_r, a_r_base + ec - 1))
                else:
                    p.wait("tensor", a_f, wsbn)
                for t in range(2):
                    p.op("tensor", lambda e, ec=ec, t=t: e.matmul(
                        pbm[ec % 2][:, t * 512:(t + 1) * 512],
                        ebb[:, ec * 128:(ec + 1) * 128],
                        wdt[:, t * 512:(t + 1) * 512], start=True, stop=True),
                        a_f if t == 1 else None)
                bmn = a_f.n
                p.wait("sync", a_r, a_r_base + max(0, ec - 1))
                rx1b.dma(p, "sync", ec, xp_sb[ec % 2][:], x1_dram[ec * 128:(ec + 1) * 128, :])
                rx1b.wait(p, "vector", ec)
                p.wait("vector", a_f, bmn)
                if ec >= 2:
                    p.waitf("vector", rxr2.free(ec - 2))
                p.wait("vector", a_ln, a_ln.n)
                p.op("vector", lambda e, ec=ec: e.tensor_mul(
                    xr_sb[ec % 2][:], xp_sb[ec % 2][:], wsb_sb[:]), a_ln)
                p.wait("vector", a_ln, a_ln.n)
                p.op("vector", lambda e, ec=ec: e.tensor_add(
                    xr_sb[ec % 2][:], xr_sb[ec % 2][:], pbm[ec % 2][:]), a_r)
                rn = a_r.n
                p.wait("sync", a_r, rn)
                rxr2.dma(p, "sync", ec, xr_dram[ec * 128:(ec + 1) * 128, :], xr_sb[ec % 2][:])
                p.wait("scalar", a_r, rn)
                if ec >= 2:
                    p.wait("scalar", a_sm, a_sm_base + ec - 1)
                p.op("scalar", lambda e, ec=ec: e.activation(
                    xrb_sb[ec % 2][:], xr_sb[ec % 2][:], AF.Copy, scale=1.0), a_c)
                cn = a_c.n
                p.wait("vector", a_c, cn)
                if ec >= 2:
                    p.wait("vector", a_sm, a_sm_base + ec - 1)
                p.op("vector", lambda e, ec=ec: e.tensor_mul(
                    sq_sb[ec % 2][:], xrb_sb[ec % 2][:], xrb_sb[ec % 2][:]), a_q)
                qn = a_q.n
                def emit_def(ec=ec, cn=cn, qn=qn):
                    p.wait("tensor", a_c, cn)
                    p.wait("tensor", a_q, qn)
                    for hh in range(2):
                        p.op("tensor", lambda e, ec=ec, hh=hh: e.matmul(
                            stats2[hh][:], onec[:], xrb_sb[ec % 2][:, hh * 512:(hh + 1) * 512],
                            start=(ec == 0), stop=(ec == NC - 1), skip_group_check=True), None)
                        p.op("tensor", lambda e, ec=ec, hh=hh: e.matmul(
                            stats2[2 + hh][:], onec[:], sq_sb[ec % 2][:, hh * 512:(hh + 1) * 512],
                            start=(ec == 0), stop=(ec == NC - 1), skip_group_check=True),
                            a_sm if hh == 1 else None)
                stat_pend.append(emit_def)
            for fn2 in stat_pend:
                fn2()
            s5p.close()
            stf = ln_finalize_apply(stats2, a_sm.n, xr_dram, rxr2, xout, False, "ln2")
    else:
        with ExitStack() as s3:
            ps_h = s3.enter_context(nc.psum_tensor("ps_h", [128, 1024], F32, allow_name_mangling=True))
            ps_cb = s3.enter_context(nc.psum_tensor("ps_cb", [128, 1024], F32, allow_name_mangling=True))
            p.wait("tensor", a_f, a_f.n)
            for t in range(2):
                p.op("tensor", lambda e, t=t: e.matmul(
                    ps_cb[:, t * 512:(t + 1) * 512], ex8_sb[:],
                    wdt[:, t * 512:(t + 1) * 512], start=True, stop=True), a_f)
            cbn = a_f.n
            p.wait("scalar", a_f, cbn)
            p.op("scalar", lambda e: e.activation(cb_sb[:], ps_cb[:], AF.Copy, scale=1.0), a_f)
            p.wait("tensor", a_f, a_f.n)
            for dd in range(NC):
                for t in range(2):
                    p.op("tensor", lambda e, dd=dd, t=t: e.matmul(
                        ps_h[:, t * 512:(t + 1) * 512],
                        asb[:, dd * 128:(dd + 1) * 128],
                        ab[:, dd * 1024 + t * 512: dd * 1024 + (t + 1) * 512],
                        start=(dd == 0), stop=(dd == NC - 1)),
                        a_f if (dd == NC - 1 and t == 1) else None)
            hn = a_f.n
            p.wait("vector", a_f, hn)
            p.op("vector", lambda e: e.scalar_tensor_tensor(
                hs_sb[:], ps_h[:], 2.0, cb_sb[:], ALU.mult, ALU.mult), a_f)
            p.wait("tensor", a_f, a_f.n)
        hs_n = a_f.n
        rxp2 = Ring(p, "rxp2", 2)
        rxr2 = Ring(p, "rxr2", 2)
        p.wait("sync", a_r, a_r.n)
        with ExitStack() as s6s:
            stats2 = [s6s.enter_context(nc.psum_tensor(f"st2_{i}", [1, 512], F32, allow_name_mangling=True)) for i in range(4)]
            s6p = ExitStack()
            psq2b = [s6p.enter_context(nc.psum_tensor(f"psqb{i}", [128, 1024], F32, allow_name_mangling=True)) for i in range(2)]

            def extra1(ec, psten, s_mm):
                p.wait("tensor", a_f, hs_n)
                for t in range(2):
                    p.op("tensor", lambda e, ec=ec, t=t, psten=psten: e.matmul(
                        psten[:, t * 512:(t + 1) * 512],
                        bsb[:, ec * 128:(ec + 1) * 128],
                        hs_sb[:, t * 512:(t + 1) * 512], start=False, stop=True),
                        s_mm if t == 1 else None)

            a_r_base = a_r.n
            st1.wait_all(p, "sync")
            ev2 = mk_evac(rxp2, rxr2, stats2, a_sm.n,
                          lambda e, ec, psten: e.scalar_tensor_tensor(
                              xr_sb[ec % 2][:], psten[:], bbf[:, ec:ec + 1],
                              xp_sb[ec % 2][:], ALU.add, ALU.add),
                          x1_dram)
            def ev2g(ec, psten, defer, mm_need, s_mm):
                p.wait("sync", a_r, a_r_base + max(0, ec - 1))
                return ev2(ec, psten, defer, mm_need, s_mm)
            emit_proj(p, "wb", wbase, NC, ab, psq2b, ev2g,
                      pre_mm_waits=[(a_f, hs_n), (a_c, ab2_ready)], extra_mm_cb=extra1,
                      depth=6)
            s6p.close()
            stf = ln_finalize_apply(stats2, a_sm.n, xr_dram, rxr2, xout, False, "ln2")

    stf.wait_all(p, "gpsimd")


def build_post(layer):
    nc = mk_nc()
    attin = nc.dram_tensor("attin", [D, TL], BF16, kind="ExternalInput")
    wo = nc.dram_tensor("wo", [D, D], F32, kind="ExternalInput")
    xprev = nc.dram_tensor("xprev", [D, TL], F32, kind="ExternalInput")
    xout = nc.dram_tensor("xout", [D, TL], F32, kind="ExternalOutput")
    kw = {}
    if layer == 0:
        kw["rw"] = nc.dram_tensor("rw", [D, 64], F32, kind="ExternalInput")
        kw["rbias"] = nc.dram_tensor("rbias", [64, 1], F32, kind="ExternalInput")
        kw["eb64"] = nc.dram_tensor("eb64", [64, D], F32, kind="ExternalInput")
    else:
        kw["wbase"] = nc.dram_tensor("wbase", [D, D], F32, kind="ExternalInput")
        kw["bb_in"] = nc.dram_tensor("bb", [D, 1], F32, kind="ExternalInput")
        kw["wr"] = nc.dram_tensor("wr", [D, 8], F32, kind="ExternalInput")
        kw["astk"] = nc.dram_tensor("astk", [D, 128], F32, kind="ExternalInput")
        kw["bstk"] = nc.dram_tensor("bstk", [128, D], F32, kind="ExternalInput")
    xr_dram = nc.dram_tensor("xr_scr", [D, TL], F32)
    x1_dram = nc.dram_tensor("x1_scr", [D, TL], F32)
    with ExitStack() as ctx:
        ctx.enter_context(nc.allow_low_precision(reason="bf16 compute per stated tolerance"))
        p = P(nc, ctx)
        emit_post(nc, p, ctx, "", layer, attin, wo, xprev, xout, xr_dram, x1_dram, **kw)
        p.flush("post")
    return nc



def emit_load_xb_bf(p, tag, src_dram, xb):
    s_l = p.ctr(f"{tag}_l")
    for f in range(NC):
        p.dma("sync", xb[:, f * 1024:(f + 1) * 1024], src_dram[f * 128:(f + 1) * 128, :], s_l)
    return s_l


def owned_cols(c):
    return np.concatenate([np.arange((8 * k + c) * 128, (8 * k + c) * 128 + 128)
                           for k in range(8)])


def rope_tables():
    inv = 1.0 / (10000.0 ** (np.arange(0, 128, 2, dtype=np.float64) / 128.0))
    t = np.arange(SEQ, dtype=np.float64)
    fr = np.outer(t, inv)
    emb = np.concatenate([fr, fr], axis=-1)
    return np.cos(emb).astype(np.float32), np.sin(emb).astype(np.float32)


def build_fused():
    nc = mk_nc()
    xin = nc.dram_tensor("xin", [D, TL], F32, kind="ExternalInput")
    cos_in = nc.dram_tensor("cosin", [128, TL], F32, kind="ExternalInput")
    sin_in = nc.dram_tensor("sinin", [128, TL], F32, kind="ExternalInput")
    wq0 = nc.dram_tensor("wq0", [D, D], F32, kind="ExternalInput")
    wk0 = nc.dram_tensor("wk0", [D, D], F32, kind="ExternalInput")
    wv0 = nc.dram_tensor("wv0", [D, D], F32, kind="ExternalInput")
    wo0 = nc.dram_tensor("wo0", [D, D], F32, kind="ExternalInput")
    rw = nc.dram_tensor("rw", [D, 64], F32, kind="ExternalInput")
    rbias = nc.dram_tensor("rbias", [64, 1], F32, kind="ExternalInput")
    eb64 = nc.dram_tensor("eb64", [64, D], F32, kind="ExternalInput")
    wq1 = nc.dram_tensor("wq1", [D, D], F32, kind="ExternalInput")
    wk1 = nc.dram_tensor("wk1", [D, D], F32, kind="ExternalInput")
    wv1 = nc.dram_tensor("wv1", [D, D], F32, kind="ExternalInput")
    wo1 = nc.dram_tensor("wo1", [D, D], F32, kind="ExternalInput")
    wbase = nc.dram_tensor("wbase", [D, D], F32, kind="ExternalInput")
    bb_in = nc.dram_tensor("bb", [D, 1], F32, kind="ExternalInput")
    wr = nc.dram_tensor("wr", [D, 8], F32, kind="ExternalInput")
    astk = nc.dram_tensor("astk", [D, 128], F32, kind="ExternalInput")
    bstk = nc.dram_tensor("bstk", [128, D], F32, kind="ExternalInput")
    msk_in = nc.dram_tensor("msk", [8 * 128, 128], BF16, kind="ExternalInput")
    xout = nc.dram_tensor("xout", [D, TL], F32, kind="ExternalOutput")
    q0l = nc.dram_tensor("q0l", [D, TL], BF16)
    k0l = nc.dram_tensor("k0l", [D, TL], BF16)
    v0l = nc.dram_tensor("v0l", [TL, D], BF16)
    kag0 = nc.dram_tensor("kag0", [NCORE * D, TL], BF16, addr_space="Shared")
    vag0 = nc.dram_tensor("vag0", [NCORE * TL, D], BF16, addr_space="Shared")
    att0 = nc.dram_tensor("att0", [D, TL], BF16)
    x2d = nc.dram_tensor("x2d", [D, TL], F32)
    q1l = nc.dram_tensor("q1l", [D, TL], BF16)
    k1l = nc.dram_tensor("k1l", [D, TL], BF16)
    v1l = nc.dram_tensor("v1l", [TL, D], BF16)
    kag1 = nc.dram_tensor("kag1", [NCORE * D, TL], BF16, addr_space="Shared")
    vag1 = nc.dram_tensor("vag1", [NCORE * TL, D], BF16, addr_space="Shared")
    att1 = nc.dram_tensor("att1", [D, TL], BF16)
    xr_dram = nc.dram_tensor("xr_scr", [D, TL], F32)
    x1_dram = nc.dram_tensor("x1_scr", [D, TL], F32)

    with ExitStack() as octx:
        octx.enter_context(nc.allow_low_precision(reason="bf16 compute per stated tolerance"))
        p = P(nc, octx)

        def mk_ag_cb(pairs_map, cc):
            def cb(name, snaps):
                if name in pairs_map:
                    src_d, dst_d = pairs_map[name]
                    for (sw, vw) in snaps:
                        p.wait("gpsimd", sw, vw)
                    p.op("gpsimd", lambda e, src_d=src_d, dst_d=dst_d: e.collective_compute(
                        "AllGather", ALU.bypass, replica_groups=[list(range(NCORE))],
                        ins=[src_d[:]], outs=[dst_d[:]]), cc)
            return cb

        cc0 = p.ctr("cc0")
        with ExitStack() as c:
            p.actx = c
            emit_qkv(nc, p, c, "0", True, xin, wq0, wk0, wv0, q0l, k0l, v0l, cos_in, sin_in,
                     ag_cb=mk_ag_cb({"k": (k0l, kag0), "v": (v0l, vag0)}, cc0))
        p.flush("phA")
        nc.all_engine_barrier()
        with ExitStack() as c:
            p.actx = c
            emit_attn(nc, p, c, False, L0_SCALE, q0l, kag0, vag0, att0,
                      pre_waits=[(cc0, 2)])
        p.flush("phB")
        nc.all_engine_barrier()
        with ExitStack() as c:
            p.actx = c
            emit_post(nc, p, c, "0", 0, att0, wo0, xin, x2d, xr_dram, x1_dram,
                      rw=rw, rbias=rbias, eb64=eb64)
        p.flush("phC")
        nc.all_engine_barrier()
        cc1 = p.ctr("cc1")
        with ExitStack() as c:
            p.actx = c
            emit_qkv(nc, p, c, "1", False, x2d, wq1, wk1, wv1, q1l, k1l, v1l,
                     ag_cb=mk_ag_cb({"k": (k1l, kag1), "v": (v1l, vag1)}, cc1))
        p.flush("phD")
        nc.all_engine_barrier()
        with ExitStack() as c:
            p.actx = c
            emit_attn(nc, p, c, True, L1_SCALE, q1l, kag1, vag1, att1, msk_in,
                      pre_waits=[(cc1, 2)])
        p.flush("phE")
        nc.all_engine_barrier()
        with ExitStack() as c:
            p.actx = c
            emit_post(nc, p, c, "1", 1, att1, wo1, x2d, xout, xr_dram, x1_dram,
                      wbase=wbase, bb_in=bb_in, wr=wr, astk=astk, bstk=bstk)
        p.flush("phF")
    return nc


def kernel(**inputs):
    g = {k: np.asarray(v) for k, v in inputs.items()}
    x = np.ascontiguousarray(g["L0_attn_hidden_states"][0].T)
    wqT = np.ascontiguousarray(g["L0_attn_Wq"].T.astype(np.float32, copy=False))
    wkT = np.ascontiguousarray(g["L0_attn_Wk"].T.astype(np.float32, copy=False))
    wvT = np.ascontiguousarray(g["L0_attn_Wv"].T.astype(np.float32, copy=False))
    woT = np.ascontiguousarray(g["L0_attn_Wo"].T.astype(np.float32, copy=False))
    rwT = np.ascontiguousarray(g["L0_ffn_router_weight"].T.astype(np.float32, copy=False))
    rbias = np.ascontiguousarray(g["L0_ffn_router_bias"].reshape(64, 1).astype(np.float32, copy=False))
    eb64v = np.ascontiguousarray(g["L0_ffn_expert_bias"].astype(np.float32, copy=False))
    wq1f = g["L1_attn_wq"].astype(np.float32, copy=False)
    wq1u = np.ascontiguousarray(
        np.concatenate([wq1f[:, h * 192:h * 192 + 128] for h in range(32)], axis=1))
    wkv1 = g["L1_attn_wkv"].astype(np.float32, copy=False)
    wk1v = np.ascontiguousarray(wkv1[:, 0:4096])
    wv1v = np.ascontiguousarray(wkv1[:, 4096:8192])
    wo1v = np.ascontiguousarray(g["L1_attn_wo"].astype(np.float32, copy=False))
    wbT = np.ascontiguousarray(g["L1_ffn_W_base"].T.astype(np.float32, copy=False))
    bbv = np.ascontiguousarray(g["L1_ffn_b_base"].reshape(4096, 1).astype(np.float32, copy=False))
    wrT = np.ascontiguousarray(g["L1_ffn_W_router"].T.astype(np.float32, copy=False))
    astkv = np.ascontiguousarray(
        g["L1_ffn_A_stack"].astype(np.float32, copy=False).reshape(128, 4096).T)
    bstkv = np.ascontiguousarray(
        g["L1_ffn_B_stack"].astype(np.float32, copy=False).transpose(0, 2, 1).reshape(128, 4096))

    cols = [owned_cols(c) for c in range(NCORE)]
    cos, sin = rope_tables()
    nc = build_fused()
    in_maps = []
    for c in range(NCORE):
        in_maps.append({
            "xin": np.ascontiguousarray(x[:, cols[c]]),
            "cosin": np.ascontiguousarray(cos[cols[c]].T),
            "sinin": np.ascontiguousarray(sin[cols[c]].T),
            "wq0": wqT, "wk0": wkT, "wv0": wvT, "wo0": woT,
            "rw": rwT, "rbias": rbias, "eb64": eb64v,
            "wq1": wq1u, "wk1": wk1v, "wv1": wv1v, "wo1": wo1v,
            "wbase": wbT, "bb": bbv, "wr": wrT, "astk": astkv, "bstk": bstkv,
            "msk": build_masks(c),
        })
    res = run_bass_kernel_spmd(nc, in_maps, core_ids=list(range(NCORE))).results
    out = np.empty((D, SEQ), dtype=np.float32)
    for c in range(NCORE):
        out[:, cols[c]] = res[c]["xout"]
    return np.ascontiguousarray(out.T).reshape(1, SEQ, D)


def run_qkv_launch(rope, x_full_T, wq_de, wk_de, wv_de):
    nc = build_qkv(rope)
    in_maps = []
    if rope:
        cos, sin = rope_tables()
    for c in range(NCORE):
        cols = owned_cols(c)
        m = {"xin": np.ascontiguousarray(x_full_T[:, cols]),
             "wq": wq_de, "wk": wk_de, "wv": wv_de}
        if rope:
            m["cosin"] = np.ascontiguousarray(cos[cols].T)
            m["sinin"] = np.ascontiguousarray(sin[cols].T)
        in_maps.append(m)
    res = run_bass_kernel_spmd(nc, in_maps, core_ids=list(range(NCORE)))
    return res.results


def build_masks(c):
    m = np.zeros((8, 128, 128), dtype=np.float32)
    for r in range(8):
        if r < c:
            m[r] = 1.0
        elif r == c:
            m[r] = np.triu(np.ones((128, 128), dtype=np.float32))
    return np.ascontiguousarray(m.reshape(8 * 128, 128).astype(BF))


def run_qkv(rope, xT_cores, wq_de, wk_de, wv_de, extra=None):
    nc = build_qkv(rope)
    in_maps = []
    for c in range(NCORE):
        m = {"xin": xT_cores[c], "wq": wq_de, "wk": wk_de, "wv": wv_de}
        if rope:
            m.update(extra[c])
        in_maps.append(m)
    return run_bass_kernel_spmd(nc, in_maps, core_ids=list(range(NCORE))).results


def kernel_unfused(**inputs):
    g = {k: np.asarray(v) for k, v in inputs.items()}
    x = np.ascontiguousarray(g["L0_attn_hidden_states"][0].T)
    wqT = np.ascontiguousarray(g["L0_attn_Wq"].T.astype(np.float32, copy=False))
    wkT = np.ascontiguousarray(g["L0_attn_Wk"].T.astype(np.float32, copy=False))
    wvT = np.ascontiguousarray(g["L0_attn_Wv"].T.astype(np.float32, copy=False))
    woT = np.ascontiguousarray(g["L0_attn_Wo"].T.astype(np.float32, copy=False))
    rwT = np.ascontiguousarray(g["L0_ffn_router_weight"].T.astype(np.float32, copy=False))
    rbias = np.ascontiguousarray(g["L0_ffn_router_bias"].reshape(64, 1).astype(np.float32, copy=False))
    eb64 = np.ascontiguousarray(g["L0_ffn_expert_bias"].astype(np.float32, copy=False))
    wq1 = g["L1_attn_wq"].astype(np.float32, copy=False)
    wq1u = np.ascontiguousarray(
        np.concatenate([wq1[:, h * 192:h * 192 + 128] for h in range(32)], axis=1))
    wkv1 = g["L1_attn_wkv"].astype(np.float32, copy=False)
    wk1 = np.ascontiguousarray(wkv1[:, 0:4096])
    wv1 = np.ascontiguousarray(wkv1[:, 4096:8192])
    wo1 = np.ascontiguousarray(g["L1_attn_wo"].astype(np.float32, copy=False))
    wbT = np.ascontiguousarray(g["L1_ffn_W_base"].T.astype(np.float32, copy=False))
    bb = np.ascontiguousarray(g["L1_ffn_b_base"].reshape(4096, 1).astype(np.float32, copy=False))
    wrT = np.ascontiguousarray(g["L1_ffn_W_router"].T.astype(np.float32, copy=False))
    astk = np.ascontiguousarray(
        g["L1_ffn_A_stack"].astype(np.float32, copy=False).reshape(128, 4096).T)
    bstk = np.ascontiguousarray(
        g["L1_ffn_B_stack"].astype(np.float32, copy=False).transpose(0, 2, 1).reshape(128, 4096))

    cols = [owned_cols(c) for c in range(NCORE)]
    xT_cores = [np.ascontiguousarray(x[:, cols[c]]) for c in range(NCORE)]
    cos, sin = rope_tables()
    rope_extra = [{"cosin": np.ascontiguousarray(cos[cols[c]].T),
                   "sinin": np.ascontiguousarray(sin[cols[c]].T)} for c in range(NCORE)]

    resA = run_qkv(True, xT_cores, wqT, wkT, wvT, rope_extra)
    kag0 = np.concatenate([r["kout"] for r in resA], axis=0)
    vag0 = np.concatenate([r["vout"] for r in resA], axis=0)

    ncB = build_attn(False, L0_SCALE)
    in_maps = [{"qin": resA[c]["qout"], "kag": kag0, "vag": vag0} for c in range(NCORE)]
    resB = run_bass_kernel_spmd(ncB, in_maps, core_ids=list(range(NCORE))).results

    ncC = build_post(0)
    in_maps = [{"attin": resB[c]["att"], "wo": woT, "xprev": xT_cores[c],
                "rw": rwT, "rbias": rbias, "eb64": eb64} for c in range(NCORE)]
    resC = run_bass_kernel_spmd(ncC, in_maps, core_ids=list(range(NCORE))).results

    x2_cores = [np.ascontiguousarray(resC[c]["xout"]) for c in range(NCORE)]
    resD = run_qkv(False, x2_cores, wq1u, wk1, wv1)
    kag1 = np.concatenate([r["kout"] for r in resD], axis=0)
    vag1 = np.concatenate([r["vout"] for r in resD], axis=0)

    ncE = build_attn(True, L1_SCALE)
    in_maps = [{"qin": resD[c]["qout"], "kag": kag1, "vag": vag1,
                "msk": build_masks(c)} for c in range(NCORE)]
    resE = run_bass_kernel_spmd(ncE, in_maps, core_ids=list(range(NCORE))).results

    ncF = build_post(1)
    in_maps = [{"attin": resE[c]["att"], "wo": wo1, "xprev": x2_cores[c],
                "wbase": wbT, "bb": bb, "wr": wrT, "astk": astk, "bstk": bstk}
               for c in range(NCORE)]
    resF = run_bass_kernel_spmd(ncF, in_maps, core_ids=list(range(NCORE))).results

    out = np.empty((D, SEQ), dtype=np.float32)
    for c in range(NCORE):
        out[:, cols[c]] = resF[c]["xout"]
    return np.ascontiguousarray(out.T).reshape(1, SEQ, D)


# version 09 (final)
# speedup vs baseline: 1.5081x; 1.1965x over previous
import numpy as np
import ml_dtypes
from contextlib import ExitStack
import concourse.bass as bass
import concourse.mybir as mybir
from concourse.bass_utils import run_bass_kernel_spmd

F32 = mybir.dt.float32
BF16 = mybir.dt.bfloat16
AF = mybir.ActivationFunctionType
ALU = mybir.AluOpType
AX = mybir.AxisListType
BF = ml_dtypes.bfloat16

NCORE = 8
D = 4096
NC = 32
TL = 1024
NH = 32
SEQ = 8192
L0_SCALE = float(1.0 / np.sqrt(np.float32(128.0)))
L1_SCALE = float(np.float64(192.0) ** (-0.5))
EPS = 1e-5


class Ctr:
    def __init__(self, ctx, nc, name):
        self.h = ctx.enter_context(nc.semaphore(name))
        self.n = 0


class Ring:
    def __init__(self, p, name, n):
        self.sems = [p.ctr(f"{name}_r{i}") for i in range(n)]
        self.n = n
        self.rec = {}
        self.nk = 0

    def take(self, cnt):
        b = self.nk
        self.nk += cnt
        return b

    def dma(self, p, eng, i, out, in_, **kw):
        s = self.sems[i % self.n]
        p.waitf(eng, self.rec.get(i - self.n))
        p.dma(eng, out, in_, s, **kw)
        self.rec[i] = (s, s.n)

    def wait(self, p, eng, i):
        p.waitf(eng, self.rec[i])

    def free(self, i):
        return self.rec[i]

    def wait_all(self, p, eng):
        for s in self.sems:
            p.wait(eng, s, s.n)


class P:
    ENGS = ("sync", "gpsimd", "tensor", "vector", "scalar")

    def __init__(self, nc, ctx):
        self.nc = nc
        self.ctx = ctx
        self.actx = ctx
        self.s = {e: [] for e in P.ENGS}
        self._ctrs = {}
        self._rings = {}

    def ctr(self, name):
        if name not in self._ctrs:
            self._ctrs[name] = Ctr(self.ctx, self.nc, name)
        return self._ctrs[name]

    def ring(self, name, n):
        if name not in self._rings:
            self._rings[name] = Ring(self, name, n)
        return self._rings[name]

    def wait(self, eng, ctr, val):
        if val > 0:
            h = ctr.h
            self.s[eng].append(lambda e: e.wait_ge(h, val))

    def waitf(self, eng, free):
        if free is not None:
            self.wait(eng, free[0], free[1])

    def op(self, eng, fn, inc=None, k=1):
        if inc is None:
            self.s[eng].append(fn)
            return None
        h = inc.h
        self.s[eng].append(lambda e: fn(e).then_inc(h, k))
        inc.n += k
        return inc.n

    def dma(self, eng, out, in_, inc, **kw):
        h = inc.h
        self.s[eng].append(lambda e: e.dma_start(out=out, in_=in_, **kw).then_inc(h, 16))
        inc.n += 16
        return inc.n

    def flush(self, name):
        with self.nc.Block(name) as blk:
            for eng in P.ENGS:
                ops = self.s[eng]
                if not ops:
                    continue
                def runner(e, ops=ops):
                    for f in ops:
                        f(e)
                getattr(blk, eng)(runner)
        self.s = {e: [] for e in P.ENGS}


def mk_nc():
    return bass.Bass("TRN2", target_bir_lowering=False, num_devices=NCORE)


def emit_load_xb(p, tag, x_dram, xb):
    nc, ctx = p.nc, p.actx
    xf = [ctx.enter_context(nc.sbuf_tensor(f"{tag}_xf{i}", [128, 1024], F32, allow_name_mangling=True)) for i in range(4)]
    rl = Ring(p, f"{tag}_xl", 4)
    s_c = p.ctr(f"{tag}_xc")
    c_base = s_c.n
    for f in range(NC):
        p.wait("sync", s_c, c_base + f - 3)
        rl.dma(p, "sync", f, xf[f % 4][:], x_dram[f * 128:(f + 1) * 128, :])
        rl.wait(p, "scalar", f)
        p.op("scalar", lambda e, f=f: e.activation(
            xb[:, f * 1024:(f + 1) * 1024], xf[f % 4][:], AF.Copy, scale=1.0), s_c)
    return s_c


def emit_proj(p, tag, w_dram, n_d, xb, psq, evac_cb, pre_mm_waits=None, extra_mm_cb=None,
              psum_free=None, pf_base=0, depth=8, sctx=None):
    nc, ctx = p.nc, (sctx if sctx is not None else p.actx)
    n_e = w_dram.shape[1] // 128
    nrot = len(psq)
    wsf = [ctx.enter_context(nc.sbuf_tensor(f"{tag}_wsf{i}", [128, 256], F32, allow_name_mangling=True)) for i in range(depth)]
    wsb = [ctx.enter_context(nc.sbuf_tensor(f"{tag}_wsb{i}", [128, 256], BF16, allow_name_mangling=True)) for i in range(depth)]
    rwd = p.ring("wd", 8)
    rwd_kb = rwd.take((w_dram.shape[1] // 256) * n_d)
    s_wc = p.ctr(f"{tag}_wc")
    wc_base = s_wc.n
    s_mm = p.ctr(f"{tag}_mm")
    mm_done = {}
    if psum_free is None:
        psum_free = {}
    pend = []

    def defer(fn):
        pend.append(fn)

    n_eg = n_e // 2
    for eg in range(n_eg):
        for fn in pend:
            fn()
        pend = []
        for d in range(n_d):
            idx = eg * n_d + d
            p.wait("sync", s_wc, wc_base + idx - (depth - 1))
            rwd.dma(p, "sync", rwd_kb + idx, wsf[idx % depth][:],
                    w_dram[d * 128:(d + 1) * 128, eg * 256:(eg + 1) * 256])
            rwd.wait(p, "vector", rwd_kb + idx)
            p.wait("vector", s_mm, mm_done.get(idx - depth, 0))
            p.op("vector", lambda e, a=wsb[idx % depth], b=wsf[idx % depth]:
                 e.tensor_copy(a[:], b[:]), s_wc)
            p.wait("tensor", s_wc, wc_base + idx + 1)
            if d == 0:
                if eg == 0 and pre_mm_waits:
                    for (ctr, val) in pre_mm_waits:
                        p.wait("tensor", ctr, val)
                for e2 in range(2):
                    ec = eg * 2 + e2
                    p.waitf("tensor", psum_free.get(ec % nrot))
            for e2 in range(2):
                ec = eg * 2 + e2
                psten = psq[ec % nrot]
                for t in range(2):
                    fn = (lambda e, psten=psten, w=wsb[idx % depth], e2=e2, d=d, t=t:
                          e.matmul(psten[:, t * 512:(t + 1) * 512],
                                   w[:, e2 * 128:(e2 + 1) * 128],
                                   xb[:, d * 1024 + t * 512: d * 1024 + (t + 1) * 512],
                                   start=(d == 0),
                                   stop=(d == n_d - 1 and extra_mm_cb is None)))
                    p.op("tensor", fn, s_mm if (e2 == 1 and t == 1) else None)
            mm_done[idx] = s_mm.n
        for e2 in range(2):
            ec = eg * 2 + e2
            if extra_mm_cb is not None:
                extra_mm_cb(ec, psq[ec % nrot], s_mm)
            psum_free[ec % nrot] = evac_cb(ec, psq[ec % nrot], defer, s_mm.n, s_mm)
    for fn in pend:
        fn()
    return psum_free, s_mm


def emit_qkv(nc, p, ctx, tag, rope, xin, wq, wk, wv, qout, kout, vout,
             cos_in=None, sin_in=None, ag_cb=None):
    if rope:
        rsign = np.zeros((128, 128), dtype=BF)
        for i in range(64):
            rsign[i + 64, i] = -1.0
            rsign[i, i + 64] = 1.0
        rsign_d = nc.inline_tensor(rsign, f"rsign{tag}")
    ident_d = nc.inline_tensor(np.eye(128, dtype=BF), f"ident{tag}")
    sb = lambda n, s, d: ctx.enter_context(nc.sbuf_tensor(n, s, d, allow_name_mangling=True))
    psa = lambda n, s, d=F32: ctx.enter_context(nc.psum_tensor(n, s, d, allow_name_mangling=True))

    xb = sb("xb", [128, NC * 1024], BF16)
    psq = [psa(f"psq{i}", [128, 1024]) for i in range(3)]
    ptr_bf = psa("ptr_bf", [128, 1024], BF16)
    prot = psa("prot", [128, 512])
    ev_sb = [sb(f"ev{i}", [128, 1024], BF16) for i in range(3)]
    ob_sb = [sb(f"ob{i}", [128, 1024], BF16) for i in range(2)]
    tmp1 = [sb(f"tmp1{i}", [128, 1024], BF16) for i in range(2)]
    tmp2 = [sb(f"tmp2{i}", [128, 1024], BF16) for i in range(2)]
    ident_sb = sb("ident_sb", [128, 128], BF16)
    if rope:
        cosb = sb("cosb", [128, 1024], BF16)
        sinb = sb("sinb", [128, 1024], BF16)
        cosf = sb("cosf", [128, 1024], F32)
        rs_sb = sb("rs_sb", [128, 128], BF16)

    s_aux = p.ctr("aux")
    s_rot = p.ctr("rot")
    s_rop = p.ctr("rope")
    rst_ob = Ring(p, "st_ob", 2)
    rst_eb = Ring(p, "st_eb", 3)
    ob_sti = [0]
    eb_sti = [0]

    aux0 = s_aux.n
    p.dma("sync", ident_sb[:], ident_d[:], s_aux)
    if rope:
        p.dma("sync", cosf[:], cos_in[:], s_aux)
        p.dma("sync", rs_sb[:], rsign_d[:], s_aux)
        p.wait("vector", s_aux, aux0 + 48)
        p.op("vector", lambda e: e.tensor_copy(cosb[:], cosf[:]), s_aux)
        p.wait("sync", s_aux, aux0 + 49)
        p.dma("sync", cosf[:], sin_in[:], s_aux)
        p.wait("vector", s_aux, aux0 + 65)
        p.op("vector", lambda e: e.tensor_copy(sinb[:], cosf[:]), s_aux)
    aux_ready = s_aux.n

    xb_ready = emit_load_xb(p, "x", xin, xb)

    G = {"n": 0}
    eb_free = {}
    ob_free = {}
    rot_free = {}
    tmp_free = {}

    s_evq = p.ctr("evq")

    def make_evac(kind, out_dram):
        def evac(ec, psten, defer, mm_need, s_mm):
            gec = G["n"]; G["n"] += 1
            eb = ev_sb[gec % 3]
            ob = ob_sb[gec % 2]
            t1, t2 = tmp1[gec % 2], tmp2[gec % 2]
            p.wait("scalar", s_mm, mm_need)
            p.waitf("scalar", eb_free.get(gec - 3))
            p.op("scalar", lambda e: e.activation(
                eb[:], psten[:], AF.Copy, scale=1.0), s_evq)
            ev_now = s_evq.n

            if kind == "plain":
                p.wait("scalar", s_evq, ev_now)
                rst_eb.dma(p, "scalar", gec, out_dram[ec * 128:(ec + 1) * 128, :], eb[:])
                eb_free[gec] = rst_eb.free(gec)
                return (s_evq, ev_now)

            def emit_def():
                if kind == "rope":
                    p.wait("tensor", s_evq, ev_now)
                    p.waitf("tensor", rot_free.get(gec - 1))
                    p.op("tensor", lambda e: e.matmul(
                        prot[:], rs_sb[:], eb[:, 0:512], start=True, stop=True), s_rot)
                    r1 = s_rot.n
                    p.wait("vector", s_rot, r1)
                    p.waitf("vector", tmp_free.get(gec - 2))
                    p.op("vector", lambda e: e.tensor_tensor(
                        t2[:, 0:512], prot[:], sinb[:, 0:512], ALU.mult), s_rop)
                    v1 = s_rop.n
                    p.wait("tensor", s_rop, v1)
                    p.op("tensor", lambda e: e.matmul(
                        prot[:], rs_sb[:], eb[:, 512:1024], start=True, stop=True), s_rot)
                    r2 = s_rot.n
                    rot_free[gec] = (s_rop, v1 + 1)
                    p.wait("vector", s_rot, r2)
                    p.op("vector", lambda e: e.tensor_tensor(
                        t2[:, 512:1024], prot[:], sinb[:, 512:1024], ALU.mult), s_rop)
                    p.op("vector", lambda e: e.tensor_mul(t1[:], eb[:], cosb[:]), s_rop)
                    v2 = s_rop.n
                    eb_free[gec] = (s_rop, v2)
                    p.waitf("vector", ob_free.get(gec - 2))
                    p.wait("vector", s_rop, v2)
                    p.op("vector", lambda e: e.tensor_add(ob[:], t1[:], t2[:]), s_rop)
                    tmp_free[gec] = (s_rop, s_rop.n)
                    p.wait("scalar", s_rop, s_rop.n)
                    rst_ob.dma(p, "scalar", gec, out_dram[ec * 128:(ec + 1) * 128, :], ob[:])
                    ob_free[gec] = rst_ob.free(gec)
                else:
                    p.wait("tensor", s_evq, ev_now)
                    p.waitf("tensor", rot_free.get(gec - 1))
                    def pe_tr(e):
                        inst = None
                        for i in range(8):
                            inst = e.transpose(ptr_bf[:, i * 128:(i + 1) * 128],
                                               eb[:, i * 128:(i + 1) * 128], ident_sb[:])
                        return inst
                    p.op("tensor", pe_tr, s_rot)
                    eb_free[gec] = (s_rot, s_rot.n)
                    p.waitf("scalar", ob_free.get(gec - 2))
                    p.wait("scalar", s_rot, s_rot.n)
                    p.op("scalar", lambda e: e.activation(
                        ob[:], ptr_bf[:], AF.Copy, scale=1.0), s_rop)
                    rot_free[gec] = (s_rop, s_rop.n)
                    p.wait("scalar", s_rop, s_rop.n)
                    dst = out_dram.ap().rearrange("(b p) e -> p b e", p=128)[:, :, ec * 128:(ec + 1) * 128]
                    srcap = ob[:].rearrange("p (b d) -> p b d", d=128)
                    rst_ob.dma(p, "scalar", gec, dst, srcap)
                    ob_free[gec] = rst_ob.free(gec)
            defer(emit_def)
            return (s_evq, ev_now)
        return evac

    kind_q = "rope" if rope else "plain"
    pf = {}

    def ring_snap():
        return [(s, s.n) for s in rst_ob.sems + rst_eb.sems]

    emit_proj(p, "k", wk, NC, xb, psq, make_evac(kind_q, kout),
              pre_mm_waits=[(xb_ready, xb_ready.n), (s_aux, aux_ready)],
              psum_free=pf, pf_base=0)
    if ag_cb is not None:
        ag_cb("k", ring_snap())
    emit_proj(p, "v", wv, NC, xb, psq, make_evac("v", vout),
              psum_free=pf, pf_base=32)
    if ag_cb is not None:
        ag_cb("v", ring_snap())
    emit_proj(p, "q", wq, NC, xb, psq, make_evac(kind_q, qout),
              psum_free=pf, pf_base=64)

    rst_ob.wait_all(p, "gpsimd")
    rst_eb.wait_all(p, "gpsimd")


def build_qkv(rope):
    nc = mk_nc()
    xin = nc.dram_tensor("xin", [D, TL], F32, kind="ExternalInput")
    wq = nc.dram_tensor("wq", [D, D], F32, kind="ExternalInput")
    wk = nc.dram_tensor("wk", [D, D], F32, kind="ExternalInput")
    wv = nc.dram_tensor("wv", [D, D], F32, kind="ExternalInput")
    qout = nc.dram_tensor("qout", [D, TL], BF16, kind="ExternalOutput")
    kout = nc.dram_tensor("kout", [D, TL], BF16, kind="ExternalOutput")
    vout = nc.dram_tensor("vout", [TL, D], BF16, kind="ExternalOutput")
    cos_in = sin_in = None
    if rope:
        cos_in = nc.dram_tensor("cosin", [128, TL], F32, kind="ExternalInput")
        sin_in = nc.dram_tensor("sinin", [128, TL], F32, kind="ExternalInput")
    with ExitStack() as ctx:
        ctx.enter_context(nc.allow_low_precision(reason="bf16 compute per stated tolerance"))
        p = P(nc, ctx)
        emit_qkv(nc, p, ctx, "", rope, xin, wq, wk, wv, qout, kout, vout, cos_in, sin_in)
        p.flush("qkv")
    return nc



def emit_attn(nc, p, ctx, causal, scale, qin, kag, vag, att, msk_in=None, pre_waits=None):
    sb = lambda n, s, d: ctx.enter_context(nc.sbuf_tensor(n, s, d, allow_name_mangling=True))
    psa = lambda n, s, d=F32: ctx.enter_context(nc.psum_tensor(n, s, d, allow_name_mangling=True))

    ksb = [sb(f"ksb{i}", [128, 64 * 128], BF16) for i in range(2)]
    vsb = [sb(f"vsb{i}", [128, 64 * 128], BF16) for i in range(2)]
    qsb = [sb(f"qsb{i}", [128, TL], BF16) for i in range(2)]
    esb = [sb(f"esb{i}", [128, 1024], BF16) for i in range(3)]
    dsb2 = [[sb(f"dsb{s}_{i}", [128, 1024], BF16) for i in range(2)] for s in range(2)]
    rbf = sb("rbf", [1, 1024], BF16)
    bcs2 = [sb(f"bcs{i}", [128, 512], F32) for i in range(2)]
    att_sb = [sb(f"attsb{i}", [128, TL], BF16) for i in range(2)]
    onec = sb("onec", [128, 1], BF16)
    oner = sb("oner", [1, 128], BF16)
    if causal:
        msk_sb = sb("msk_sb", [128, 8 * 128], BF16)

    pst = [psa(f"pst{i}", [128, 1024]) for i in range(2)]
    ps_pv = [psa(f"ps_pv{i}", [128, 512]) for i in range(2)]
    ps_den = psa("ps_den", [1, 512])
    ps_bc = psa("ps_bc", [128, 512])

    rkv = [p.ctr("rkv0"), p.ctr("rkv1")]
    rst = Ring(p, "rst", 2)
    a_s = p.ctr("a_s")
    a_x = p.ctr("a_x")
    a_e = p.ctr("a_e")
    a_p = p.ctr("a_p")
    a_v = p.ctr("a_v")
    a_x2 = p.ctr("a_x2")
    s_aux = p.ctr("aux")

    p.op("vector", lambda e: e.memset(onec[:], 1.0), s_aux)
    p.op("vector", lambda e: e.memset(oner[:], 1.0), s_aux)
    if causal:
        p.dma("sync", msk_sb[:].rearrange("p (r q) -> p r q", q=128),
              msk_in.ap().rearrange("(r p) q -> p r q", p=128), s_aux)
        p.wait("vector", s_aux, s_aux.n)

    kv_ready = {}

    def loads(h):
        hb = h % 2
        s = rkv[hb]
        for k8 in range(8):
            p.dma("sync", ksb[hb][:, k8 * 1024:(k8 + 1) * 1024].rearrange("p (c t) -> p c t", t=128),
                  kag.ap().rearrange("(c x) t -> x c t", c=8)[h * 128:(h + 1) * 128, :, k8 * 128:(k8 + 1) * 128], s)
            p.dma("sync", vsb[hb][:, k8 * 1024:(k8 + 1) * 1024].rearrange("p (c t) -> p c t", t=128),
                  vag.ap().rearrange("(c y) e -> y c e", c=8)[k8 * 128:(k8 + 1) * 128, :, h * 128:(h + 1) * 128], s)
        p.dma("sync", qsb[hb][:], qin[h * 128:(h + 1) * 128, :], s)
        kv_ready[h] = s.n

    gj0 = [None]
    scount = {}
    xcount = {}
    ecount = {}
    pcount = {}
    div_done = {}
    xcount_r = {}
    pv_ok = {}
    dme_done = {}

    if pre_waits:
        for (cw, vw) in pre_waits:
            p.wait("sync", cw, vw)
    loads(0)
    loads(1)

    nB = 64
    npair = 32 if causal else 64
    pend_div = []

    for h in range(NH):
        hb = h % 2
        p.wait("tensor", rkv[hb], kv_ready[h])

        dsb = dsb2[h % 2]
        p.waitf("vector", div_done.get(h - 2))
        p.op("vector", lambda e, dsb=dsb: e.memset(dsb[0][:], 0.0), a_v)
        p.op("vector", lambda e, dsb=dsb: e.memset(dsb[1][:], 0.0), a_v)
        dmn = a_v.n

        if gj0[0] is None:
            gj0[0] = 0
        gj_head = gj0[0]

        def colA(j):
            return (j // 8) * 128 if causal else 0

        def colB(j):
            if causal and j >= 32:
                return (j // 8 - 4) * 128
            return 0

        for j in range(nB):
            if j <= 2 and pend_div:
                pend_div.pop(0)()
            gj = gj0[0]; gj0[0] += 1
            paired = j < npair
            c0a = colA(j)
            c0b = colB(j)
            lo = c0a if paired else 512 + c0b
            p.waitf("tensor", xcount_r.get(gj - 2))
            if paired:
                p.op("tensor", lambda e, j=j, gj=gj, c0a=c0a, hb=hb: e.matmul(
                    pst[gj % 2][:, c0a:512],
                    ksb[hb][:, j * 128:(j + 1) * 128],
                    qsb[hb][:, c0a:512], start=True, stop=True), None)
            p.op("tensor", lambda e, j=j, gj=gj, c0b=c0b, hb=hb: e.matmul(
                pst[gj % 2][:, 512 + c0b:1024],
                ksb[hb][:, j * 128:(j + 1) * 128],
                qsb[hb][:, 512 + c0b:1024], start=True, stop=True), a_s)
            scount[gj] = a_s.n
            if j >= 2:
                jj = j - 2
                gjj = gj - 2
                p.waitf("tensor", pv_ok.get(gjj))
                if jj == 0:
                    p.waitf("tensor", div_done.get(h - 1))
                if jj < npair:
                    p.op("tensor", lambda e, jj=jj, gjj=gjj, hb=hb, c0=colA(jj): e.matmul(
                        ps_pv[0][:, c0:512],
                        vsb[hb][:, jj * 128:(jj + 1) * 128],
                        esb[gjj % 3][:, c0:512],
                        start=(jj == 0), stop=(jj == npair - 1)), None)
                p.op("tensor", lambda e, jj=jj, gjj=gjj, hb=hb, c0=colB(jj): e.matmul(
                    ps_pv[1][:, c0:512],
                    vsb[hb][:, jj * 128:(jj + 1) * 128],
                    esb[gjj % 3][:, 512 + c0:1024],
                    start=(jj == 0), stop=(jj == nB - 1)), a_p)
                pcount[gjj] = (a_p, a_p.n)
            p.wait("scalar", a_s, scount[gj])
            p.waitf("scalar", pcount.get(gj - 3))
            p.waitf("scalar", ecount.get(gj - 3))
            p.op("scalar", lambda e, gj=gj, lo=lo: e.activation(
                esb[gj % 3][:, lo:1024], pst[gj % 2][:, lo:1024], AF.Exp,
                scale=scale), a_x)
            xcount[gj] = a_x.n
            xcount_r[gj] = (a_x, a_x.n)
            p.wait("vector", a_x, xcount[gj])
            if j <= 1:
                p.wait("vector", a_v, dmn)
            else:
                p.waitf("vector", ecount.get(gj - 2))
            mr = j % 8
            if causal and paired:
                p.op("vector", lambda e, gj=gj, c0a=c0a, mr=mr: e.tensor_tensor(
                    esb[gj % 3][:, c0a:c0a + 128], esb[gj % 3][:, c0a:c0a + 128],
                    msk_sb[:, mr * 128:(mr + 1) * 128], ALU.mult), a_v)
                p.wait("vector", a_v, a_v.n)
            if causal and j >= 32:
                p.op("vector", lambda e, gj=gj, c0b=c0b, mr=mr: e.tensor_tensor(
                    esb[gj % 3][:, 512 + c0b:512 + c0b + 128], esb[gj % 3][:, 512 + c0b:512 + c0b + 128],
                    msk_sb[:, mr * 128:(mr + 1) * 128], ALU.mult), a_v)
                p.wait("vector", a_v, a_v.n)
            if causal and (paired or j >= 32):
                pv_ok[gj] = (a_v, a_v.n)
            else:
                pv_ok[gj] = (a_x, xcount[gj])
            p.op("vector", lambda e, gj=gj, lo=lo, dsb=dsb: e.tensor_tensor(
                dsb[gj % 2][:, lo:1024], dsb[gj % 2][:, lo:1024],
                esb[gj % 3][:, lo:1024], ALU.add), a_e)
            ecount[gj] = (a_e, a_e.n)

        for jj in (nB - 2, nB - 1):
            gjj = gj_head + jj
            p.waitf("tensor", pv_ok.get(gjj))
            if jj < npair:
                p.op("tensor", lambda e, jj=jj, gjj=gjj, hb=hb, c0=colA(jj): e.matmul(
                    ps_pv[0][:, c0:512],
                    vsb[hb][:, jj * 128:(jj + 1) * 128],
                    esb[gjj % 3][:, c0:512],
                    start=(jj == 0), stop=(jj == npair - 1)), None)
            p.op("tensor", lambda e, jj=jj, gjj=gjj, hb=hb, c0=colB(jj): e.matmul(
                ps_pv[1][:, c0:512],
                vsb[hb][:, jj * 128:(jj + 1) * 128],
                esb[gjj % 3][:, 512 + c0:1024],
                start=(jj == 0), stop=(jj == nB - 1)), a_p)
            pcount[gjj] = (a_p, a_p.n)
        hp_end = a_p.n

        p.waitf("vector", ecount[gj0[0] - 1])
        p.op("vector", lambda e, dsb=dsb: e.tensor_add(dsb[0][:], dsb[0][:], dsb[1][:]), a_v)
        castn = a_v.n

        def make_div(h=h, hb=hb, hp_end=hp_end, castn=castn, dsb=dsb):
            st = {}
            def stage1():
                p.wait("tensor", a_v, castn)
                if h >= 1:
                    p.waitf("tensor", div_done.get(h - 1))
                for off in (0, 512):
                    p.op("tensor", lambda e, off=off, dsb=dsb: e.matmul(
                        ps_den[:], onec[:], dsb[0][:, off:off + 512], start=True, stop=True), a_s)
                    on_n = a_s.n
                    p.wait("vector", a_s, on_n)
                    p.op("vector", lambda e, off=off: e.reciprocal(rbf[:, off:off + 512], ps_den[:]), a_v)
                    if off == 0:
                        p.wait("tensor", a_v, a_v.n)
                st["recip"] = a_v.n
            def stage2():
                p.wait("tensor", a_v, st["recip"])
                for i2, off in enumerate((0, 512)):
                    if i2 == 1:
                        p.wait("tensor", a_x2, a_x2.n)
                    p.op("tensor", lambda e, off=off: e.matmul(
                        ps_bc[:], oner[:], rbf[:, off:off + 512], start=True, stop=True), a_s)
                    bc_n = a_s.n
                    p.wait("scalar", a_s, bc_n)
                    p.op("scalar", lambda e, i2=i2: e.activation(
                        bcs2[i2][:], ps_bc[:], AF.Copy, scale=1.0), a_x2)
                st["bce"] = a_x2.n
            def stage3():
                p.wait("vector", a_x2, st["bce"])
                p.wait("vector", a_p, hp_end)
                p.waitf("vector", rst.free(h - 2) if h >= 2 else None)
                for i2, (pvp, off) in enumerate(((0, 0), (1, 512))):
                    p.op("vector", lambda e, pvp=pvp, off=off, hb=hb, i2=i2: e.tensor_tensor(
                        att_sb[hb][:, off:off + 512], ps_pv[pvp][:], bcs2[i2][:], ALU.mult), a_v)
                div_done[h] = (a_v, a_v.n)
                p.wait("sync", a_v, div_done[h][1])
                rst.dma(p, "sync", h, att[h * 128:(h + 1) * 128, :], att_sb[hb][:])
                if h + 2 < NH:
                    p.wait("sync", a_p, hp_end)
                    loads(h + 2)
            return [stage1, stage2, stage3]
        pend_div = make_div()
        if h == NH - 1:
            for fn in pend_div:
                fn()
            pend_div = []

    rst.wait_all(p, "gpsimd")


def build_attn(causal, scale):
    nc = mk_nc()
    qin = nc.dram_tensor("qin", [D, TL], BF16, kind="ExternalInput")
    kag = nc.dram_tensor("kag", [NCORE * D, TL], BF16, kind="ExternalInput")
    vag = nc.dram_tensor("vag", [NCORE * TL, D], BF16, kind="ExternalInput")
    att = nc.dram_tensor("att", [D, TL], BF16, kind="ExternalOutput")
    msk_in = None
    if causal:
        msk_in = nc.dram_tensor("msk", [8 * 128, 128], BF16, kind="ExternalInput")
    with ExitStack() as ctx:
        ctx.enter_context(nc.allow_low_precision(reason="bf16 compute per stated tolerance"))
        p = P(nc, ctx)
        emit_attn(nc, p, ctx, causal, scale, qin, kag, vag, att, msk_in)
        p.flush("attn")
    return nc



def emit_post(nc, p, ctx, tag, layer, attin, wo, xprev, xout, xr_dram, x1_dram,
              rw=None, rbias=None, eb64=None,
              wbase=None, bb_in=None, wr=None, astk=None, bstk=None):
    if layer == 0:
        NE = 64
        KK = 8
    else:
        ex8 = np.zeros((8, 128), dtype=BF)
        for e8 in range(8):
            ex8[e8, e8 * 16:(e8 + 1) * 16] = 1.0
        ex8_d = nc.inline_tensor(ex8, f"ex8{tag}")
        NE = 8
        KK = 2
    identb_d = nc.inline_tensor(np.eye(128, dtype=BF), f"identb{tag}")
    identf_d = nc.inline_tensor(np.eye(128, dtype=np.float32), f"identf{tag}")
    sb = lambda n, s, d: ctx.enter_context(nc.sbuf_tensor(n, s, d, allow_name_mangling=True))

    ab = sb("ab", [128, NC * 1024], BF16)
    xr_sb = [sb(f"xr{i}", [128, 1024], F32) for i in range(2)]
    xp_sb = [sb(f"xp{i}", [128, 1024], F32) for i in range(2)]
    xrb_sb = [sb(f"xrb{i}", [128, 1024], BF16) for i in range(2)]
    sq_sb = [sb(f"sq{i}", [128, 1024], BF16) for i in range(2)]
    x1c_sb = [sb(f"x1c{i}", [128, 1024], F32) for i in range(2)]
    onec = sb("onec", [128, 1], BF16)
    oner = sb("oner", [1, 128], BF16)
    epsb = sb("epsb", [1, 1], F32)
    identb = sb("identb_sb", [128, 128], BF16)
    identf = sb("identf_sb", [128, 128], F32)
    m_sb = sb("m_sb", [1, 1024], F32)
    v_sb = sb("v_sb", [1, 1024], F32)
    t1_sb = sb("t1_sb", [1, 1024], F32)
    rstd_sb = sb("rstd_sb", [1, 1024], F32)
    rstdb = sb("rstdb", [1, 1024], BF16)
    mrb = sb("mrb", [1, 1024], BF16)

    s_aux = p.ctr("aux")
    a_r = p.ctr("a_r")
    a_c = p.ctr("a_c")
    a_q = p.ctr("a_q")
    a_sm = p.ctr("a_sm")
    a_ln = p.ctr("a_ln")
    a_f = p.ctr("a_f")
    a_t = p.ctr("a_t")

    p.dma("sync", identb[:], identb_d[:], s_aux)
    p.dma("sync", identf[:], identf_d[:], s_aux)
    p.op("vector", lambda e: e.memset(onec[:], 1.0), s_aux)
    p.op("vector", lambda e: e.memset(oner[:], 1.0), s_aux)
    p.op("vector", lambda e: e.memset(epsb[:], EPS), s_aux)
    aux_n = s_aux.n

    ab_ready = emit_load_xb_bf(p, "ab", attin, ab)

    def ln_finalize_apply(stats, stats_total, src_dram, src_ring, dst_dram, cast_to_ab, tagn):
        ld = p.ring("lnld", 3)
        st = p.ring("lnst", 3)
        ldb = ld.take(NC)
        stb = st.take(NC)
        src_ring.wait_all(p, "sync")
        for eng in ("sync", "vector", "scalar"):
            p.wait(eng, a_sm, stats_total)
            p.wait(eng, a_r, a_r.n)
            p.wait(eng, a_c, a_c.n)
            p.wait(eng, a_q, a_q.n)
        for hh in range(2):
            p.op("vector", lambda e, hh=hh: e.tensor_scalar(
                m_sb[:, hh * 512:(hh + 1) * 512], stats[hh][:], 1.0 / D, None, ALU.mult), a_ln)
            p.op("vector", lambda e, hh=hh: e.tensor_scalar(
                v_sb[:, hh * 512:(hh + 1) * 512], stats[2 + hh][:], 1.0 / D, None, ALU.mult), a_ln)
        p.wait("vector", a_ln, a_ln.n)
        p.op("vector", lambda e: e.tensor_mul(t1_sb[:], m_sb[:], m_sb[:]), a_ln)
        p.wait("vector", a_ln, a_ln.n)
        p.op("vector", lambda e: e.tensor_sub(v_sb[:], v_sb[:], t1_sb[:]), a_ln)
        vn = a_ln.n
        p.wait("scalar", a_ln, vn)
        p.op("scalar", lambda e: e.activation(t1_sb[:], v_sb[:], AF.Ln, bias=epsb[:], scale=1.0), a_ln)
        p.wait("scalar", a_ln, a_ln.n)
        p.op("scalar", lambda e: e.activation(rstd_sb[:], t1_sb[:], AF.Exp, scale=-0.5), a_ln)
        rn = a_ln.n
        p.wait("vector", a_ln, rn)
        p.op("vector", lambda e: e.tensor_mul(t1_sb[:], m_sb[:], rstd_sb[:]), a_ln)
        p.op("vector", lambda e: e.tensor_copy(rstdb[:], rstd_sb[:]), a_ln)
        p.wait("vector", a_ln, a_ln.n)
        p.op("vector", lambda e: e.tensor_copy(mrb[:], t1_sb[:]), a_ln)
        castn = a_ln.n
        with nc.psum_tensor(f"{tagn}_rsb", [128, 1024], F32, allow_name_mangling=True) as rs_bc, \
             nc.psum_tensor(f"{tagn}_mrb", [128, 1024], F32, allow_name_mangling=True) as mr_bc, \
             nc.sbuf_tensor(f"{tagn}_xr3", [128, 1024], F32, allow_name_mangling=True) as xr3, \
             nc.sbuf_tensor(f"{tagn}_x13", [128, 1024], F32, allow_name_mangling=True) as x13:
            xrs = [xr_sb[0], xr_sb[1], xr3]
            x1s = [x1c_sb[0], x1c_sb[1], x13]
            p.wait("tensor", a_ln, castn)
            for hh in range(2):
                p.op("tensor", lambda e, hh=hh: e.matmul(
                    rs_bc[:, hh * 512:(hh + 1) * 512], oner[:],
                    rstdb[:, hh * 512:(hh + 1) * 512], start=True, stop=True), a_ln)
                p.op("tensor", lambda e, hh=hh: e.matmul(
                    mr_bc[:, hh * 512:(hh + 1) * 512], oner[:],
                    mrb[:, hh * 512:(hh + 1) * 512], start=True, stop=True), a_ln)
            bcn = a_ln.n
            apply_done = {}
            cast_done = {}
            for f in range(NC):
                p.waitf("sync", apply_done.get(f - 3))
                ld.dma(p, "sync", ldb + f, xrs[f % 3][:], src_dram[f * 128:(f + 1) * 128, :])
                ld.wait(p, "vector", ldb + f)
                p.wait("vector", a_ln, bcn)
                if f >= 3:
                    p.waitf("vector", apply_done.get(f - 3))
                    if cast_to_ab:
                        p.wait("vector", a_c, cast_done[f - 3])
                p.op("vector", lambda e, f=f: e.tensor_mul(x1s[f % 3][:], xrs[f % 3][:], rs_bc[:]), a_ln)
                p.wait("vector", a_ln, a_ln.n)
                p.op("vector", lambda e, f=f: e.tensor_sub(x1s[f % 3][:], x1s[f % 3][:], mr_bc[:]), a_ln)
                here = a_ln.n
                p.wait("scalar", a_ln, here)
                if cast_to_ab:
                    p.op("scalar", lambda e, f=f: e.activation(
                        ab[:, f * 1024:(f + 1) * 1024], x1s[f % 3][:], AF.Copy, scale=1.0), a_c)
                    cast_done[f] = a_c.n
                st.dma(p, "scalar", stb + f, dst_dram[f * 128:(f + 1) * 128, :], x1s[f % 3][:])
                apply_done[f] = st.free(stb + f)
            st.wait_all(p, "tensor")
        return st

    rxp1 = Ring(p, "rxp1", 2)
    rxr1 = Ring(p, "rxr1", 2)
    stats_pend = []

    def mk_evac(rxp, rxr, stats, a_sm_base, xr_src, resid_dram):
        def evac(ec, psten, defer, mm_need, s_mm):
            rxp.dma(p, "sync", ec, xp_sb[ec % 2][:], resid_dram[ec * 128:(ec + 1) * 128, :])
            p.wait("vector", s_mm, mm_need)
            rxp.wait(p, "vector", ec)
            if ec >= 2:
                p.waitf("vector", rxr.free(ec - 2))
            p.op("vector", lambda e, ec=ec, psten=psten: xr_src(e, ec, psten), a_r)
            rn = a_r.n
            p.wait("scalar", a_r, rn)
            rxr.dma(p, "scalar", ec, xr_dram[ec * 128:(ec + 1) * 128, :], xr_sb[ec % 2][:])
            if ec >= 2:
                p.wait("scalar", a_sm, a_sm_base + ec - 1)
            p.op("scalar", lambda e, ec=ec: e.activation(
                xrb_sb[ec % 2][:], xr_sb[ec % 2][:], AF.Copy, scale=1.0), a_c)
            cn = a_c.n
            p.wait("vector", a_c, cn)
            if ec >= 2:
                p.wait("vector", a_sm, a_sm_base + ec - 1)
            p.op("vector", lambda e, ec=ec: e.tensor_mul(
                sq_sb[ec % 2][:], xrb_sb[ec % 2][:], xrb_sb[ec % 2][:]), a_q)
            qn = a_q.n
            def emit_def(ec=ec, cn=cn, qn=qn):
                p.wait("tensor", a_c, cn)
                p.wait("tensor", a_q, qn)
                if ec == 0:
                    p.wait("tensor", s_aux, aux_n)
                for hh in range(2):
                    p.op("tensor", lambda e, ec=ec, hh=hh: e.matmul(
                        stats[hh][:], onec[:], xrb_sb[ec % 2][:, hh * 512:(hh + 1) * 512],
                        start=(ec == 0), stop=(ec == NC - 1), skip_group_check=True), None)
                    p.op("tensor", lambda e, ec=ec, hh=hh: e.matmul(
                        stats[2 + hh][:], onec[:], sq_sb[ec % 2][:, hh * 512:(hh + 1) * 512],
                        start=(ec == 0), stop=(ec == NC - 1), skip_group_check=True),
                        a_sm if hh == 1 else None)
            defer(emit_def)
            return (a_r, rn)
        return evac

    with ExitStack() as s1s:
        stats = [s1s.enter_context(nc.psum_tensor(f"st{i}", [1, 512], F32, allow_name_mangling=True)) for i in range(4)]
        a_r_base1 = a_r.n
        with ExitStack() as s1p:
            psq2 = [s1p.enter_context(nc.psum_tensor(f"psq{i}", [128, 1024], F32, allow_name_mangling=True)) for i in range(2)]
            ev1 = mk_evac(rxp1, rxr1, stats, 0,
                          lambda e, ec, psten: e.tensor_add(xr_sb[ec % 2][:], psten[:], xp_sb[ec % 2][:]),
                          xprev)
            def ev1g(ec, psten, defer, mm_need, s_mm):
                p.wait("sync", a_r, a_r_base1 + max(0, ec - 1))
                return ev1(ec, psten, defer, mm_need, s_mm)
            emit_proj(p, "wo", wo, NC, ab, psq2, ev1g,
                      pre_mm_waits=[(ab_ready, ab_ready.n)])
        st1 = ln_finalize_apply(stats, a_sm.n, xr_dram, rxr1, x1_dram, True, "ln1")
    ab2_ready = a_c.n

    lgf = sb("lgf", [64, 1024], F32)
    tokl2 = sb("tokl2", [128, 8 * NE], F32)
    ex_sb = sb("ex_t", [128, NE], F32)
    cur_sb = sb("cur_t", [128, NE], F32)
    w_sb = sb("w_t", [128, NE], F32)
    wb_sb = sb("wb_t", [128, NE], BF16)
    mx_sb = sb("mx_t", [128, 1], F32)
    nmx_sb = sb("nmx_t", [128, 1], F32)
    sm_sb = sb("sm_t", [128, 1], F32)
    rc_sb = sb("rc_t", [128, 1], F32)
    t8_sb = sb("t8_t", [128, 1], F32)
    wdt = sb("wdt", [NE, 1024], BF16)
    wsb_sb = sb("wsb_sb", [128, 1024], F32)
    ws1b = sb("ws1b", [1, 1024], BF16)

    if layer == 0:
        rwf = sb("rwf", [128, NC * 64], F32)
        rwb = sb("rwb", [128, NC * 64], BF16)
        ebf = sb("ebf", [64, D], F32)
        ebb = sb("ebb", [64, D], BF16)
        rb_sb = sb("rb_sb", [64, 1], F32)
        p.dma("sync", rwf[:].rearrange("p (d e) -> p d e", e=64),
              rw.ap().rearrange("(d p) e -> p d e", p=128), s_aux)
        p.dma("sync", ebf[:], eb64[:], s_aux)
        p.dma("sync", rb_sb[:], rbias[:], s_aux)
        p.wait("vector", s_aux, s_aux.n)
        p.op("vector", lambda e: e.tensor_copy(rwb[:], rwf[:]), a_f)
        p.op("vector", lambda e: e.tensor_copy(ebb[:], ebf[:]), a_f)
        wready = a_f.n
        wchunks = rwb
    else:
        bbf = sb("bbf", [128, 32], F32)
        wrf = sb("wrf", [128, NC * 8], F32)
        wrb = sb("wrb", [128, NC * 8], BF16)
        asf = sb("asf", [128, NC * 128], F32)
        asb = sb("asb", [128, NC * 128], BF16)
        bsf = sb("bsf", [128, D], F32)
        bsb = sb("bsb", [128, D], BF16)
        ex8_sb = sb("ex8_sb", [8, 128], BF16)
        hs_sb = sb("hs_sb", [128, 1024], BF16)
        cb_sb = sb("cb_sb", [128, 1024], F32)
        p.dma("sync", bbf[:].rearrange("p (f o) -> p f o", o=1),
              bb_in.ap().rearrange("(f p) o -> p f o", p=128), s_aux,
              allow_slow_non_contiguous=True)
        p.dma("sync", wrf[:].rearrange("p (d e) -> p d e", e=8),
              wr.ap().rearrange("(d p) e -> p d e", p=128), s_aux)
        p.dma("sync", asf[:].rearrange("p (d e) -> p d e", e=128),
              astk.ap().rearrange("(d p) e -> p d e", p=128), s_aux)
        p.dma("sync", bsf[:], bstk[:], s_aux)
        p.dma("sync", ex8_sb[:], ex8_d[:], s_aux)
        p.wait("vector", s_aux, s_aux.n)
        p.op("vector", lambda e: e.tensor_copy(wrb[:], wrf[:]), a_f)
        p.op("vector", lambda e: e.tensor_copy(asb[:], asf[:]), a_f)
        p.op("vector", lambda e: e.tensor_copy(bsb[:], bsf[:]), a_f)
        wready = a_f.n
        wchunks = wrb

    def emit_router_and_topk(n_e, kk):
        with ExitStack() as s2:
            pr = s2.enter_context(nc.psum_tensor("pr", [n_e, 1024], F32, allow_name_mangling=True))
            ptt = s2.enter_context(nc.psum_tensor("ptt", [128, n_e], F32, allow_name_mangling=True))
            ptw = s2.enter_context(nc.psum_tensor("ptw", [n_e, 128], BF16, allow_name_mangling=True))
            p.wait("tensor", a_f, wready)
            p.wait("tensor", a_c, ab2_ready)
            for dd in range(NC):
                for t in range(2):
                    p.op("tensor", lambda e, dd=dd, t=t: e.matmul(
                        pr[:, t * 512:(t + 1) * 512],
                        wchunks[:, dd * n_e:(dd + 1) * n_e],
                        ab[:, dd * 1024 + t * 512: dd * 1024 + (t + 1) * 512],
                        start=(dd == 0), stop=(dd == NC - 1)),
                        a_f if (dd == NC - 1 and t == 1) else None)
            prn = a_f.n
            p.wait("vector", a_f, prn)
            if layer == 0:
                p.op("vector", lambda e: e.tensor_scalar(
                    lgf[0:n_e, :], pr[:], rb_sb[:], None, ALU.add), a_f)
            else:
                p.op("vector", lambda e: e.tensor_copy(lgf[0:n_e, :], pr[:]), a_f)
            for tg in range(8):
                p.wait("tensor", a_f, a_f.n)
                p.op("tensor", lambda e, tg=tg: e.transpose(
                    ptt[:, 0:n_e], lgf[0:n_e, tg * 128:(tg + 1) * 128],
                    identf[0:n_e, 0:n_e]), a_f)
                p.wait("scalar", a_f, a_f.n)
                p.op("scalar", lambda e, tg=tg: e.activation(
                    tokl2[:, tg * n_e:(tg + 1) * n_e], ptt[:, 0:n_e], AF.Copy, scale=1.0), a_f)
            for tg in range(8):
                base = tokl2[:, tg * n_e:(tg + 1) * n_e]
                p.wait("vector", a_f, a_f.n)
                p.wait("vector", a_t, a_t.n)
                p.op("vector", lambda e, base=base: e.reduce_max(mx_sb[:], base, axis=AX.X), a_t)
                p.wait("vector", a_t, a_t.n)
                p.op("vector", lambda e: e.tensor_scalar(nmx_sb[:], mx_sb[:], -1.0, None, ALU.mult), a_t)
                nmn = a_t.n
                p.wait("scalar", a_t, nmn)
                p.op("scalar", lambda e, base=base: e.activation(
                    ex_sb[:], base, AF.Exp, bias=nmx_sb[:], scale=1.0), a_t)
                p.wait("vector", a_t, a_t.n)
                p.op("vector", lambda e: e.reduce_sum(sm_sb[:], ex_sb[:], axis=AX.X), a_t)
                p.wait("vector", a_t, a_t.n)
                p.op("vector", lambda e: e.reciprocal(rc_sb[:], sm_sb[:]), a_t)
                p.wait("vector", a_t, a_t.n)
                p.op("vector", lambda e: e.tensor_scalar(
                    ex_sb[:], ex_sb[:], rc_sb[:], None, ALU.mult), a_t)
                p.wait("vector", a_t, a_t.n)
                p.op("vector", lambda e: e.tensor_copy(cur_sb[:], ex_sb[:]), a_t)
                for ki in range(kk):
                    p.wait("vector", a_t, a_t.n)
                    p.op("vector", lambda e: e.reduce_max(t8_sb[:], cur_sb[:], axis=AX.X), a_t)
                    if ki < kk - 1:
                        p.wait("vector", a_t, a_t.n)
                        p.op("vector", lambda e: e.tensor_scalar(
                            w_sb[:], cur_sb[:], t8_sb[:], None, ALU.is_ge), a_t)
                        p.wait("vector", a_t, a_t.n)
                        p.op("vector", lambda e: e.scalar_tensor_tensor(
                            cur_sb[:], w_sb[:], -1e30, cur_sb[:], ALU.mult, ALU.add), a_t)
                p.wait("vector", a_t, a_t.n)
                p.op("vector", lambda e: e.tensor_scalar(
                    w_sb[:], ex_sb[:], t8_sb[:], None, ALU.is_ge), a_t)
                p.wait("vector", a_t, a_t.n)
                p.op("vector", lambda e: e.tensor_mul(w_sb[:], w_sb[:], ex_sb[:]), a_t)
                if layer == 1:
                    p.wait("vector", a_t, a_t.n)
                    p.op("vector", lambda e: e.reduce_sum(sm_sb[:], w_sb[:], axis=AX.X), a_t)
                    p.wait("vector", a_t, a_t.n)
                    p.op("vector", lambda e: e.reciprocal(rc_sb[:], sm_sb[:]), a_t)
                    p.wait("vector", a_t, a_t.n)
                    p.op("vector", lambda e: e.tensor_scalar(
                        w_sb[:], w_sb[:], rc_sb[:], None, ALU.mult), a_t)
                p.wait("vector", a_t, a_t.n)
                p.op("vector", lambda e: e.tensor_copy(wb_sb[:], w_sb[:]), a_t)
                wbn = a_t.n
                p.wait("tensor", a_t, wbn)
                p.wait("tensor", a_f, a_f.n)
                p.op("tensor", lambda e: e.transpose(
                    ptw[:, :], wb_sb[:, 0:n_e], identb[:]), a_f)
                p.wait("scalar", a_f, a_f.n)
                p.op("scalar", lambda e, tg=tg: e.activation(
                    wdt[:, tg * 128:(tg + 1) * 128], ptw[0:n_e, :], AF.Copy, scale=1.0), a_f)
            p.wait("tensor", a_f, a_f.n)

    emit_router_and_topk(NE, KK)

    if layer == 0:
        with ExitStack() as s3:
            ps_ws = [s3.enter_context(nc.psum_tensor(f"ps_ws{i}", [1, 512], F32, allow_name_mangling=True)) for i in range(2)]
            for hh in range(2):
                p.op("tensor", lambda e, hh=hh: e.matmul(
                    ps_ws[hh][:], onec[0:64, :], wdt[:, hh * 512:(hh + 1) * 512],
                    start=True, stop=True), a_f)
            wsn = a_f.n
            p.wait("vector", a_f, wsn)
            for hh in range(2):
                p.op("vector", lambda e, hh=hh: e.tensor_scalar(
                    ws1b[:, hh * 512:(hh + 1) * 512], ps_ws[hh][:], 1.0, None, ALU.add), a_f)
            p.wait("tensor", a_f, a_f.n)
            ps_wb = s3.enter_context(nc.psum_tensor("ps_wb", [128, 1024], F32, allow_name_mangling=True))
            for hh in range(2):
                p.op("tensor", lambda e, hh=hh: e.matmul(
                    ps_wb[:, hh * 512:(hh + 1) * 512], oner[:],
                    ws1b[:, hh * 512:(hh + 1) * 512], start=True, stop=True), a_f)
            p.wait("scalar", a_f, a_f.n)
            p.op("scalar", lambda e: e.activation(wsb_sb[:], ps_wb[:], AF.Copy, scale=1.0), a_f)
            p.wait("tensor", a_f, a_f.n)
        wsbn = a_f.n
        rx1b = Ring(p, "rx1b", 2)
        rxr2 = Ring(p, "rxr2", 2)
        st1.wait_all(p, "sync")
        p.wait("sync", a_r, a_r.n)
        with ExitStack() as s5s:
            stats2 = [s5s.enter_context(nc.psum_tensor(f"st2_{i}", [1, 512], F32, allow_name_mangling=True)) for i in range(4)]
            s5p = ExitStack()
            pbm = [s5p.enter_context(nc.psum_tensor(f"pbm{i}", [128, 1024], F32, allow_name_mangling=True)) for i in range(2)]
            stat_pend = []
            a_r_base = a_r.n
            a_sm_base = a_sm.n
            for ec in range(NC):
                for fn2 in stat_pend:
                    fn2()
                stat_pend = []
                if ec >= 2:
                    p.waitf("tensor", (a_r, a_r_base + ec - 1))
                else:
                    p.wait("tensor", a_f, wsbn)
                for t in range(2):
                    p.op("tensor", lambda e, ec=ec, t=t: e.matmul(
                        pbm[ec % 2][:, t * 512:(t + 1) * 512],
                        ebb[:, ec * 128:(ec + 1) * 128],
                        wdt[:, t * 512:(t + 1) * 512], start=True, stop=True),
                        a_f if t == 1 else None)
                bmn = a_f.n
                p.wait("sync", a_r, a_r_base + max(0, ec - 1))
                rx1b.dma(p, "sync", ec, xp_sb[ec % 2][:], x1_dram[ec * 128:(ec + 1) * 128, :])
                rx1b.wait(p, "vector", ec)
                p.wait("vector", a_f, bmn)
                if ec >= 2:
                    p.waitf("vector", rxr2.free(ec - 2))
                p.wait("vector", a_ln, a_ln.n)
                p.op("vector", lambda e, ec=ec: e.tensor_mul(
                    xr_sb[ec % 2][:], xp_sb[ec % 2][:], wsb_sb[:]), a_ln)
                p.wait("vector", a_ln, a_ln.n)
                p.op("vector", lambda e, ec=ec: e.tensor_add(
                    xr_sb[ec % 2][:], xr_sb[ec % 2][:], pbm[ec % 2][:]), a_r)
                rn = a_r.n
                p.wait("scalar", a_r, rn)
                rxr2.dma(p, "scalar", ec, xr_dram[ec * 128:(ec + 1) * 128, :], xr_sb[ec % 2][:])
                p.wait("scalar", a_r, rn)
                if ec >= 2:
                    p.wait("scalar", a_sm, a_sm_base + ec - 1)
                p.op("scalar", lambda e, ec=ec: e.activation(
                    xrb_sb[ec % 2][:], xr_sb[ec % 2][:], AF.Copy, scale=1.0), a_c)
                cn = a_c.n
                p.wait("vector", a_c, cn)
                if ec >= 2:
                    p.wait("vector", a_sm, a_sm_base + ec - 1)
                p.op("vector", lambda e, ec=ec: e.tensor_mul(
                    sq_sb[ec % 2][:], xrb_sb[ec % 2][:], xrb_sb[ec % 2][:]), a_q)
                qn = a_q.n
                def emit_def(ec=ec, cn=cn, qn=qn):
                    p.wait("tensor", a_c, cn)
                    p.wait("tensor", a_q, qn)
                    for hh in range(2):
                        p.op("tensor", lambda e, ec=ec, hh=hh: e.matmul(
                            stats2[hh][:], onec[:], xrb_sb[ec % 2][:, hh * 512:(hh + 1) * 512],
                            start=(ec == 0), stop=(ec == NC - 1), skip_group_check=True), None)
                        p.op("tensor", lambda e, ec=ec, hh=hh: e.matmul(
                            stats2[2 + hh][:], onec[:], sq_sb[ec % 2][:, hh * 512:(hh + 1) * 512],
                            start=(ec == 0), stop=(ec == NC - 1), skip_group_check=True),
                            a_sm if hh == 1 else None)
                stat_pend.append(emit_def)
            for fn2 in stat_pend:
                fn2()
            s5p.close()
            stf = ln_finalize_apply(stats2, a_sm.n, xr_dram, rxr2, xout, False, "ln2")
    else:
        with ExitStack() as s3:
            ps_h = s3.enter_context(nc.psum_tensor("ps_h", [128, 1024], F32, allow_name_mangling=True))
            ps_cb = s3.enter_context(nc.psum_tensor("ps_cb", [128, 1024], F32, allow_name_mangling=True))
            p.wait("tensor", a_f, a_f.n)
            for t in range(2):
                p.op("tensor", lambda e, t=t: e.matmul(
                    ps_cb[:, t * 512:(t + 1) * 512], ex8_sb[:],
                    wdt[:, t * 512:(t + 1) * 512], start=True, stop=True), a_f)
            cbn = a_f.n
            p.wait("scalar", a_f, cbn)
            p.op("scalar", lambda e: e.activation(cb_sb[:], ps_cb[:], AF.Copy, scale=1.0), a_f)
            p.wait("tensor", a_f, a_f.n)
            for dd in range(NC):
                for t in range(2):
                    p.op("tensor", lambda e, dd=dd, t=t: e.matmul(
                        ps_h[:, t * 512:(t + 1) * 512],
                        asb[:, dd * 128:(dd + 1) * 128],
                        ab[:, dd * 1024 + t * 512: dd * 1024 + (t + 1) * 512],
                        start=(dd == 0), stop=(dd == NC - 1)),
                        a_f if (dd == NC - 1 and t == 1) else None)
            hn = a_f.n
            p.wait("vector", a_f, hn)
            p.op("vector", lambda e: e.scalar_tensor_tensor(
                hs_sb[:], ps_h[:], 2.0, cb_sb[:], ALU.mult, ALU.mult), a_f)
            p.wait("tensor", a_f, a_f.n)
        hs_n = a_f.n
        rxp2 = Ring(p, "rxp2", 2)
        rxr2 = Ring(p, "rxr2", 2)
        p.wait("sync", a_r, a_r.n)
        with ExitStack() as s6s:
            stats2 = [s6s.enter_context(nc.psum_tensor(f"st2_{i}", [1, 512], F32, allow_name_mangling=True)) for i in range(4)]
            s6p = ExitStack()
            psq2b = [s6p.enter_context(nc.psum_tensor(f"psqb{i}", [128, 1024], F32, allow_name_mangling=True)) for i in range(2)]

            def extra1(ec, psten, s_mm):
                p.wait("tensor", a_f, hs_n)
                for t in range(2):
                    p.op("tensor", lambda e, ec=ec, t=t, psten=psten: e.matmul(
                        psten[:, t * 512:(t + 1) * 512],
                        bsb[:, ec * 128:(ec + 1) * 128],
                        hs_sb[:, t * 512:(t + 1) * 512], start=False, stop=True),
                        s_mm if t == 1 else None)

            a_r_base = a_r.n
            st1.wait_all(p, "sync")
            ev2 = mk_evac(rxp2, rxr2, stats2, a_sm.n,
                          lambda e, ec, psten: e.scalar_tensor_tensor(
                              xr_sb[ec % 2][:], psten[:], bbf[:, ec:ec + 1],
                              xp_sb[ec % 2][:], ALU.add, ALU.add),
                          x1_dram)
            def ev2g(ec, psten, defer, mm_need, s_mm):
                p.wait("sync", a_r, a_r_base + max(0, ec - 1))
                return ev2(ec, psten, defer, mm_need, s_mm)
            emit_proj(p, "wb", wbase, NC, ab, psq2b, ev2g,
                      pre_mm_waits=[(a_f, hs_n), (a_c, ab2_ready)], extra_mm_cb=extra1,
                      depth=6, sctx=s6p)
            s6p.close()
            stf = ln_finalize_apply(stats2, a_sm.n, xr_dram, rxr2, xout, False, "ln2")

    stf.wait_all(p, "gpsimd")


def build_post(layer):
    nc = mk_nc()
    attin = nc.dram_tensor("attin", [D, TL], BF16, kind="ExternalInput")
    wo = nc.dram_tensor("wo", [D, D], F32, kind="ExternalInput")
    xprev = nc.dram_tensor("xprev", [D, TL], F32, kind="ExternalInput")
    xout = nc.dram_tensor("xout", [D, TL], F32, kind="ExternalOutput")
    kw = {}
    if layer == 0:
        kw["rw"] = nc.dram_tensor("rw", [D, 64], F32, kind="ExternalInput")
        kw["rbias"] = nc.dram_tensor("rbias", [64, 1], F32, kind="ExternalInput")
        kw["eb64"] = nc.dram_tensor("eb64", [64, D], F32, kind="ExternalInput")
    else:
        kw["wbase"] = nc.dram_tensor("wbase", [D, D], F32, kind="ExternalInput")
        kw["bb_in"] = nc.dram_tensor("bb", [D, 1], F32, kind="ExternalInput")
        kw["wr"] = nc.dram_tensor("wr", [D, 8], F32, kind="ExternalInput")
        kw["astk"] = nc.dram_tensor("astk", [D, 128], F32, kind="ExternalInput")
        kw["bstk"] = nc.dram_tensor("bstk", [128, D], F32, kind="ExternalInput")
    xr_dram = nc.dram_tensor("xr_scr", [D, TL], F32)
    x1_dram = nc.dram_tensor("x1_scr", [D, TL], F32)
    with ExitStack() as ctx:
        ctx.enter_context(nc.allow_low_precision(reason="bf16 compute per stated tolerance"))
        p = P(nc, ctx)
        emit_post(nc, p, ctx, "", layer, attin, wo, xprev, xout, xr_dram, x1_dram, **kw)
        p.flush("post")
    return nc



def emit_load_xb_bf(p, tag, src_dram, xb):
    s_l = p.ctr(f"{tag}_l")
    for f in range(NC):
        p.dma("sync", xb[:, f * 1024:(f + 1) * 1024], src_dram[f * 128:(f + 1) * 128, :], s_l)
    return s_l


def owned_cols(c):
    return np.concatenate([np.arange((8 * k + c) * 128, (8 * k + c) * 128 + 128)
                           for k in range(8)])


def rope_tables():
    inv = 1.0 / (10000.0 ** (np.arange(0, 128, 2, dtype=np.float64) / 128.0))
    t = np.arange(SEQ, dtype=np.float64)
    fr = np.outer(t, inv)
    emb = np.concatenate([fr, fr], axis=-1)
    return np.cos(emb).astype(np.float32), np.sin(emb).astype(np.float32)


def build_fused():
    nc = mk_nc()
    xin = nc.dram_tensor("xin", [D, TL], F32, kind="ExternalInput")
    cos_in = nc.dram_tensor("cosin", [128, TL], F32, kind="ExternalInput")
    sin_in = nc.dram_tensor("sinin", [128, TL], F32, kind="ExternalInput")
    wq0 = nc.dram_tensor("wq0", [D, D], F32, kind="ExternalInput")
    wk0 = nc.dram_tensor("wk0", [D, D], F32, kind="ExternalInput")
    wv0 = nc.dram_tensor("wv0", [D, D], F32, kind="ExternalInput")
    wo0 = nc.dram_tensor("wo0", [D, D], F32, kind="ExternalInput")
    rw = nc.dram_tensor("rw", [D, 64], F32, kind="ExternalInput")
    rbias = nc.dram_tensor("rbias", [64, 1], F32, kind="ExternalInput")
    eb64 = nc.dram_tensor("eb64", [64, D], F32, kind="ExternalInput")
    wq1 = nc.dram_tensor("wq1", [D, D], F32, kind="ExternalInput")
    wk1 = nc.dram_tensor("wk1", [D, D], F32, kind="ExternalInput")
    wv1 = nc.dram_tensor("wv1", [D, D], F32, kind="ExternalInput")
    wo1 = nc.dram_tensor("wo1", [D, D], F32, kind="ExternalInput")
    wbase = nc.dram_tensor("wbase", [D, D], F32, kind="ExternalInput")
    bb_in = nc.dram_tensor("bb", [D, 1], F32, kind="ExternalInput")
    wr = nc.dram_tensor("wr", [D, 8], F32, kind="ExternalInput")
    astk = nc.dram_tensor("astk", [D, 128], F32, kind="ExternalInput")
    bstk = nc.dram_tensor("bstk", [128, D], F32, kind="ExternalInput")
    msk_in = nc.dram_tensor("msk", [8 * 128, 128], BF16, kind="ExternalInput")
    xout = nc.dram_tensor("xout", [D, TL], F32, kind="ExternalOutput")
    q0l = nc.dram_tensor("q0l", [D, TL], BF16)
    k0l = nc.dram_tensor("k0l", [D, TL], BF16)
    v0l = nc.dram_tensor("v0l", [TL, D], BF16)
    kag0 = nc.dram_tensor("kag0", [NCORE * D, TL], BF16, addr_space="Shared")
    vag0 = nc.dram_tensor("vag0", [NCORE * TL, D], BF16, addr_space="Shared")
    att0 = nc.dram_tensor("att0", [D, TL], BF16)
    x2d = nc.dram_tensor("x2d", [D, TL], F32)
    q1l = nc.dram_tensor("q1l", [D, TL], BF16)
    k1l = nc.dram_tensor("k1l", [D, TL], BF16)
    v1l = nc.dram_tensor("v1l", [TL, D], BF16)
    kag1 = nc.dram_tensor("kag1", [NCORE * D, TL], BF16, addr_space="Shared")
    vag1 = nc.dram_tensor("vag1", [NCORE * TL, D], BF16, addr_space="Shared")
    att1 = nc.dram_tensor("att1", [D, TL], BF16)
    xr_dram = nc.dram_tensor("xr_scr", [D, TL], F32)
    x1_dram = nc.dram_tensor("x1_scr", [D, TL], F32)

    with ExitStack() as octx:
        octx.enter_context(nc.allow_low_precision(reason="bf16 compute per stated tolerance"))
        p = P(nc, octx)

        def mk_ag_cb(pairs_map, cc):
            def cb(name, snaps):
                if name in pairs_map:
                    src_d, dst_d = pairs_map[name]
                    for (sw, vw) in snaps:
                        p.wait("gpsimd", sw, vw)
                    p.op("gpsimd", lambda e, src_d=src_d, dst_d=dst_d: e.collective_compute(
                        "AllGather", ALU.bypass, replica_groups=[list(range(NCORE))],
                        ins=[src_d[:]], outs=[dst_d[:]]), cc)
            return cb

        cc0 = p.ctr("cc0")
        with ExitStack() as c:
            p.actx = c
            emit_qkv(nc, p, c, "0", True, xin, wq0, wk0, wv0, q0l, k0l, v0l, cos_in, sin_in,
                     ag_cb=mk_ag_cb({"k": (k0l, kag0), "v": (v0l, vag0)}, cc0))
        p.flush("phA")
        nc.all_engine_barrier()
        with ExitStack() as c:
            p.actx = c
            emit_attn(nc, p, c, False, L0_SCALE, q0l, kag0, vag0, att0,
                      pre_waits=[(cc0, 2)])
        p.flush("phB")
        nc.all_engine_barrier()
        with ExitStack() as c:
            p.actx = c
            emit_post(nc, p, c, "0", 0, att0, wo0, xin, x2d, xr_dram, x1_dram,
                      rw=rw, rbias=rbias, eb64=eb64)
        p.flush("phC")
        nc.all_engine_barrier()
        cc1 = p.ctr("cc1")
        with ExitStack() as c:
            p.actx = c
            emit_qkv(nc, p, c, "1", False, x2d, wq1, wk1, wv1, q1l, k1l, v1l,
                     ag_cb=mk_ag_cb({"k": (k1l, kag1), "v": (v1l, vag1)}, cc1))
        p.flush("phD")
        nc.all_engine_barrier()
        with ExitStack() as c:
            p.actx = c
            emit_attn(nc, p, c, True, L1_SCALE, q1l, kag1, vag1, att1, msk_in,
                      pre_waits=[(cc1, 2)])
        p.flush("phE")
        nc.all_engine_barrier()
        with ExitStack() as c:
            p.actx = c
            emit_post(nc, p, c, "1", 1, att1, wo1, x2d, xout, xr_dram, x1_dram,
                      wbase=wbase, bb_in=bb_in, wr=wr, astk=astk, bstk=bstk)
        p.flush("phF")
    return nc


def kernel(**inputs):
    g = {k: np.asarray(v) for k, v in inputs.items()}
    x = np.ascontiguousarray(g["L0_attn_hidden_states"][0].T)
    wqT = np.ascontiguousarray(g["L0_attn_Wq"].T.astype(np.float32, copy=False))
    wkT = np.ascontiguousarray(g["L0_attn_Wk"].T.astype(np.float32, copy=False))
    wvT = np.ascontiguousarray(g["L0_attn_Wv"].T.astype(np.float32, copy=False))
    woT = np.ascontiguousarray(g["L0_attn_Wo"].T.astype(np.float32, copy=False))
    rwT = np.ascontiguousarray(g["L0_ffn_router_weight"].T.astype(np.float32, copy=False))
    rbias = np.ascontiguousarray(g["L0_ffn_router_bias"].reshape(64, 1).astype(np.float32, copy=False))
    eb64v = np.ascontiguousarray(g["L0_ffn_expert_bias"].astype(np.float32, copy=False))
    wq1f = g["L1_attn_wq"].astype(np.float32, copy=False)
    wq1u = np.ascontiguousarray(
        np.concatenate([wq1f[:, h * 192:h * 192 + 128] for h in range(32)], axis=1))
    wkv1 = g["L1_attn_wkv"].astype(np.float32, copy=False)
    wk1v = np.ascontiguousarray(wkv1[:, 0:4096])
    wv1v = np.ascontiguousarray(wkv1[:, 4096:8192])
    wo1v = np.ascontiguousarray(g["L1_attn_wo"].astype(np.float32, copy=False))
    wbT = np.ascontiguousarray(g["L1_ffn_W_base"].T.astype(np.float32, copy=False))
    bbv = np.ascontiguousarray(g["L1_ffn_b_base"].reshape(4096, 1).astype(np.float32, copy=False))
    wrT = np.ascontiguousarray(g["L1_ffn_W_router"].T.astype(np.float32, copy=False))
    astkv = np.ascontiguousarray(
        g["L1_ffn_A_stack"].astype(np.float32, copy=False).reshape(128, 4096).T)
    bstkv = np.ascontiguousarray(
        g["L1_ffn_B_stack"].astype(np.float32, copy=False).transpose(0, 2, 1).reshape(128, 4096))

    cols = [owned_cols(c) for c in range(NCORE)]
    cos, sin = rope_tables()
    nc = build_fused()
    in_maps = []
    for c in range(NCORE):
        in_maps.append({
            "xin": np.ascontiguousarray(x[:, cols[c]]),
            "cosin": np.ascontiguousarray(cos[cols[c]].T),
            "sinin": np.ascontiguousarray(sin[cols[c]].T),
            "wq0": wqT, "wk0": wkT, "wv0": wvT, "wo0": woT,
            "rw": rwT, "rbias": rbias, "eb64": eb64v,
            "wq1": wq1u, "wk1": wk1v, "wv1": wv1v, "wo1": wo1v,
            "wbase": wbT, "bb": bbv, "wr": wrT, "astk": astkv, "bstk": bstkv,
            "msk": build_masks(c),
        })
    res = run_bass_kernel_spmd(nc, in_maps, core_ids=list(range(NCORE))).results
    out = np.empty((D, SEQ), dtype=np.float32)
    for c in range(NCORE):
        out[:, cols[c]] = res[c]["xout"]
    return np.ascontiguousarray(out.T).reshape(1, SEQ, D)


def run_qkv_launch(rope, x_full_T, wq_de, wk_de, wv_de):
    nc = build_qkv(rope)
    in_maps = []
    if rope:
        cos, sin = rope_tables()
    for c in range(NCORE):
        cols = owned_cols(c)
        m = {"xin": np.ascontiguousarray(x_full_T[:, cols]),
             "wq": wq_de, "wk": wk_de, "wv": wv_de}
        if rope:
            m["cosin"] = np.ascontiguousarray(cos[cols].T)
            m["sinin"] = np.ascontiguousarray(sin[cols].T)
        in_maps.append(m)
    res = run_bass_kernel_spmd(nc, in_maps, core_ids=list(range(NCORE)))
    return res.results


def build_masks(c):
    m = np.zeros((8, 128, 128), dtype=np.float32)
    for r in range(8):
        if r < c:
            m[r] = 1.0
        elif r == c:
            m[r] = np.triu(np.ones((128, 128), dtype=np.float32))
    return np.ascontiguousarray(m.reshape(8 * 128, 128).astype(BF))


def run_qkv(rope, xT_cores, wq_de, wk_de, wv_de, extra=None):
    nc = build_qkv(rope)
    in_maps = []
    for c in range(NCORE):
        m = {"xin": xT_cores[c], "wq": wq_de, "wk": wk_de, "wv": wv_de}
        if rope:
            m.update(extra[c])
        in_maps.append(m)
    return run_bass_kernel_spmd(nc, in_maps, core_ids=list(range(NCORE))).results


def kernel_unfused(**inputs):
    g = {k: np.asarray(v) for k, v in inputs.items()}
    x = np.ascontiguousarray(g["L0_attn_hidden_states"][0].T)
    wqT = np.ascontiguousarray(g["L0_attn_Wq"].T.astype(np.float32, copy=False))
    wkT = np.ascontiguousarray(g["L0_attn_Wk"].T.astype(np.float32, copy=False))
    wvT = np.ascontiguousarray(g["L0_attn_Wv"].T.astype(np.float32, copy=False))
    woT = np.ascontiguousarray(g["L0_attn_Wo"].T.astype(np.float32, copy=False))
    rwT = np.ascontiguousarray(g["L0_ffn_router_weight"].T.astype(np.float32, copy=False))
    rbias = np.ascontiguousarray(g["L0_ffn_router_bias"].reshape(64, 1).astype(np.float32, copy=False))
    eb64 = np.ascontiguousarray(g["L0_ffn_expert_bias"].astype(np.float32, copy=False))
    wq1 = g["L1_attn_wq"].astype(np.float32, copy=False)
    wq1u = np.ascontiguousarray(
        np.concatenate([wq1[:, h * 192:h * 192 + 128] for h in range(32)], axis=1))
    wkv1 = g["L1_attn_wkv"].astype(np.float32, copy=False)
    wk1 = np.ascontiguousarray(wkv1[:, 0:4096])
    wv1 = np.ascontiguousarray(wkv1[:, 4096:8192])
    wo1 = np.ascontiguousarray(g["L1_attn_wo"].astype(np.float32, copy=False))
    wbT = np.ascontiguousarray(g["L1_ffn_W_base"].T.astype(np.float32, copy=False))
    bb = np.ascontiguousarray(g["L1_ffn_b_base"].reshape(4096, 1).astype(np.float32, copy=False))
    wrT = np.ascontiguousarray(g["L1_ffn_W_router"].T.astype(np.float32, copy=False))
    astk = np.ascontiguousarray(
        g["L1_ffn_A_stack"].astype(np.float32, copy=False).reshape(128, 4096).T)
    bstk = np.ascontiguousarray(
        g["L1_ffn_B_stack"].astype(np.float32, copy=False).transpose(0, 2, 1).reshape(128, 4096))

    cols = [owned_cols(c) for c in range(NCORE)]
    xT_cores = [np.ascontiguousarray(x[:, cols[c]]) for c in range(NCORE)]
    cos, sin = rope_tables()
    rope_extra = [{"cosin": np.ascontiguousarray(cos[cols[c]].T),
                   "sinin": np.ascontiguousarray(sin[cols[c]].T)} for c in range(NCORE)]

    resA = run_qkv(True, xT_cores, wqT, wkT, wvT, rope_extra)
    kag0 = np.concatenate([r["kout"] for r in resA], axis=0)
    vag0 = np.concatenate([r["vout"] for r in resA], axis=0)

    ncB = build_attn(False, L0_SCALE)
    in_maps = [{"qin": resA[c]["qout"], "kag": kag0, "vag": vag0} for c in range(NCORE)]
    resB = run_bass_kernel_spmd(ncB, in_maps, core_ids=list(range(NCORE))).results

    ncC = build_post(0)
    in_maps = [{"attin": resB[c]["att"], "wo": woT, "xprev": xT_cores[c],
                "rw": rwT, "rbias": rbias, "eb64": eb64} for c in range(NCORE)]
    resC = run_bass_kernel_spmd(ncC, in_maps, core_ids=list(range(NCORE))).results

    x2_cores = [np.ascontiguousarray(resC[c]["xout"]) for c in range(NCORE)]
    resD = run_qkv(False, x2_cores, wq1u, wk1, wv1)
    kag1 = np.concatenate([r["kout"] for r in resD], axis=0)
    vag1 = np.concatenate([r["vout"] for r in resD], axis=0)

    ncE = build_attn(True, L1_SCALE)
    in_maps = [{"qin": resD[c]["qout"], "kag": kag1, "vag": vag1,
                "msk": build_masks(c)} for c in range(NCORE)]
    resE = run_bass_kernel_spmd(ncE, in_maps, core_ids=list(range(NCORE))).results

    ncF = build_post(1)
    in_maps = [{"attin": resE[c]["att"], "wo": wo1, "xprev": x2_cores[c],
                "wbase": wbT, "bb": bb, "wr": wrT, "astk": astk, "bstk": bstk}
               for c in range(NCORE)]
    resF = run_bass_kernel_spmd(ncF, in_maps, core_ids=list(range(NCORE))).results

    out = np.empty((D, SEQ), dtype=np.float32)
    for c in range(NCORE):
        out[:, cols[c]] = resF[c]["xout"]
    return np.ascontiguousarray(out.T).reshape(1, SEQ, D)
